# version 01
# baseline (speedup 1.0000x reference)
import numpy as np
import ml_dtypes
from contextlib import ExitStack

import concourse.bass as bass
import concourse.mybir as mybir
from concourse.bass_utils import run_bass_kernel_spmd

F32 = mybir.dt.float32
BF16 = mybir.dt.bfloat16
AF = mybir.ActivationFunctionType
ALU = mybir.AluOpType
AX = mybir.AxisListType

B, S, D = 16, 2048, 1024
NCORE = 8
BC = B // NCORE
T = BC * S
CH = 256
NCH = S // CH
NCHT = BC * NCH
NHM, PD, NST = 32, 64, 128
DIN, CONVD, DIP = 2048, 2304, 4384
NTC = 18
NHA, HH = 16, 32
ER = 7
EPS = 1e-5
NEG = -1.0e9


class Sem:
    def __init__(self, nc, es, name):
        self.h = es.enter_context(nc.semaphore(name))
        self.n = 0
    def inc(self, inst, k=1):
        inst.then_inc(self.h, k)
        self.n += k
    def wait(self, eng, val=None):
        v = self.n if val is None else val
        if v > 0:
            eng.wait_ge(self.h, v)


def build_program(dbg=False, upto=4):
    nc = bass.Bass("TRN2", target_bir_lowering=False, detect_race_conditions=False)
    es = ExitStack()

    def sb(name, shape, dt=F32):
        return es.enter_context(nc.sbuf_tensor(name, shape, dt))
    def sem(name):
        return Sem(nc, es, name)

    dk = "ExternalOutput" if dbg else "Internal"
    di = lambda n, shp, dt=F32: nc.dram_tensor(n, shp, dt, kind="ExternalInput").ap()

    d_x = di("x_fm", [D, T])
    d_winT = di("w_inT", [D, DIP])
    d_convw = di("convw_r", [128, NTC * 4])
    d_convb = di("convb_r", [128, NTC])
    d_dtbias = di("dt_bias_c", [32, 1])
    d_alog = di("a_log_c", [32, 1])
    d_dD = di("dD_r", [128, 16])
    d_normw = di("normw_r", [128, 16])
    d_woutT = di("w_outT", [DIN, D])
    d_w1 = di("w1_r", [128, 8])
    d_w2 = di("w2_r", [128, 8])
    d_wkvd = di("wkvd", [D, 256])
    d_wqd = di("wqd", [D, 256])
    d_wqint = di("w_qint", [256, D])
    d_wkuexp = di("wku_exp", [256, D])
    d_wrkexp = di("wropek_exp", [D, D])
    d_wvu = di("wvu", [256, D])
    d_wo = di("wo", [D, D])
    d_wgs = di("wg_s", [D, D])
    d_wus = di("wu_s", [D, D])
    d_wds = di("wd_s", [D, D])
    d_wgr = di("wg_r", [ER, D, D])
    d_wur = di("wu_r", [ER, D, D])
    d_wdr = di("wd_r", [ER, D, D])
    d_wrout = di("wrouter", [D, ER])
    d_rbias = di("rbias_c", [ER, 1])
    d_identb = di("ident_bf", [128, 128], BF16)
    d_identf = di("ident_f", [128, 128])
    d_onescol = di("onescol", [128, 1], BF16)
    d_esel = di("esel32", [128, 32 * 128], BF16)
    d_maskneg = di("maskneg", [128, 2 * CH], BF16)
    d_amask = di("attn_mask", [128, 4 * 512], BF16)
    d_ctab = di("rope_cos", [128, S], BF16)
    d_stab = di("rope_sin", [128, S], BF16)

    s_xn = nc.dram_tensor("s_xn", [D, T], BF16, kind=dk).ap()
    s_sz = nc.dram_tensor("s_sz", [DIN, T], BF16, kind=dk).ap()
    s_cvo = nc.dram_tensor("s_cvo", [CONVD, T], BF16, kind=dk).ap()
    s_dtr = nc.dram_tensor("s_dtr", [32, T], F32, kind=dk).ap()
    s_xbc = nc.dram_tensor("s_xbc", [CONVD, T], BF16, kind=dk).ap() if dbg else None
    s_part = nc.dram_tensor("s_part", [D, T], F32, kind=dk).ap()
    s_part2 = nc.dram_tensor("s_part2", [D, T], F32, kind=dk).ap()
    s_rms2 = nc.dram_tensor("s_rms2", [D, T], BF16, kind=dk).ap()
    s_wts = nc.dram_tensor("s_wts", [32, T], BF16, kind="ExternalOutput").ap() if dbg else None
    d_out = nc.dram_tensor("out_fm", [D, T], F32, kind="ExternalOutput").ap()

    d_x3 = d_x.rearrange("(k p) n -> p k n", p=128)
    s_xn3 = s_xn.rearrange("(k p) n -> p k n", p=128)
    s_sz3 = s_sz.rearrange("(k p) n -> p k n", p=128)
    s_cvo3 = s_cvo.rearrange("(k p) n -> p k n", p=128)
    s_part3 = s_part.rearrange("(k p) n -> p k n", p=128)
    s_part23 = s_part2.rearrange("(k p) n -> p k n", p=128)
    s_rms23 = s_rms2.rearrange("(k p) n -> p k n", p=128)
    d_out3 = d_out.rearrange("(k p) n -> p k n", p=128)

    identb = sb("identb", [128, 128], BF16)
    identf = sb("identf", [128, 128])
    onescol = sb("onescol_s", [128, 1], BF16)
    esel = sb("esel_s", [128, 32 * 128], BF16)
    maskneg = sb("maskneg_s", [128, 2 * CH], BF16)
    convw = sb("convw_s", [128, NTC * 4])
    convb = sb("convb_s", [128, NTC])
    dtbias = sb("dtbias_s", [32, 1])
    aneg = sb("aneg_s", [32, 1])
    dDr = sb("dD_s", [128, 16])
    normw = sb("normw_s", [128, 16])
    w1r = sb("w1_s", [128, 8])
    w2r = sb("w2_s", [128, 8])
    rbias = sb("rbias_s", [32, 1])
    zer32 = sb("zer32", [32, CH])
    cstp = sb("cstp", [128, 2])
    rows_f = sb("rows_f", [128, 2048])
    rows_b = sb("rows_b", [128, 1024], BF16)

    s_boot = sem("s_boot")
    s_bootc = sem("s_bootc")
    nc.const_aps.register(F32, EPS, cstp[:, 0:1])
    nc.const_aps.register(F32, 1.0, cstp[:, 1:2])

    with nc.Block() as blk:
        def g_boot(g):
            for dst, src in [(identb[:, :], d_identb), (identf[:, :], d_identf),
                             (onescol[:, :], d_onescol),
                             (esel[:, :], d_esel),
                             (maskneg[:, :], d_maskneg),
                             (convw[:, :], d_convw), (convb[:, :], d_convb),
                             (dtbias[:, :], d_dtbias),
                             (aneg[:, :], d_alog), (dDr[:, :], d_dD), (normw[:, :], d_normw),
                             (w1r[:, :], d_w1), (w2r[:, :], d_w2), (rbias[0:ER, :], d_rbias)]:
                g.dma_start(dst, src).then_inc(s_boot.h, 16)
            g.memset(zer32[:, :], 0.0)
            g.memset(rows_f[:, :], 0.0)
            g.memset(rows_b[:, :], 0.0)
            g.memset(cstp[:, 0:1], EPS)
            g.memset(cstp[:, 1:2], 1.0)
        s_boot.n += 14 * 16
        blk.gpsimd(g_boot)
        def s_bootf(s):
            s.wait_ge(s_boot.h, s_boot.n)
            s.activation(aneg[:, :], aneg[:, :], AF.Exp).then_inc(s_bootc.h, 1)
        s_bootc.n += 1
        blk.scalar(s_bootf)
    with nc.Block() as blk:
        def v_bootf(v):
            v.tensor_scalar(aneg[:, :], aneg[:, :], -1.0, None, ALU.mult)
        blk.vector(v_bootf)

    if upto >= 1:
        _phase_ab1(nc, locals())
    if upto >= 2:
        _phase_ab2(nc, locals())
    if upto >= 3:
        _phase_c(nc, locals())
    _phase_d(nc, locals())

    es.close()
    return nc


def _phase_ab1(nc, g):
    stage = 99
    neuter = set()
    es1 = ExitStack()
    def sb1(name, shape, dt=F32):
        return es1.enter_context(nc.sbuf_tensor(name, shape, dt))
    def ps1(name, shape, dt=F32):
        return es1.enter_context(nc.psum_tensor(name, shape, dt))
    def sem(name):
        return Sem(nc, es1, name)

    esel, maskneg, convw, convb, dtbias = g["esel"], g["maskneg"], g["convw"], g["convb"], g["dtbias"]
    onescol, rows_f, rows_b, s_boot = g["onescol"], g["rows_f"], g["rows_b"], g["s_boot"]
    s_bootc = g["s_bootc"]
    d_x3, s_xn3, s_sz3, s_cvo3, s_dtr = g["d_x3"], g["s_xn3"], g["s_sz3"], g["s_cvo3"], g["s_dtr"]
    d_winT = g["d_winT"]
    s_xbc = g.get("s_xbc")
    o_xb = sem("o_xb")

    w_inT = sb1("w_inT_s", [128, 8 * DIP], BF16)
    x_sb = [sb1(f"x{i}", [128, 8 * CH]) for i in range(2)]
    xn_sb = [sb1(f"xn{i}", [128, 8 * CH], BF16) for i in range(2)]
    sz_sb = sb1("sz", [128, 16 * CH], BF16)
    xbc_sb = sb1("xbc", [128, NTC * 260], BF16)
    cvo_sb = sb1("cvo", [128, NTC * CH], BF16)
    dtr_sb = sb1("dtr", [32, CH])
    lnb_p = sb1("lnb", [128, 4 * CH], BF16)
    lnq_p = sb1("lnq", [128, 4 * CH], BF16)
    xnb_p = sb1("xnb", [128, 2 * CH])
    cacc_p = sb1("cacc", [128, 2 * CH])

    pb_ln = ps1("pb_ln1", [128, 512])
    pb_tr = ps1("pb_tr1", [128, 512])
    pb_f0 = ps1("pb_f01", [128, 512])
    pb_f1 = ps1("pb_f11", [128, 512])

    a_w = sem("a_w")
    a_wc = sem("a_wc")
    a_x = sem("a_x")
    a_xbf = sem("a_xbf")
    a_xsq = sem("a_xsq")
    a_lnmm = sem("a_lnmm")
    a_srow = sem("a_srow")
    a_vrow = sem("a_vrow")
    a_bc = sem("a_bc")
    a_xn = sem("a_xn")
    a_ip = sem("a_ip")
    a_ze = sem("a_ze")
    a_xe = sem("a_xe")
    a_cv = sem("a_cv")
    a_cs = sem("a_cs")
    o_xn = sem("o_xn")
    o_sz = sem("o_sz")
    o_cv = sem("o_cv")
    o_dt = sem("o_dt")

    xbc3 = xbc_sb[:, :].rearrange("p (t n) -> p t n", t=NTC)

    with nc.Block() as blk:
        def g_w(gp):
            gp.dma_start(w_inT[:, :].rearrange("p (k n) -> p k n", k=8),
                         d_winT.rearrange("(k p) n -> p k n", p=128)).then_inc(a_w.h, 16)
            gp.wait_ge(a_w.h, 16)
            gp.nop().then_inc(a_wc.h, 1)
        a_w.n += 16
        a_wc.n += 1
        blk.gpsimd(g_w)

        for c in range(NCHT):
            cc, i = c % NCH, c % 2
            c0 = c * CH

            def sp_ld(sy, c=c, i=i, c0=c0):
                if c >= 2:
                    a_xn.wait(sy, 8 * (c - 1))
                    a_lnmm.wait(sy, 8 * (c - 1))
                sy.dma_start(x_sb[i][:, :].rearrange("p (k n) -> p k n", k=8),
                             d_x3[:, :, c0:c0 + CH]).then_inc(a_x.h, 16)
            if stage >= 1:
                a_x.n += 16
                blk.sync(sp_ld)

            def v_xbf(v, c=c, i=i):
                a_x.wait(v, 16 * (c + 1))
                for k in range(8):
                    q = c * 8 + k
                    if q >= 4:
                        a_lnmm.wait(v, q - 4 + 1)
                    v.tensor_copy(lnb_p[:, (k % 4) * CH:(k % 4 + 1) * CH],
                                  x_sb[i][:, k * CH:(k + 1) * CH]).then_inc(a_xbf.h, 1)
            if stage >= 2:
                a_xbf.n += 8
                blk.vector(v_xbf)

            def s_xsq(s, c=c, i=i):
                a_x.wait(s, 16 * (c + 1))
                for k in range(8):
                    q = c * 8 + k
                    if q >= 4:
                        a_lnmm.wait(s, q - 4 + 1)
                    s.activation(lnq_p[:, (k % 4) * CH:(k % 4 + 1) * CH],
                                 x_sb[i][:, k * CH:(k + 1) * CH], AF.Square).then_inc(a_xsq.h, 1)
            if stage >= 2:
                a_xsq.n += 8
                blk.scalar(s_xsq)

            def te_ln(t, c=c):
                s_bootc.wait(t)
                if c >= 1:
                    a_srow.wait(t, 2 * c)
                    a_xn.wait(t, 8 * c)
                for k in range(8):
                    q = c * 8 + k
                    a_xbf.wait(t, q + 1)
                    a_xsq.wait(t, q + 1)
                    t.matmul(pb_ln[0:1, 0:CH], onescol[:, :], lnb_p[:, (k % 4) * CH:(k % 4 + 1) * CH],
                             start=(k == 0), stop=(k == 7))
                    t.matmul(pb_tr[0:1, 256:512], onescol[:, :], lnq_p[:, (k % 4) * CH:(k % 4 + 1) * CH],
                             start=(k == 0), stop=(k == 7)).then_inc(a_lnmm.h, 1)
            if stage >= 3:
                a_lnmm.n += 8
                blk.tensor(te_ln)

            bp = 32 * i
            RM = rows_f[bp:bp + 1, 0 * CH:1 * CH]
            REX = rows_f[bp:bp + 1, 1 * CH:2 * CH]
            RMSQ = rows_f[bp:bp + 1, 2 * CH:3 * CH]
            RVAR = rows_f[bp:bp + 1, 3 * CH:4 * CH]
            RLN = rows_f[bp:bp + 1, 4 * CH:5 * CH]
            RSTD = rows_f[bp:bp + 1, 5 * CH:6 * CH]
            RT1 = rows_f[bp:bp + 1, 6 * CH:7 * CH]
            RT2 = rows_f[bp:bp + 1, 7 * CH:8 * CH]
            MH = rows_b[bp:bp + 1, 0 * CH:1 * CH]
            ML = rows_b[bp:bp + 1, 1 * CH:2 * CH]
            SH = rows_b[bp:bp + 1, 2 * CH:3 * CH]
            SL = rows_b[bp:bp + 1, 3 * CH:4 * CH]

            def s_row1(s, c=c, RM=RM, REX=REX, RMSQ=RMSQ):
                a_lnmm.wait(s, 8 * (c + 1))
                if c >= 2:
                    a_bc.wait(s, c - 1)
                s.mul(RM, pb_ln[0:1, 0:CH], 1.0 / 1024.0)
                s.mul(REX, pb_tr[0:1, 256:512], 1.0 / 1024.0)
                s.activation(RMSQ, RM, AF.Square).then_inc(a_srow.h, 1)
            if stage >= 4:
                a_srow.n += 1
                blk.scalar(s_row1)
            def v_row1(v, REX=REX, RMSQ=RMSQ, RVAR=RVAR, n0=a_srow.n):
                v.wait_ge(a_srow.h, n0)
                v.tensor_sub(RVAR, REX, RMSQ).then_inc(a_vrow.h, 1)
            if stage >= 4:
                a_vrow.n += 1
                blk.vector(v_row1)
            def s_row2(s, RVAR=RVAR, RLN=RLN, RSTD=RSTD, n0=a_vrow.n):
                s.wait_ge(a_vrow.h, n0)
                s.activation(RLN, RVAR, AF.Ln, bias=EPS)
                s.activation(RSTD, RLN, AF.Exp, scale=-0.5).then_inc(a_srow.h, 1)
            if stage >= 4:
                a_srow.n += 1
                blk.scalar(s_row2)
            def v_row2(v, RM=RM, RSTD=RSTD, MH=MH, ML=ML, SH=SH, SL=SL, RT1=RT1, RT2=RT2,
                       n0=a_srow.n):
                v.wait_ge(a_srow.h, n0)
                v.tensor_copy(MH, RM)
                v.tensor_sub(RT1, RM, MH)
                v.tensor_copy(ML, RT1)
                v.tensor_copy(SH, RSTD)
                v.tensor_sub(RT2, RSTD, SH)
                v.tensor_copy(SL, RT2).then_inc(a_vrow.h, 1)
            if stage >= 4:
                a_vrow.n += 1
                blk.vector(v_row2)

            def te_bc(t, c=c, i=i, n0=a_vrow.n):
                t.wait_ge(a_vrow.h, n0)
                if c >= 1:
                    a_xn.wait(t, 8 * c)
                bp = 32 * i
                rt = rows_b[bp:bp + 32, :]
                esl = esel[bp:bp + 32, 0:128]
                t.matmul(pb_ln[:, 256:512], esl, rt[:, 0 * CH:1 * CH], start=True, stop=False)
                t.matmul(pb_ln[:, 256:512], esl, rt[:, 1 * CH:2 * CH], start=False, stop=True)
                t.matmul(pb_tr[:, 0:256], esl, rt[:, 2 * CH:3 * CH], start=True, stop=False)
                t.matmul(pb_tr[:, 0:256], esl, rt[:, 3 * CH:4 * CH], start=False, stop=True).then_inc(a_bc.h, 1)
            if stage >= 5:
                a_bc.n += 1
                blk.tensor(te_bc)

            def v_xn(v, c=c, i=i, n0=a_bc.n):
                v.wait_ge(a_bc.h, n0)
                if c >= 2:
                    o_xn.wait(v, 16 * (c - 1))
                    a_ip.wait(v, 35 * (c - 1))
                for k in range(8):
                    v.tensor_sub(xnb_p[:, (k % 2) * CH:(k % 2 + 1) * CH],
                                 x_sb[i][:, k * CH:(k + 1) * CH], pb_ln[:, 256:512])
                    v.tensor_tensor(xn_sb[i][:, k * CH:(k + 1) * CH],
                                    xnb_p[:, (k % 2) * CH:(k % 2 + 1) * CH],
                                    pb_tr[:, 0:256], ALU.mult).then_inc(a_xn.h, 1)
            if stage >= 6:
                a_xn.n += 8
                blk.vector(v_xn)

            def sp_sxn(sy, c=c, i=i, c0=c0):
                a_xn.wait(sy, 8 * (c + 1))
                sy.dma_start(s_xn3[:, :, c0:c0 + CH],
                             xn_sb[i][:, :].rearrange("p (k n) -> p k n", k=8)).then_inc(o_xn.h, 16)
            if stage >= 6:
                o_xn.n += 16
                blk.sync(sp_sxn)

            def te_ip(t, c=c, i=i):
                a_wc.wait(t, 1)
                a_xn.wait(t, 8 * (c + 1))
                for m in range(35):
                    if m >= 2:
                        mm = m - 2
                        if mm < 16:
                            a_ze.wait(t, c * 16 + mm + 1)
                        elif mm < 34:
                            a_xe.wait(t, c * 20 + 1 + (mm - 16) + 1)
                        else:
                            a_xe.wait(t, c * 20 + 20)
                    elif c >= 1:
                        a_xe.wait(t, (c - 1) * 20 + 20 if m == 0 else (c - 1) * 20 + 19)
                    pb = pb_f0 if (m % 2 == 0) else pb_f1
                    mw = 128 if m < 34 else 32
                    off = m * 128
                    if "m34" in neuter and m == 34:
                        mw, off = 128, 4224
                    for k in range(8):
                        mm = t.matmul(pb[0:mw, 0:CH], w_inT[:, k * DIP + off:k * DIP + off + mw],
                                      xn_sb[i][:, k * CH:(k + 1) * CH],
                                      start=(k == 0), stop=(k == 7))
                    mm.then_inc(a_ip.h, 1)
            if stage >= 7:
                a_ip.n += 35
                blk.tensor(te_ip)

            def s_zev(s, c=c, i=i):
                o_sz.wait(s, 16 * c)
                for m in range(16):
                    a_ip.wait(s, c * 35 + m + 1)
                    pb = pb_f0 if (m % 2 == 0) else pb_f1
                    if "zsilu" in neuter:
                        s.activation(sz_sb[:, m * CH:(m + 1) * CH], x_sb[i][:, 0:CH], AF.Silu).then_inc(a_ze.h, 1)
                    else:
                        s.activation(sz_sb[:, m * CH:(m + 1) * CH], pb[:, 0:CH], AF.Silu).then_inc(a_ze.h, 1)
            if stage >= 8:
                a_ze.n += 16
                blk.scalar(s_zev)

            def sp_ssz(sy, c=c, c0=c0):
                a_ze.wait(sy, 16 * (c + 1))
                if "szdma" in neuter:
                    sy.sem_inc(o_sz.h, 16)
                else:
                    sy.dma_start(s_sz3[:, :, c0:c0 + CH],
                                 sz_sb[:, :].rearrange("p (k n) -> p k n", k=16)).then_inc(o_sz.h, 16)
            if stage >= 8:
                o_sz.n += 16
                blk.sync(sp_ssz)

            def v_xev(v, c=c, cc=cc):
                if c >= 1:
                    a_cv.wait(v, NTC * c)
                if cc == 0:
                    v.memset(xbc3[:, :, 0:3], 0.0).then_inc(a_xe.h, 1)
                else:
                    v.tensor_copy(xbc3[:, :, 0:3], xbc3[:, :, 256:259]).then_inc(a_xe.h, 1)
                for m in range(16, 34):
                    tt = m - 16
                    a_ip.wait(v, c * 35 + m + 1)
                    pb = pb_f0 if (m % 2 == 0) else pb_f1
                    v.tensor_copy(xbc3[:, tt, 3:259], pb[:, 0:CH]).then_inc(a_xe.h, 1)
                a_ip.wait(v, c * 35 + 35)
                o_dt.wait(v, 16 * c)
                v.tensor_copy(dtr_sb[0:32, :], pb_f0[0:32, 0:CH]).then_inc(a_xe.h, 1)
            if stage >= 9:
                a_xe.n += 20
                blk.vector(v_xev)

            def sp_sdt(sy, c=c, c0=c0):
                a_xe.wait(sy, c * 20 + 20)
                sy.dma_start(s_dtr[0:32, c0:c0 + CH], dtr_sb[0:32, :]).then_inc(o_dt.h, 16)
            if stage >= 9:
                o_dt.n += 16
                blk.sync(sp_sdt)

            def v_conv(v, c=c):
                for tt in range(NTC):
                    a_xe.wait(v, c * 20 + 1 + tt + 1)
                    if c * NTC + tt >= 2:
                        a_cs.wait(v, c * NTC + tt - 2 + 1)
                    ac = cacc_p[:, (tt % 2) * CH:(tt % 2 + 1) * CH]
                    v.tensor_scalar(ac, xbc3[:, tt, 0:256], convw[:, tt * 4:tt * 4 + 1], None, ALU.mult)
                    for k in range(1, 4):
                        mm = v.scalar_tensor_tensor(ac, xbc3[:, tt, k:k + 256],
                                                    convw[:, tt * 4 + k:tt * 4 + k + 1],
                                                    ac, ALU.mult, ALU.add)
                    mm.then_inc(a_cv.h, 1)
            if stage >= 10:
                a_cv.n += NTC
                blk.vector(v_conv)

            def s_cvs(s, c=c):
                o_cv.wait(s, 16 * c)
                for tt in range(NTC):
                    a_cv.wait(s, c * NTC + tt + 1)
                    s.activation(cvo_sb[:, tt * CH:(tt + 1) * CH],
                                 cacc_p[:, (tt % 2) * CH:(tt % 2 + 1) * CH], AF.Silu,
                                 bias=convb[:, tt:tt + 1]).then_inc(a_cs.h, 1)
            if stage >= 11:
                a_cs.n += NTC
                blk.scalar(s_cvs)

            def sp_sxb(sy, c=c, c0=c0):
                a_cv.wait(sy, NTC * (c + 1))
                sy.dma_start(s_xbc.rearrange("(k p) n -> p k n", p=128)[:, :, c0:c0 + CH],
                             xbc_sb[:, :].rearrange("p (t n) -> p t n", t=NTC)[:, :, 3:259]).then_inc(o_xb.h, 16)
            if s_xbc is not None:
                o_xb.n += 16
                blk.sync(sp_sxb)

            def sp_scv(sy, c=c, c0=c0):
                a_cs.wait(sy, NTC * (c + 1))
                sy.dma_start(s_cvo3[:, :, c0:c0 + CH],
                             cvo_sb[:, :].rearrange("p (k n) -> p k n", k=NTC)).then_inc(o_cv.h, 16)
            if stage >= 11:
                o_cv.n += 16
                blk.sync(sp_scv)

        fulls = [(a_x, 16), (a_xbf, 8), (a_xsq, 8), (a_lnmm, 8), (a_srow, 2), (a_vrow, 2),
                 (a_bc, 1), (a_xn, 8), (a_ip, 35), (a_ze, 16), (a_xe, 20), (a_cv, 18),
                 (a_cs, 18), (o_xn, 16), (o_sz, 16), (o_cv, 16), (o_dt, 16)]
        need_fill = [(sm, per * NCHT - sm.n) for sm, per in fulls if per * NCHT - sm.n > 0]
        if need_fill:
            def g_fill(gp):
                for sm, d in need_fill:
                    gp.sem_inc(sm.h, d)
                    sm.n += 0
            blk.gpsimd(g_fill)
            for sm, d in need_fill:
                sm.n += d
        def sp_fin(sy):
            o_xn.wait(sy)
            o_sz.wait(sy)
            o_cv.wait(sy)
            o_dt.wait(sy)
        blk.sync(sp_fin)
    es1.close()


def _phase_ab2(nc, g):
    es2 = ExitStack()
    def sb2(name, shape, dt=F32):
        return es2.enter_context(nc.sbuf_tensor(name, shape, dt))
    def ps2(name, shape, dt=F32):
        return es2.enter_context(nc.psum_tensor(name, shape, dt))
    def sem(name):
        return Sem(nc, es2, name)

    esel, maskneg, dtbias, aneg = g["esel"], g["maskneg"], g["dtbias"], g["aneg"]
    onescol, identf, identb = g["onescol"], g["identf"], g["identb"]
    dDr, normw, zer32 = g["dDr"], g["normw"], g["zer32"]
    rows_f, rows_b = g["rows_f"], g["rows_b"]
    d_x3, s_sz3, s_cvo3, s_dtr, s_part3 = g["d_x3"], g["s_sz3"], g["s_cvo3"], g["s_dtr"], g["s_part3"]
    d_woutT = g["d_woutT"]

    w_outT = sb2("w_outT_s", [128, 16 * D], BF16)
    x_sb = [sb2(f"bx{i}", [128, 8 * CH], BF16) for i in range(2)]
    sz_sb = [sb2(f"bsz{i}", [128, 16 * CH], BF16) for i in range(2)]
    cvo_sb = [sb2(f"bcvo{i}", [128, NTC * CH], BF16) for i in range(2)]
    dtf = sb2("bdtf", [32, 12 * CH])
    cshl = sb2("bcshl", [32, 4 * CH], BF16)
    ecsb = sb2("becs", [32, 2 * CH], BF16)
    trn = [sb2(f"btrn{i}", [128, 8 * 32]) for i in range(2)]
    xtm_sb = sb2("bxtm", [128, 2 * DIN], BF16)
    btm_sb = sb2("bbtm", [128, 2 * 128], BF16)
    gt_sb = sb2("bgt", [128, 2 * CH], BF16)
    carry = sb2("bcarry", [128, DIN])
    t_sb = sb2("bt", [128, 16 * CH], BF16)
    part_sb = sb2("bpart", [128, 4 * CH])
    arg_p = sb2("barg", [128, 4 * CH])
    lh_p = sb2("blh", [128, 4 * CH], BF16)
    m_p = sb2("bm", [128, 4 * CH], BF16)
    csc_p = sb2("bcsc", [128, 2 * CH], BF16)
    xds_p = sb2("bxds", [128, 4 * 64], BF16)
    cbf_p = sb2("bcbf", [128, 2 * 64], BF16)
    cup_p = sb2("bcup", [128, 2 * 64])
    tv_p = sb2("btv", [128, 2 * CH])
    tsq_p = sb2("btsq", [128, 2 * CH], BF16)

    pb_t0 = ps2("pb_t0", [128, 512])
    pb_t1 = ps2("pb_t1", [128, 512])
    pb_o = [ps2("pb_bo0", [128, 512]), ps2("pb_bo1", [128, 512])]
    pb_y = [ps2("pb_by0", [128, 512]), ps2("pb_by1", [128, 512])]
    pb_l2 = ps2("pb_l2", [128, 512])

    b_w = sem("b_w")
    b_wc = sem("b_wc")
    b_ld = sem("b_ld")
    b_ldc = sem("b_ldc")
    b_lx = sem("b_lx")
    b_ds = sem("b_ds")
    b_dv = sem("b_dv")
    b_tr = sem("b_tr")
    b_te = sem("b_te")
    b_ho = sem("b_ho")
    b_ha = sem("b_ha")
    b_he = sem("b_he")
    b_hm = sem("b_hm")
    b_hy = sem("b_hy")
    b_hc = sem("b_hc")
    b_ye = sem("b_ye")
    b_ts = sem("b_ts")
    b_t2 = sem("b_t2")
    b_srow = sem("b_srow")
    b_vrow = sem("b_vrow")
    b_bc = sem("b_bc")
    b_yn = sem("b_yn")
    b_wog = sem("b_wog")
    b_ptt = sem("b_ptt")
    b_so = sem("b_so")

    with nc.Block() as blk:
        def g_w(gp):
            gp.dma_start(w_outT[:, :].rearrange("p (k n) -> p k n", k=16),
                         d_woutT.rearrange("(k p) n -> p k n", p=128)).then_inc(b_w.h, 16)
            gp.wait_ge(b_w.h, 16)
            gp.nop().then_inc(b_wc.h, 1)
        b_w.n += 16
        b_wc.n += 1
        blk.gpsimd(g_w)

        for c in range(NCHT):
            cc, i = c % NCH, c % 2
            c0 = c * CH

            def sp_ld(sy, c=c, i=i, c0=c0):
                if c >= 2:
                    b_hy.wait(sy, 32 * (c - 1))
                    b_ye.wait(sy, 16 * (c - 1))
                    b_ptt.wait(sy, 8 * (c - 1))
                    b_ds.wait(sy, 2 * (c - 1))
                    b_tr.wait(sy, 42 * (c - 1))
                sy.dma_start(cvo_sb[i][:, :].rearrange("p (k n) -> p k n", k=NTC),
                             s_cvo3[:, :, c0:c0 + CH]).then_inc(b_ld.h, 16)
                sy.dma_start(sz_sb[i][:, :].rearrange("p (k n) -> p k n", k=16),
                             s_sz3[:, :, c0:c0 + CH]).then_inc(b_ld.h, 16)
                sy.dma_start(dtf[0:32, (0 + 6 * i) * CH:(1 + 6 * i) * CH],
                             s_dtr[0:32, c0:c0 + CH]).then_inc(b_ld.h, 16)
            b_ld.n += 48
            blk.sync(sp_ld)
            def g_ldx(gp, c=c, i=i, c0=c0):
                if c >= 2:
                    b_ptt.wait(gp, 8 * (c - 1))
                gp.dma_start(x_sb[i][:, :].rearrange("p (k n) -> p k n", k=8),
                             d_x3[:, :, c0:c0 + CH]).then_inc(b_lx.h, 16)
                gp.wait_ge(b_ld.h, 48 * (c + 1))
                gp.nop().then_inc(b_ldc.h, 1)
            b_lx.n += 16
            b_ldc.n += 1
            blk.gpsimd(g_ldx)

            def s_dt1(s, c=c, i=i):
                b_ld.wait(s, 48 * (c + 1))
                if c >= 2:
                    b_tr.wait(s, 42 * (c - 1))
                s.activation(dtf[0:32, (1 + 6 * i) * CH:(2 + 6 * i) * CH],
                             dtf[0:32, (0 + 6 * i) * CH:(1 + 6 * i) * CH], AF.Abs, bias=dtbias[:, :])
                s.activation(dtf[0:32, (1 + 6 * i) * CH:(2 + 6 * i) * CH],
                             dtf[0:32, (1 + 6 * i) * CH:(2 + 6 * i) * CH], AF.Exp, scale=-1.0)
                s.activation(dtf[0:32, (1 + 6 * i) * CH:(2 + 6 * i) * CH],
                             dtf[0:32, (1 + 6 * i) * CH:(2 + 6 * i) * CH], AF.Ln, bias=1.0).then_inc(b_ds.h, 1)
            b_ds.n += 1
            blk.scalar(s_dt1)
            def v_dt1(v, c=c, i=i, n0=b_ds.n):
                v.wait_ge(b_ds.h, n0)
                if c >= 2:
                    b_hy.wait(v, 32 * (c - 1))
                v.tensor_scalar(dtf[0:32, (2 + 6 * i) * CH:(3 + 6 * i) * CH],
                                dtf[0:32, (0 + 6 * i) * CH:(1 + 6 * i) * CH], dtbias[:, :], 0.0,
                                ALU.add, ALU.max)
                v.tensor_add(dtf[0:32, (2 + 6 * i) * CH:(3 + 6 * i) * CH],
                             dtf[0:32, (2 + 6 * i) * CH:(3 + 6 * i) * CH],
                             dtf[0:32, (1 + 6 * i) * CH:(2 + 6 * i) * CH])
                v.tensor_scalar(dtf[0:32, (3 + 6 * i) * CH:(4 + 6 * i) * CH],
                                dtf[0:32, (2 + 6 * i) * CH:(3 + 6 * i) * CH], aneg[:, :],
                                None, ALU.mult)
                v.tensor_tensor_scan(dtf[0:32, (4 + 6 * i) * CH:(5 + 6 * i) * CH],
                                     dtf[0:32, (3 + 6 * i) * CH:(4 + 6 * i) * CH],
                                     zer32[:, :], 0.0, ALU.add, ALU.add)
                v.tensor_copy(cshl[0:32, (2 * i) * CH:(2 * i + 1) * CH],
                              dtf[0:32, (4 + 6 * i) * CH:(5 + 6 * i) * CH])
                v.tensor_sub(dtf[0:32, (3 + 6 * i) * CH:(4 + 6 * i) * CH],
                             dtf[0:32, (4 + 6 * i) * CH:(5 + 6 * i) * CH],
                             cshl[0:32, (2 * i) * CH:(2 * i + 1) * CH])
                v.tensor_copy(cshl[0:32, (2 * i + 1) * CH:(2 * i + 2) * CH],
                              dtf[0:32, (3 + 6 * i) * CH:(4 + 6 * i) * CH]).then_inc(b_dv.h, 1)
            b_dv.n += 1
            blk.vector(v_dt1)
            def s_dt2(s, c=c, i=i, n0=b_dv.n):
                s.wait_ge(b_dv.h, n0)
                s.activation(ecsb[0:32, i * CH:(i + 1) * CH],
                             dtf[0:32, (4 + 6 * i) * CH:(5 + 6 * i) * CH], AF.Exp)
                s.activation(dtf[0:32, (5 + 6 * i) * CH:(6 + 6 * i) * CH],
                             dtf[0:32, (4 + 6 * i) * CH:(5 + 6 * i) * CH], AF.Exp,
                             scale=-1.0,
                             bias=dtf[0:32, (4 + 6 * i) * CH + 255:(4 + 6 * i) * CH + 256]).then_inc(b_ds.h, 1)
            b_ds.n += 1
            blk.scalar(s_dt2)

            def te_tr(t, c=c, i=i):
                b_ds.wait(t, 2 * (c + 1))
                b_ldc.wait(t, c + 1)
                if c >= 1:
                    b_ptt.wait(t, 8 * c)
                for n in range(42):
                    pb = pb_t0 if (n % 2 == 0) else pb_t1
                    if n >= 2:
                        b_te.wait(t, c * 43 + n - 2 + 1)
                    if n < 6:
                        src = {0: 4, 1: 4, 2: 2, 3: 2, 4: 5, 5: 5}[n]
                        jj = n % 2
                        t.transpose(pb[0:128, 0:32],
                                    dtf[0:32, (src + 6 * i) * CH + jj * 128:(src + 6 * i) * CH + (jj + 1) * 128],
                                    identf[0:32, 0:32]).then_inc(b_tr.h, 1)
                    elif n < 8:
                        j = n - 6
                        t.transpose(pb[0:128, 0:64].bitcast(BF16),
                                    cvo_sb[i][:, 16 * CH + j * 128:16 * CH + (j + 1) * 128],
                                    identb[:, :]).then_inc(b_tr.h, 1)
                    elif n < 40:
                        tt, j = (n - 8) // 2, (n - 8) % 2
                        t.transpose(pb[0:128, 0:64].bitcast(BF16),
                                    cvo_sb[i][:, tt * CH + j * 128:tt * CH + (j + 1) * 128],
                                    identb[:, :]).then_inc(b_tr.h, 1)
                    else:
                        j = n - 40
                        t.matmul(pb[:, 0:256],
                                 cvo_sb[i][:, 16 * CH + j * 128:16 * CH + (j + 1) * 128],
                                 cvo_sb[i][:, 17 * CH:18 * CH],
                                 start=True, stop=True).then_inc(b_tr.h, 1)
            b_tr.n += 42
            blk.tensor(te_tr)

            def v_tre(v, c=c, i=i):
                if c >= 2:
                    b_hm.wait(v, 64 * (c - 1))
                for n in range(42):
                    if n == 8 and c >= 1:
                        b_hy.wait(v, 32 * c)
                    if n == 40 and c >= 1:
                        b_hm.wait(v, 64 * c)
                    b_tr.wait(v, c * 42 + n + 1)
                    pb = pb_t0 if (n % 2 == 0) else pb_t1
                    if n < 2:
                        op = v.tensor_scalar(trn[i][:, n * 32:(n + 1) * 32],
                                             pb[0:128, 0:32], -1.0, None, ALU.mult)
                    elif n < 6:
                        op = v.tensor_copy(trn[i][:, n * 32:(n + 1) * 32], pb[0:128, 0:32])
                    elif n < 8:
                        op = v.tensor_copy(btm_sb[:, (n - 6) * 128:(n - 5) * 128],
                                           pb[0:128, 0:64].bitcast(BF16))
                    elif n < 40:
                        tt, j = (n - 8) // 2, (n - 8) % 2
                        op = v.tensor_copy(xtm_sb[:, j * DIN + tt * 128:j * DIN + (tt + 1) * 128],
                                           pb[0:128, 0:64].bitcast(BF16))
                    else:
                        j = n - 40
                        op = v.tensor_copy(gt_sb[:, j * CH:(j + 1) * CH], pb[:, 0:256])
                    op.then_inc(b_te.h, 1)
                v.tensor_mul(trn[i][:, 6 * 32:8 * 32], trn[i][:, 2 * 32:4 * 32],
                             trn[i][:, 4 * 32:6 * 32]).then_inc(b_te.h, 1)
            b_te.n += 43
            blk.vector(v_tre)

            if cc == 0:
                def v_cz(v):
                    v.memset(carry[:, :], 0.0)
                blk.vector(v_cz)

            for h in range(NHM):
                p, hb = h // 2, h % 2
                pb = pb_y[p % 2]
                pbo = pb_o[h % 2]
                hq = c * NHM + h
                gp_ = c * 16 + p

                def te_ho(t, c=c, i=i, h=h, hq=hq, pbo=pbo):
                    b_ds.wait(t, 2 * (c + 1))
                    b_dv.wait(t, c + 1)
                    if hq >= 2:
                        b_ha.wait(t, 6 * (hq - 2) + 3)
                    t.matmul(pbo[:, 0:CH], esel[0:32, h * 128:(h + 1) * 128],
                             cshl[0:32, (2 * i) * CH:(2 * i + 1) * CH], start=True, stop=False)
                    t.matmul(pbo[:, 0:CH], esel[0:32, h * 128:(h + 1) * 128],
                             cshl[0:32, (2 * i + 1) * CH:(2 * i + 2) * CH],
                             start=False, stop=True)
                    t.matmul(pbo[:, 256:256 + CH], esel[0:32, h * 128:(h + 1) * 128],
                             ecsb[0:32, i * CH:(i + 1) * CH], start=True, stop=True).then_inc(b_ho.h, 1)
                b_ho.n += 1
                blk.tensor(te_ho)

                def v_ha(v, c=c, i=i, h=h, hq=hq, pbo=pbo):
                    b_ho.wait(v, hq + 1)
                    b_te.wait(v, c * 43 + 43)
                    if hq >= 2:
                        b_he.wait(v, 2 * (hq - 2) + 2)
                        b_hy.wait(v, hq - 1)
                    sl2 = hq % 2
                    for j in range(2):
                        v.scalar_tensor_tensor(arg_p[:, (2 * sl2 + j) * CH:(2 * sl2 + j + 1) * CH],
                                               pbo[:, 0:CH], trn[i][:, j * 32 + h:j * 32 + h + 1],
                                               maskneg[:, j * CH:(j + 1) * CH],
                                               ALU.add, ALU.add).then_inc(b_ha.h, 1)
                    v.tensor_tensor(csc_p[:, sl2 * CH:(sl2 + 1) * CH], cvo_sb[i][:, 17 * CH:18 * CH],
                                    pbo[:, 256:256 + CH], ALU.mult).then_inc(b_ha.h, 1)
                    for j in range(2):
                        v.tensor_scalar(xds_p[:, (2 * sl2 + j) * 64:(2 * sl2 + j + 1) * 64],
                                        xtm_sb[:, j * DIN + h * 64:j * DIN + (h + 1) * 64],
                                        trn[i][:, 6 * 32 + j * 32 + h:6 * 32 + j * 32 + h + 1],
                                        None, ALU.mult).then_inc(b_ha.h, 1)
                    v.tensor_copy(cbf_p[:, sl2 * 64:(sl2 + 1) * 64],
                                  carry[:, h * 64:(h + 1) * 64]).then_inc(b_ha.h, 1)
                b_ha.n += 6
                blk.vector(v_ha)

                def s_he(s, hq=hq):
                    sl2 = hq % 2
                    for j in range(2):
                        b_ha.wait(s, 6 * hq + j + 1)
                        s.activation(lh_p[:, (2 * sl2 + j) * CH:(2 * sl2 + j + 1) * CH],
                                     arg_p[:, (2 * sl2 + j) * CH:(2 * sl2 + j + 1) * CH],
                                     AF.Exp).then_inc(b_he.h, 1)
                b_he.n += 2
                blk.scalar(s_he)

                def v_hm(v, i=i, h=h, hq=hq):
                    sl2 = hq % 2
                    if hq >= 2:
                        b_hy.wait(v, hq - 1)
                    for j in range(2):
                        b_he.wait(v, 2 * hq + j + 1)
                        v.scalar_tensor_tensor(m_p[:, (2 * sl2 + j) * CH:(2 * sl2 + j + 1) * CH],
                                               lh_p[:, (2 * sl2 + j) * CH:(2 * sl2 + j + 1) * CH],
                                               trn[i][:, 2 * 32 + j * 32 + h:2 * 32 + j * 32 + h + 1],
                                               gt_sb[:, j * CH:(j + 1) * CH],
                                               ALU.mult, ALU.mult).then_inc(b_hm.h, 1)
                b_hm.n += 2
                blk.vector(v_hm)

                def te_hy(t, c=c, i=i, h=h, p=p, hb=hb, hq=hq, gp_=gp_, pb=pb):
                    if hb == 0 and gp_ >= 2:
                        b_ye.wait(t, gp_ - 2 + 1)
                        b_hc.wait(t, 2 * (gp_ - 2) + 2)
                    b_ha.wait(t, 6 * hq + 6)
                    b_hm.wait(t, 2 * hq + 2)
                    sl2 = hq % 2
                    t.matmul(pb[hb * 64:(hb + 1) * 64, 0:CH], cbf_p[:, sl2 * 64:(sl2 + 1) * 64],
                             csc_p[:, sl2 * CH:(sl2 + 1) * CH], start=True, stop=False)
                    for j in range(2):
                        t.matmul(pb[hb * 64:(hb + 1) * 64, 0:CH],
                                 xtm_sb[:, j * DIN + h * 64:j * DIN + (h + 1) * 64],
                                 m_p[:, (2 * sl2 + j) * CH:(2 * sl2 + j + 1) * CH],
                                 start=False, stop=(j == 1))
                    for j in range(2):
                        t.matmul(pb[:, 256 + hb * 64:256 + (hb + 1) * 64],
                                 btm_sb[:, j * 128:(j + 1) * 128],
                                 xds_p[:, (2 * sl2 + j) * 64:(2 * sl2 + j + 1) * 64],
                                 start=(j == 0), stop=(j == 1))
                    t.matmul(pb[:, 384 + hb:384 + hb + 1], esel[0:32, h * 128:(h + 1) * 128],
                             ecsb[0:32, i * CH + 255:i * CH + 256], start=True, stop=True).then_inc(b_hy.h, 1)
                b_hy.n += 1
                blk.tensor(te_hy)

                def v_hc(v, h=h, hb=hb, hq=hq, pb=pb):
                    b_hy.wait(v, hq + 1)
                    sl2 = hq % 2
                    v.tensor_scalar(cup_p[:, sl2 * 64:(sl2 + 1) * 64], carry[:, h * 64:(h + 1) * 64],
                                    pb[:, 384 + hb:384 + hb + 1], None, ALU.mult)
                    v.tensor_tensor(carry[:, h * 64:(h + 1) * 64], cup_p[:, sl2 * 64:(sl2 + 1) * 64],
                                    pb[:, 256 + hb * 64:256 + (hb + 1) * 64], ALU.add).then_inc(b_hc.h, 1)
                b_hc.n += 1
                blk.vector(v_hc)

                if hb == 1:
                    def v_ye(v, c=c, i=i, p=p, hq=hq, gp_=gp_, pb=pb):
                        b_hy.wait(v, hq + 1)
                        if c >= 1:
                            if p == 0:
                                b_wog.wait(v, 8 * c)
                            b_ts.wait(v, 16 * (c - 1) + p + 1)
                        slp = p % 2
                        v.scalar_tensor_tensor(tv_p[:, slp * CH:(slp + 1) * CH],
                                               cvo_sb[i][:, p * CH:(p + 1) * CH], dDr[:, p:p + 1],
                                               pb[:, 0:CH], ALU.mult, ALU.add)
                        v.tensor_tensor(t_sb[:, p * CH:(p + 1) * CH], tv_p[:, slp * CH:(slp + 1) * CH],
                                        sz_sb[i][:, p * CH:(p + 1) * CH], ALU.mult).then_inc(b_ye.h, 1)
                    b_ye.n += 1
                    blk.vector(v_ye)

                    def s_ts(s, p=p, gp_=gp_):
                        b_ye.wait(s, gp_ + 1)
                        if gp_ >= 2:
                            b_t2.wait(s, gp_ - 2 + 1)
                        slp = p % 2
                        s.activation(tsq_p[:, slp * CH:(slp + 1) * CH], t_sb[:, p * CH:(p + 1) * CH],
                                     AF.Square).then_inc(b_ts.h, 1)
                    b_ts.n += 1
                    blk.scalar(s_ts)

                    def te_t2(t, c=c, p=p, gp_=gp_):
                        b_ts.wait(t, gp_ + 1)
                        if p == 0 and c >= 1:
                            b_yn.wait(t, 16 * c)
                            b_srow.wait(t, c)
                        t.matmul(pb_l2[0:1, 0:CH], onescol[:, :], tsq_p[:, (p % 2) * CH:(p % 2 + 1) * CH],
                                 start=(p == 0), stop=(p == 15)).then_inc(b_t2.h, 1)
                    b_t2.n += 1
                    blk.tensor(te_t2)

            RS1 = rows_f[64:65, (3 * i) * CH:(3 * i + 1) * CH]
            RS2 = rows_f[64:65, (3 * i + 1) * CH:(3 * i + 2) * CH]
            RT3 = rows_f[64:65, (3 * i + 2) * CH:(3 * i + 3) * CH]
            YH = rows_b[64:65, (2 * i) * CH:(2 * i + 1) * CH]
            YL = rows_b[64:65, (2 * i + 1) * CH:(2 * i + 2) * CH]
            def s_rs(s, c=c, RS1=RS1, RS2=RS2):
                b_t2.wait(s, 16 * (c + 1))
                if c >= 2:
                    b_bc.wait(s, c - 1)
                s.activation(RS1, pb_l2[0:1, 0:CH], AF.Ln, scale=1.0 / 2048.0, bias=EPS)
                s.activation(RS2, RS1, AF.Exp, scale=-0.5).then_inc(b_srow.h, 1)
            b_srow.n += 1
            blk.scalar(s_rs)
            def v_rs(v, RS2=RS2, YH=YH, YL=YL, RT3=RT3, n0=b_srow.n):
                v.wait_ge(b_srow.h, n0)
                v.tensor_copy(YH, RS2)
                v.tensor_sub(RT3, RS2, YH)
                v.tensor_copy(YL, RT3).then_inc(b_vrow.h, 1)
            b_vrow.n += 1
            blk.vector(v_rs)
            def te_rs(t, c=c, i=i, n0=b_vrow.n):
                t.wait_ge(b_vrow.h, n0)
                if c >= 1:
                    b_yn.wait(t, 16 * c)
                t.matmul(pb_l2[:, 256:512], esel[64:96, 0:128],
                         rows_b[64:96, (2 * i) * CH:(2 * i + 1) * CH], start=True, stop=False)
                t.matmul(pb_l2[:, 256:512], esel[64:96, 0:128],
                         rows_b[64:96, (2 * i + 1) * CH:(2 * i + 2) * CH], start=False, stop=True).then_inc(b_bc.h, 1)
            b_bc.n += 1
            blk.tensor(te_rs)
            def v_yn(v, c=c, n0=b_bc.n):
                v.wait_ge(b_bc.h, n0)
                if c >= 1:
                    b_wog.wait(v, 8 * c)
                for m in range(16):
                    v.scalar_tensor_tensor(t_sb[:, m * CH:(m + 1) * CH], t_sb[:, m * CH:(m + 1) * CH],
                                           normw[:, m:m + 1], pb_l2[:, 256:512],
                                           ALU.mult, ALU.mult).then_inc(b_yn.h, 1)
            b_yn.n += 16
            blk.vector(v_yn)

            def te_wo(t, c=c):
                b_wc.wait(t, 1)
                b_yn.wait(t, 16 * (c + 1))
                b_te.wait(t, 43 * (c + 1))
                for m in range(8):
                    if m >= 2:
                        b_ptt.wait(t, c * 8 + m - 2 + 1)
                    pb = pb_t0 if (m % 2 == 0) else pb_t1
                    for k in range(16):
                        mm = t.matmul(pb[:, 0:CH], w_outT[:, k * D + m * 128:k * D + (m + 1) * 128],
                                      t_sb[:, k * CH:(k + 1) * CH], start=(k == 0), stop=(k == 15))
                    mm.then_inc(b_wog.h, 1)
            b_wog.n += 8
            blk.tensor(te_wo)

            def v_pt(v, c=c, i=i):
                b_lx.wait(v, 16 * (c + 1))
                for m in range(8):
                    b_wog.wait(v, c * 8 + m + 1)
                    if m % 4 == 0:
                        b_so.wait(v, 32 * c + 16 * (m // 4))
                    pb = pb_t0 if (m % 2 == 0) else pb_t1
                    v.tensor_tensor(part_sb[:, (m % 4) * CH:(m % 4 + 1) * CH], pb[:, 0:CH],
                                    x_sb[i][:, m * CH:(m + 1) * CH], ALU.add).then_inc(b_ptt.h, 1)
            b_ptt.n += 8
            blk.vector(v_pt)

            def sp_spt(sy, c=c, c0=c0):
                for half in range(2):
                    b_ptt.wait(sy, c * 8 + 4 * (half + 1))
                    sy.dma_start(s_part3[:, 4 * half:4 * (half + 1), c0:c0 + CH],
                                 part_sb[:, :].rearrange("p (k n) -> p k n", k=4)).then_inc(b_so.h, 16)
            b_so.n += 32
            blk.sync(sp_spt)

        def sp_fin(sy):
            b_so.wait(sy)
        blk.sync(sp_fin)
    es2.close()


def _phase_c(nc, g):
    es3 = ExitStack()
    def sb3(name, shape, dt=F32):
        return es3.enter_context(nc.sbuf_tensor(name, shape, dt))
    def ps3(name, shape, dt=F32):
        return es3.enter_context(nc.psum_tensor(name, shape, dt))
    def sem(name):
        return Sem(nc, es3, name)

    esel, onescol = g["esel"], g["onescol"]
    rows_f, rows_b = g["rows_f"], g["rows_b"]
    w1r, w2r = g["w1r"], g["w2r"]
    s_xn3, s_part3, s_part23, s_rms23 = g["s_xn3"], g["s_part3"], g["s_part23"], g["s_rms23"]
    d_amask, d_ctab, d_stab = g["d_amask"], g["d_ctab"], g["d_stab"]
    d_wkvd, d_wqd, d_wqint, d_wkuexp, d_wrkexp, d_wvu, d_wo = (
        g["d_wkvd"], g["d_wqd"], g["d_wqint"], g["d_wkuexp"], g["d_wrkexp"], g["d_wvu"], g["d_wo"])

    amask = sb3("amask_c", [128, 4 * 512], BF16)
    ctab = sb3("ctab_c", [128, S], BF16)
    stab = sb3("stab_c", [128, S], BF16)
    wkvd = sb3("wkvd_c", [128, 8 * 256], BF16)
    wqd = sb3("wqd_c", [128, 8 * 256], BF16)
    wqint = sb3("wqint_c", [128, 2 * 1024], BF16)
    wkuexp = sb3("wkuexp_c", [128, 2 * 1024], BF16)
    wsh = sb3("wsh_c", [128, 8 * 1024], BF16)
    wrkexp = wsh
    wo = wsh
    wvu = sb3("wvu_c", [128, 2 * 1024], BF16)
    qq = sb3("qq_c", [128, 8 * 1024], BF16)
    kk = sb3("kk_c", [128, 8 * 2048], BF16)
    vt = sb3("vt_c", [128, 16 * 1024], BF16)
    ysb = sb3("y_c", [128, 8 * 1024], BF16)
    pp = sb3("pp_c", [128, 2 * 1024], BF16)
    st_av = sb3("stav_c", [128, 1024], BF16)
    xnb = sb3("xnb_c", [128, 8 * 512], BF16)
    xnq = sb3("xnq_c", [128, 2 * 512], BF16)
    rm1 = sb3("rm1_c", [128, 8 * 512], BF16)
    kvl = sb3("kvl_c", [128, 2 * 512], BF16)
    ql = sb3("ql_c", [128, 2 * 512], BF16)
    rop0 = sb3("rop0_c", [128, 512], BF16)
    rop1 = sb3("rop1_c", [128, 512], BF16)
    rop2 = sb3("rop2_c", [128, 512], BF16)
    rop3 = sb3("rop3_c", [128, 512], BF16)
    xn3b = xnb
    x2b = qq[:, 0:8192].bitcast(F32)
    pld = sb3("pld_c", [128, 4 * 512])
    rm2o = rm1

    pA = ps3("pA_c", [128, 1024])
    pB = ps3("pB_c", [128, 1024])
    pC = ps3("pC_c", [128, 1024])
    pL = ps3("pL_c", [128, 1024])
    pD = pA
    pE = pB

    c_w = sem("c_w")
    c_wc = sem("c_wc")
    c_lxn = sem("c_lxn")
    c_xq = sem("c_xq")
    c_s1 = sem("c_s1")
    c_sr = sem("c_sr")
    c_vr = sem("c_vr")
    c_bc = sem("c_bc")
    c_rm = sem("c_rm")
    c_pj = sem("c_pj")
    c_pe = sem("c_pe")
    c_sm = sem("c_sm")
    c_ex = sem("c_ex")
    c_mk = sem("c_mk")
    c_av = sem("c_av")
    c_ave = sem("c_ave")
    c_sr2 = sem("c_sr2")
    c_vr2 = sem("c_vr2")
    c_b2 = sem("c_b2")
    c_ye = sem("c_ye")
    c_wog = sem("c_wog")
    c_x2 = sem("c_x2")
    c_xq3 = sem("c_xq3")
    c_s3 = sem("c_s3")
    c_sr3 = sem("c_sr3")
    c_vr3 = sem("c_vr3")
    c_bc3 = sem("c_bc3")
    c_r2 = sem("c_r2")
    c_p2 = sem("c_p2")
    c_ld3 = sem("c_ld3")
    c_st = sem("c_st")

    bank_last = {}
    def chase(t, bank):
        e = bank_last.get(id(bank))
        if e is not None:
            sm, n = e
            sm.wait(t, n)
    def set_last(bank, sm):
        bank_last[id(bank)] = (sm, sm.n)

    with nc.Block() as blk:
        def g_w(gp):
            gp.dma_start(amask[:, :], d_amask).then_inc(c_w.h, 16)
            gp.dma_start(ctab[:, :], d_ctab).then_inc(c_w.h, 16)
            gp.dma_start(stab[:, :], d_stab).then_inc(c_w.h, 16)
            gp.dma_start(wkvd[:, :].rearrange("p (k n) -> p k n", k=8),
                         d_wkvd.rearrange("(k p) n -> p k n", p=128)).then_inc(c_w.h, 16)
            gp.dma_start(wqd[:, :].rearrange("p (k n) -> p k n", k=8),
                         d_wqd.rearrange("(k p) n -> p k n", p=128)).then_inc(c_w.h, 16)
            gp.dma_start(wqint[:, :].rearrange("p (k n) -> p k n", k=2),
                         d_wqint.rearrange("(k p) n -> p k n", p=128)).then_inc(c_w.h, 16)
            gp.dma_start(wkuexp[:, :].rearrange("p (k n) -> p k n", k=2),
                         d_wkuexp.rearrange("(k p) n -> p k n", p=128)).then_inc(c_w.h, 16)
            gp.dma_start(wvu[:, :].rearrange("p (k n) -> p k n", k=2),
                         d_wvu.rearrange("(k p) n -> p k n", p=128)).then_inc(c_w.h, 16)
            gp.wait_ge(c_w.h, 128)
            gp.memset(vt[:, :], 0.0).then_inc(c_wc.h, 1)
        c_w.n += 128
        c_wc.n += 1
        blk.gpsimd(g_w)
        c_wl = sem("c_wl")
        c_wlc = sem("c_wlc")
        def _load_wsh(dsrc):
            def g_l(gp, dsrc=dsrc, npj=c_pj.n, nwog=c_wog.n, nw=c_wl.n + 16):
                if npj > 0:
                    gp.wait_ge(c_pj.h, npj)
                if nwog > 0:
                    gp.wait_ge(c_wog.h, nwog)
                gp.dma_start(wsh[:, :].rearrange("p (k n) -> p k n", k=8),
                             dsrc.rearrange("(k p) n -> p k n", p=128)).then_inc(c_wl.h, 16)
                gp.wait_ge(c_wl.h, nw)
                gp.nop().then_inc(c_wlc.h, 1)
            blk.gpsimd(g_l)
            c_wl.n += 16
            c_wlc.n += 1

        CR1 = rows_f[0:1, 0:512]
        CR2 = rows_f[0:1, 512:1024]
        CRL = rows_f[0:1, 1024:2048]
        CBH = rows_b[0:1, 0:512]
        CBL = rows_b[0:1, 512:1024]

        for sq in range(BC):
            base = sq * S
            for qh in range(2):
                _load_wsh(d_wrkexp)
                nw_c1 = c_wlc.n
                for bb in range(2):
                    b = qh * 2 + bb
                    gb = (sq * 2 + qh) * 2 + bb
                    t0 = base + b * 512

                    def sp_ld(sy, gb=gb, t0=t0):
                        if gb >= 1:
                            c_rm.wait(sy, 8 * gb)
                        sy.dma_start(xnb[:, :].rearrange("p (k n) -> p k n", k=8),
                                     s_xn3[:, :, t0:t0 + 512]).then_inc(c_lxn.h, 16)
                    c_lxn.n += 16
                    blk.sync(sp_ld)

                    def s_xq(s, gb=gb, ns3=c_s3.n):
                        c_lxn.wait(s, 16 * (gb + 1))
                        if ns3 > 0:
                            s.wait_ge(c_s3.h, ns3)
                        for k in range(8):
                            q = gb * 8 + k
                            if q >= 2:
                                c_s1.wait(s, q - 2 + 1)
                            s.activation(xnq[:, (k % 2) * 512:(k % 2 + 1) * 512],
                                         xnb[:, k * 512:(k + 1) * 512], AF.Square).then_inc(c_xq.h, 1)
                    c_xq.n += 8
                    blk.scalar(s_xq)

                    def te_s1(t, gb=gb):
                        c_wc.wait(t, 1)
                        chase(t, pL)
                        for k in range(8):
                            t.matmul(pL[0:1, 0:512], onescol[:, :],
                                     xnq[:, (k % 2) * 512:(k % 2 + 1) * 512],
                                     start=(k == 0), stop=(k == 7)).then_inc(c_s1.h, 1)
                    c_s1.n += 8
                    blk.tensor(te_s1)

                    def s_r1(s, n0=c_s1.n):
                        s.wait_ge(c_s1.h, n0)
                        s.activation(CR1, pL[0:1, 0:512], AF.Sqrt, scale=1.0 / 1024.0)
                        s.activation(CR2, CR1, AF.Ln, bias=EPS)
                        s.activation(CR1, CR2, AF.Exp, scale=-1.0).then_inc(c_sr.h, 1)
                    c_sr.n += 1
                    blk.scalar(s_r1)
                    def v_r1(v, n0=c_sr.n, nb=c_bc.n, nb2=c_b2.n, nb3=c_bc3.n):
                        v.wait_ge(c_sr.h, n0)
                        if nb: v.wait_ge(c_bc.h, nb)
                        if nb2: v.wait_ge(c_b2.h, nb2)
                        if nb3: v.wait_ge(c_bc3.h, nb3)
                        v.tensor_copy(CBH, CR1)
                        v.tensor_sub(CR2, CR1, CBH)
                        v.tensor_copy(CBL, CR2).then_inc(c_vr.h, 1)
                    c_vr.n += 1
                    blk.vector(v_r1)
                    def te_bc1(t, n0=c_vr.n):
                        t.wait_ge(c_vr.h, n0)
                        chase(t, pL)
                        t.matmul(pL[:, 512:1024], esel[0:32, 0:128], rows_b[0:32, 0:512],
                                 start=True, stop=False)
                        t.matmul(pL[:, 512:1024], esel[0:32, 0:128], rows_b[0:32, 512:1024],
                                 start=False, stop=True).then_inc(c_bc.h, 1)
                    c_bc.n += 1
                    blk.tensor(te_bc1)

                    def v_rm(v, gb=gb, n0=c_bc.n):
                        v.wait_ge(c_bc.h, n0)
                        if gb >= 1:
                            c_pj.wait(v, 28 * gb)
                        for k in range(8):
                            v.scalar_tensor_tensor(rm1[:, k * 512:(k + 1) * 512],
                                                   xnb[:, k * 512:(k + 1) * 512],
                                                   w1r[:, k:k + 1], pL[:, 512:1024],
                                                   ALU.mult, ALU.mult).then_inc(c_rm.h, 1)
                    c_rm.n += 8
                    blk.vector(v_rm)
                    set_last(pL, c_rm)

                    for ot in range(2):
                        pb = pD if (ot % 2 == 0) else pE
                        def te_kv(t, ot=ot, pb=pb, n0=c_rm.n):
                            t.wait_ge(c_rm.h, n0)
                            chase(t, pb)
                            for k in range(8):
                                mm = t.matmul(pb[:, 0:512], wkvd[:, k * 256 + ot * 128:k * 256 + (ot + 1) * 128],
                                              rm1[:, k * 512:(k + 1) * 512],
                                              start=(k == 0), stop=(k == 7))
                            mm.then_inc(c_pj.h, 1)
                        c_pj.n += 1
                        blk.tensor(te_kv)
                        def s_kve(s, ot=ot, pb=pb, n0=c_pj.n):
                            s.wait_ge(c_pj.h, n0)
                            s.copy(kvl[:, ot * 512:(ot + 1) * 512], pb[:, 0:512]).then_inc(c_pe.h, 1)
                        c_pe.n += 1
                        blk.scalar(s_kve)
                        set_last(pb, c_pe)
                    for ot in range(2):
                        pb = pD if (ot % 2 == 0) else pE
                        def te_ql(t, ot=ot, pb=pb):
                            chase(t, pb)
                            for k in range(8):
                                mm = t.matmul(pb[:, 0:512], wqd[:, k * 256 + ot * 128:k * 256 + (ot + 1) * 128],
                                              rm1[:, k * 512:(k + 1) * 512],
                                              start=(k == 0), stop=(k == 7))
                            mm.then_inc(c_pj.h, 1)
                        c_pj.n += 1
                        blk.tensor(te_ql)
                        def s_qle(s, ot=ot, pb=pb, n0=c_pj.n):
                            s.wait_ge(c_pj.h, n0)
                            s.copy(ql[:, ot * 512:(ot + 1) * 512], pb[:, 0:512]).then_inc(c_pe.h, 1)
                        c_pe.n += 1
                        blk.scalar(s_qle)
                        set_last(pb, c_pe)
                    for ot in range(8):
                        pb = pD if (ot % 2 == 0) else pE
                        def te_qq(t, ot=ot, pb=pb, n0=c_pe.n):
                            s = None
                            t.wait_ge(c_pe.h, n0)
                            chase(t, pb)
                            for kt in range(2):
                                mm = t.matmul(pb[:, 0:512], wqint[:, kt * 1024 + ot * 128:kt * 1024 + (ot + 1) * 128],
                                              ql[:, kt * 512:(kt + 1) * 512],
                                              start=(kt == 0), stop=(kt == 1))
                            mm.then_inc(c_pj.h, 1)
                        c_pj.n += 1
                        blk.tensor(te_qq)
                        def v_qqe(v, ot=ot, pb=pb, bb=bb, b=b, n0=c_pj.n):
                            v.wait_ge(c_pj.h, n0)
                            tc = slice(b * 512, (b + 1) * 512)
                            v.tensor_copy(rop0[:, :], pb[:, 0:512])
                            v.stream_shuffle(rop1[:, :], rop0[:, :],
                                             list(range(16, 32)) + list(range(0, 16)))
                            v.tensor_tensor(rop2[:, :], rop1[:, :], stab[:, tc], ALU.mult)
                            v.tensor_tensor(rop3[:, :], rop0[:, :], ctab[:, tc], ALU.mult)
                            v.tensor_tensor(qq[:, ot * 1024 + bb * 512:ot * 1024 + (bb + 1) * 512],
                                            rop3[:, :], rop2[:, :], ALU.add).then_inc(c_pe.h, 1)
                        c_pe.n += 1
                        blk.vector(v_qqe)
                        set_last(pb, c_pe)
                    for ot in range(8):
                        pb = pD if (ot % 2 == 0) else pE
                        def te_kk(t, ot=ot, pb=pb, nw=nw_c1):
                            t.wait_ge(c_wlc.h, nw)
                            chase(t, pb)
                            for kt in range(2):
                                t.matmul(pb[:, 0:512], wkuexp[:, kt * 1024 + ot * 128:kt * 1024 + (ot + 1) * 128],
                                         kvl[:, kt * 512:(kt + 1) * 512],
                                         start=(kt == 0), stop=False)
                            for k in range(8):
                                mm = t.matmul(pb[:, 0:512], wrkexp[:, k * 1024 + ot * 128:k * 1024 + (ot + 1) * 128],
                                              rm1[:, k * 512:(k + 1) * 512],
                                              start=False, stop=(k == 7))
                            mm.then_inc(c_pj.h, 1)
                        c_pj.n += 1
                        blk.tensor(te_kk)
                        def v_kke(v, ot=ot, pb=pb, b=b, n0=c_pj.n):
                            v.wait_ge(c_pj.h, n0)
                            tc = slice(b * 512, (b + 1) * 512)
                            v.tensor_copy(rop0[:, :], pb[:, 0:512])
                            v.stream_shuffle(rop1[:, :], rop0[:, :],
                                             list(range(16, 32)) + list(range(0, 16)))
                            v.tensor_tensor(rop2[:, :], rop1[:, :], stab[:, tc], ALU.mult)
                            v.tensor_tensor(rop3[:, :], rop0[:, :], ctab[:, tc], ALU.mult)
                            v.tensor_tensor(kk[:, ot * 2048 + b * 512:ot * 2048 + (b + 1) * 512],
                                            rop3[:, :], rop2[:, :], ALU.add).then_inc(c_pe.h, 1)
                        c_pe.n += 1
                        blk.vector(v_kke)
                        set_last(pb, c_pe)
                    for tth in range(8):
                        tt, half = tth // 2, tth % 2
                        pb = pD if (tth % 2 == 0) else pE
                        def te_v(t, tt=tt, half=half, pb=pb):
                            chase(t, pb)
                            for kt in range(2):
                                mm = t.matmul(pb[:, 0:512], kvl[:, kt * 512 + tt * 128:kt * 512 + (tt + 1) * 128],
                                              wvu[:, kt * 1024 + half * 512:kt * 1024 + (half + 1) * 512],
                                              start=(kt == 0), stop=(kt == 1))
                            mm.then_inc(c_pj.h, 1)
                        c_pj.n += 1
                        blk.tensor(te_v)
                        def v_ve(v, tt=tt, half=half, pb=pb, b=b, n0=c_pj.n):
                            v.wait_ge(c_pj.h, n0)
                            ttg = b * 4 + tt
                            for hh in range(8):
                                h = half * 8 + hh
                                op = v.tensor_copy(vt[:, ttg * 1024 + h * 64:ttg * 1024 + (h + 1) * 64],
                                                   pb[:, hh * 64:(hh + 1) * 64])
                            op.then_inc(c_pe.h, 1)
                        c_pe.n += 1
                        blk.vector(v_ve)
                        set_last(pb, c_pe)

                ktmax = (qh + 1) * 8
                for h in range(NHA):
                    hb = h % 2
                    ht = h // 2
                    for kt in range(ktmax):
                        pS = pA if (kt % 2 == 0) else pB
                        sl0 = max(0, (kt * 128 - qh * 1024) // 512)
                        def te_sm(t, h=h, hb=hb, ht=ht, kt=kt, qh=qh, pS=pS, sl0=sl0):
                            chase(t, pS)
                            for sl in range(sl0, 2):
                                mm = t.matmul(pS[:, sl * 512:(sl + 1) * 512],
                                              kk[hb * 64:(hb + 1) * 64, ht * 2048 + kt * 128:ht * 2048 + (kt + 1) * 128],
                                              qq[hb * 64:(hb + 1) * 64, ht * 1024 + sl * 512:ht * 1024 + (sl + 1) * 512],
                                              start=True, stop=True)
                            mm.then_inc(c_sm.h, 1)
                        c_sm.n += 1
                        blk.tensor(te_sm)
                        def s_ex(s, kt=kt, pS=pS, sl0=sl0, n0=c_sm.n):
                            s.wait_ge(c_sm.h, n0)
                            s.activation(pp[:, (kt % 2) * 1024 + sl0 * 512:(kt % 2) * 1024 + 1024],
                                         pS[:, sl0 * 512:1024], AF.Exp, scale=0.125).then_inc(c_ex.h, 1)
                        c_ex.n += 1
                        blk.scalar(s_ex)
                        set_last(pS, c_ex)
                        if kt >= qh * 8:
                            def v_mk(v, kt=kt, qh=qh, sl0=sl0, n0=c_ex.n):
                                v.wait_ge(c_ex.h, n0)
                                dlt = (kt * 128 - qh * 1024 - sl0 * 512) // 128
                                sle = (kt % 2) * 1024 + sl0 * 512
                                v.tensor_tensor(pp[:, sle:sle + 512], pp[:, sle:sle + 512],
                                                amask[:, dlt * 512:(dlt + 1) * 512],
                                                ALU.mult).then_inc(c_mk.h, 1)
                            c_mk.n += 1
                            blk.vector(v_mk)
                        def te_av(t, h=h, hb=hb, kt=kt, qh=qh, ktmax=ktmax, sl0=sl0,
                                  nex=c_ex.n, nmk=c_mk.n):
                            t.wait_ge(c_ex.h, nex)
                            if kt >= qh * 8:
                                t.wait_ge(c_mk.h, nmk)
                            if kt == 0:
                                chase(t, pC)
                                chase(t, pL)
                            for sl in range(sl0, 2):
                                kmax_sl = min(ktmax, (qh * 1024 + sl * 512 + 512) // 128)
                                last = (kt == kmax_sl - 1)
                                first = (kt == 0)
                                psl = (kt % 2) * 1024 + sl * 512
                                t.matmul(pC[hb * 64:(hb + 1) * 64, sl * 512:(sl + 1) * 512],
                                         vt[:, kt * 1024 + h * 64:kt * 1024 + (h + 1) * 64],
                                         pp[:, psl:psl + 512], start=first, stop=last)
                                mm = t.matmul(pL[0:1, sl * 512:(sl + 1) * 512],
                                              onescol[:, :], pp[:, psl:psl + 512],
                                              start=first, stop=last)
                            mm.then_inc(c_av.h, 1)
                        c_av.n += 1
                        blk.tensor(te_av)
                    def s_l(s, n0=c_av.n):
                        s.wait_ge(c_av.h, n0)
                        s.activation(CRL, pL[0:1, 0:1024], AF.Ln)
                        s.activation(CRL, CRL, AF.Exp, scale=-1.0).then_inc(c_sr2.h, 1)
                    c_sr2.n += 1
                    blk.scalar(s_l)
                    def v_l(v, n0=c_sr2.n, nb=c_bc.n, nb2=c_b2.n, nb3=c_bc3.n):
                        v.wait_ge(c_sr2.h, n0)
                        if nb: v.wait_ge(c_bc.h, nb)
                        if nb2: v.wait_ge(c_b2.h, nb2)
                        if nb3: v.wait_ge(c_bc3.h, nb3)
                        v.tensor_copy(rows_b[0:1, 0:512], CRL[0:1, 0:512])
                        v.tensor_copy(rows_b[0:1, 512:1024], CRL[0:1, 512:1024]).then_inc(c_vr2.h, 1)
                    c_vr2.n += 1
                    blk.vector(v_l)
                    def te_b2(t, n0=c_vr2.n):
                        t.wait_ge(c_vr2.h, n0)
                        t.matmul(pL[:, 0:512], esel[0:32, 0:128], rows_b[0:32, 0:512],
                                 start=True, stop=True)
                        t.matmul(pL[:, 512:1024], esel[0:32, 0:128], rows_b[0:32, 512:1024],
                                 start=True, stop=True).then_inc(c_b2.h, 1)
                    c_b2.n += 1
                    blk.tensor(te_b2)
                    def s_av(s, hb=hb, n0=c_av.n):
                        s.wait_ge(c_av.h, n0)
                        s.copy(st_av[hb * 64:(hb + 1) * 64, 0:1024],
                               pC[hb * 64:(hb + 1) * 64, 0:1024]).then_inc(c_ave.h, 1)
                    c_ave.n += 1
                    blk.scalar(s_av)
                    set_last(pC, c_ave)
                    def v_y(v, h=h, hb=hb, ht=ht, qh=qh, n0=c_b2.n, nav=c_ave.n, nwog=c_wog.n):
                        v.wait_ge(c_b2.h, n0)
                        v.wait_ge(c_ave.h, nav)
                        if nwog > 0 and h == 0:
                            v.wait_ge(c_wog.h, nwog)
                        for sl in range(2):
                            op = v.tensor_tensor(ysb[hb * 64:(hb + 1) * 64, ht * 1024 + sl * 512:ht * 1024 + (sl + 1) * 512],
                                                 st_av[hb * 64:(hb + 1) * 64, sl * 512:(sl + 1) * 512],
                                                 pL[hb * 64:(hb + 1) * 64, sl * 512:(sl + 1) * 512],
                                                 ALU.mult)
                        op.then_inc(c_ye.h, 1)
                    c_ye.n += 1
                    blk.vector(v_y)
                    set_last(pL, c_ye)

                _load_wsh(d_wo)
                nw_c3 = c_wlc.n
                for bb in range(2):
                    b = qh * 2 + bb
                    gb3 = (sq * 2 + qh) * 2 + bb
                    t0 = base + b * 512

                    def sp_l3(sy, gb3=gb3, t0=t0, nrm=c_rm.n, nx2=c_x2.n, np2=c_p2.n):
                        if nrm > 0:
                            c_rm.wait(sy, nrm)
                        if nx2 > 0:
                            c_x2.wait(sy, nx2)
                        if np2 > 0:
                            c_p2.wait(sy, np2)
                        sy.dma_start(xn3b[:, :].rearrange("p (k n) -> p k n", k=8),
                                     s_xn3[:, :, t0:t0 + 512]).then_inc(c_ld3.h, 16)
                        sy.dma_start(pld[:, :].rearrange("p (k n) -> p k n", k=4),
                                     s_part3[:, 0:4, t0:t0 + 512]).then_inc(c_ld3.h, 16)
                    c_ld3.n += 32
                    blk.sync(sp_l3)

                    for ot in range(8):
                        pb = pD if (ot % 2 == 0) else pE
                        def te_wo3(t, ot=ot, bb=bb, pb=pb, nye=c_ye.n, nw=nw_c3):
                            if ot == 0:
                                t.wait_ge(c_ye.h, nye)
                            t.wait_ge(c_wlc.h, nw)
                            chase(t, pb)
                            for k in range(8):
                                mm = t.matmul(pb[:, 0:512], wo[:, k * 1024 + ot * 128:k * 1024 + (ot + 1) * 128],
                                              ysb[:, k * 1024 + bb * 512:k * 1024 + (bb + 1) * 512],
                                              start=(k == 0), stop=(k == 7))
                            mm.then_inc(c_wog.h, 1)
                        c_wog.n += 1
                        blk.tensor(te_wo3)
                        def v_x23(v, ot=ot, pb=pb, n0=c_wog.n, nld=c_ld3.n):
                            v.wait_ge(c_wog.h, n0)
                            v.wait_ge(c_ld3.h, nld)
                            v.tensor_tensor(x2b[:, ot * 512:(ot + 1) * 512], pb[:, 0:512],
                                            xn3b[:, ot * 512:(ot + 1) * 512], ALU.add).then_inc(c_x2.h, 1)
                        c_x2.n += 1
                        blk.vector(v_x23)
                        set_last(pb, c_x2)

                    def s_xq3(s, n0=c_x2.n, ns1=c_s1.n, ns3=c_s3.n):
                        s.wait_ge(c_s1.h, ns1)
                        for k in range(8):
                            s.wait_ge(c_x2.h, n0 - 8 + k + 1)
                            if ns3 + k - 2 >= 0 and (ns3 > 0 or k >= 2):
                                s.wait_ge(c_s3.h, max(0, ns3 + k - 2) + 1 - 1) if False else None
                            if ns3 + k >= 2:
                                s.wait_ge(c_s3.h, ns3 + k - 2 + 1)
                            s.activation(xnq[:, (k % 2) * 512:(k % 2 + 1) * 512],
                                         x2b[:, k * 512:(k + 1) * 512], AF.Square).then_inc(c_xq3.h, 1)
                    c_xq3.n += 8
                    blk.scalar(s_xq3)
                    def te_s3(t, n0=c_xq3.n):
                        chase(t, pL)
                        for k in range(8):
                            t.wait_ge(c_xq3.h, n0 - 8 + k + 1)
                            t.matmul(pL[0:1, 0:512], onescol[:, :],
                                     xnq[:, (k % 2) * 512:(k % 2 + 1) * 512],
                                     start=(k == 0), stop=(k == 7)).then_inc(c_s3.h, 1)
                    c_s3.n += 8
                    blk.tensor(te_s3)
                    def s_r3(s, n0=c_s3.n):
                        s.wait_ge(c_s3.h, n0)
                        s.activation(CR1, pL[0:1, 0:512], AF.Sqrt, scale=1.0 / 1024.0)
                        s.activation(CR2, CR1, AF.Ln, bias=EPS)
                        s.activation(CR1, CR2, AF.Exp, scale=-1.0).then_inc(c_sr3.h, 1)
                    c_sr3.n += 1
                    blk.scalar(s_r3)
                    def v_r3(v, n0=c_sr3.n, nb=c_bc.n, nb2=c_b2.n, nb3=c_bc3.n):
                        v.wait_ge(c_sr3.h, n0)
                        if nb: v.wait_ge(c_bc.h, nb)
                        if nb2: v.wait_ge(c_b2.h, nb2)
                        if nb3: v.wait_ge(c_bc3.h, nb3)
                        v.tensor_copy(CBH, CR1)
                        v.tensor_sub(CR2, CR1, CBH)
                        v.tensor_copy(CBL, CR2).then_inc(c_vr3.h, 1)
                    c_vr3.n += 1
                    blk.vector(v_r3)
                    def te_b3(t, n0=c_vr3.n):
                        t.wait_ge(c_vr3.h, n0)
                        t.matmul(pL[:, 512:1024], esel[0:32, 0:128], rows_b[0:32, 0:512],
                                 start=True, stop=False)
                        t.matmul(pL[:, 512:1024], esel[0:32, 0:128], rows_b[0:32, 512:1024],
                                 start=False, stop=True).then_inc(c_bc3.h, 1)
                    c_bc3.n += 1
                    blk.tensor(te_b3)
                    def v_rm2(v, n0=c_bc3.n):
                        v.wait_ge(c_bc3.h, n0)
                        for k in range(8):
                            v.scalar_tensor_tensor(rm2o[:, k * 512:(k + 1) * 512],
                                                   x2b[:, k * 512:(k + 1) * 512],
                                                   w2r[:, k:k + 1], pL[:, 512:1024],
                                                   ALU.mult, ALU.mult).then_inc(c_r2.h, 1)
                    c_r2.n += 8
                    blk.vector(v_rm2)
                    set_last(pL, c_r2)
                    def v_p2a(v, n0=c_ld3.n):
                        v.wait_ge(c_ld3.h, n0)
                        for k in range(4):
                            v.tensor_tensor(pld[:, k * 512:(k + 1) * 512], pld[:, k * 512:(k + 1) * 512],
                                            x2b[:, k * 512:(k + 1) * 512], ALU.add).then_inc(c_p2.h, 1)
                    c_p2.n += 4
                    blk.vector(v_p2a)
                    def sp_s3a(sy, t0=t0, n1=c_r2.n, n2=c_p2.n):
                        sy.wait_ge(c_r2.h, n1)
                        sy.dma_start(s_rms23[:, :, t0:t0 + 512],
                                     rm2o[:, :].rearrange("p (k n) -> p k n", k=8)).then_inc(c_st.h, 16)
                        sy.wait_ge(c_p2.h, n2)
                        sy.dma_start(s_part23[:, 0:4, t0:t0 + 512],
                                     pld[:, :].rearrange("p (k n) -> p k n", k=4)).then_inc(c_st.h, 16)
                        sy.dma_start(pld[:, :].rearrange("p (k n) -> p k n", k=4),
                                     s_part3[:, 4:8, t0:t0 + 512]).then_inc(c_ld3.h, 16)
                    c_st.n += 32
                    c_ld3.n += 16
                    blk.sync(sp_s3a)
                    def v_p2b(v, n0=c_ld3.n):
                        v.wait_ge(c_ld3.h, n0)
                        for k in range(4):
                            v.tensor_tensor(pld[:, k * 512:(k + 1) * 512], pld[:, k * 512:(k + 1) * 512],
                                            x2b[:, (4 + k) * 512:(5 + k) * 512], ALU.add).then_inc(c_p2.h, 1)
                    c_p2.n += 4
                    blk.vector(v_p2b)
                    def sp_s3b(sy, t0=t0, n2=c_p2.n):
                        sy.wait_ge(c_p2.h, n2)
                        sy.dma_start(s_part23[:, 4:8, t0:t0 + 512],
                                     pld[:, :].rearrange("p (k n) -> p k n", k=4)).then_inc(c_st.h, 16)
                    c_st.n += 16
                    blk.sync(sp_s3b)

        def sp_cfin(sy):
            c_st.wait(sy)
        blk.sync(sp_cfin)
    es3.close()


def _phase_d(nc, g):
    es4 = ExitStack()
    def sb4(name, shape, dt=F32):
        return es4.enter_context(nc.sbuf_tensor(name, shape, dt))
    def ps4(name, shape, dt=F32):
        return es4.enter_context(nc.psum_tensor(name, shape, dt))
    def sem(name):
        return Sem(nc, es4, name)

    esel, onescol, identf, rbias = g["esel"], g["onescol"], g["identf"], g["rbias"]
    s_part23, s_rms23, d_out3 = g["s_part23"], g["s_rms23"], g["d_out3"]
    s_wts = g.get("s_wts")
    d_wgs, d_wus, d_wds = g["d_wgs"], g["d_wus"], g["d_wds"]
    d_wgr, d_wur, d_wdr = g["d_wgr"], g["d_wur"], g["d_wdr"]
    d_wrout = g["d_wrout"]

    HT = T // 2
    acc = sb4("acc_d", [128, 8 * HT])
    rms2h = sb4("rms2_d", [128, 8 * HT], BF16)
    wg = [sb4(f"wg{i}_d", [128, 8 * 1024], BF16) for i in range(2)]
    wu = sb4("wu_d", [128, 8 * 1024], BF16)
    wd = sb4("wd_d", [128, 8 * 1024], BF16)
    wro = sb4("wro_d", [128, 8 * 8], BF16)
    hmid = sb4("hmid_d", [128, 8 * 512], BF16)
    sg_p = sb4("sg_d", [128, 2 * 512], BF16)
    ub_p = sb4("ub_d", [128, 2 * 512], BF16)
    prt_p = sb4("prt_d", [128, 64])
    wtt_p = sb4("wtt_d", [128, 64])
    wts_fm = sb4("wts_d", [32, HT], BF16)
    prb_p = sb4("prb_d", [32, 512])

    pG = [ps4("pG0_d", [128, 512]), ps4("pG1_d", [128, 512])]
    pU = [ps4("pU0_d", [128, 512]), ps4("pU1_d", [128, 512])]
    pDn = [ps4("pD0_d", [128, 512]), ps4("pD1_d", [128, 512])]
    pW = ps4("pW_d", [128, 512])
    pR = ps4("pR_d", [128, 512])

    d_w = sem("d_w")
    d_wc = sem("d_wc")
    d_ld = sem("d_ld")
    d_ldc = sem("d_ldc")
    d_rt = sem("d_rt")
    d_rs = sem("d_rs")
    d_rv = sem("d_rv")
    d_g = sem("d_g")
    d_u = sem("d_u")
    d_wb = sem("d_wb")
    d_sg = sem("d_sg")
    d_ub = sem("d_ub")
    d_hm = sem("d_hm")
    d_dn = sem("d_dn")
    d_ac = sem("d_ac")
    d_st = sem("d_st")

    bank_last = {}
    def chase(t, bank):
        e = bank_last.get(id(bank))
        if e is not None:
            sm, n = e
            sm.wait(t, n)
    def set_last(bank, sm):
        bank_last[id(bank)] = (sm, sm.n)

    with nc.Block() as blk:
        def g_w0(gp):
            gp.dma_start(wro[:, :].rearrange("p (k n) -> p k n", k=8)[:, :, 0:ER],
                         d_wrout.rearrange("(k p) n -> p k n", p=128)).then_inc(d_w.h, 16)
            gp.memset(wts_fm[:, :], 0.0)
            gp.memset(prt_p[:, :], 0.0)
            gp.memset(wtt_p[:, :], 0.0)
            gp.wait_ge(d_w.h, 16)
            gp.nop().then_inc(d_wc.h, 1)
        blk.gpsimd(g_w0)
        d_w.n += 16
        d_wc.n += 1

        for half in range(2):
            h0 = half * HT

            def sp_ld(sy, h0=h0, nac=d_ac.n, nst=d_st.n):
                if nac > 0:
                    sy.wait_ge(d_ac.h, nac)
                if nst > 0:
                    sy.wait_ge(d_st.h, nst)
                sy.dma_start(acc[:, :].rearrange("p (k n) -> p k n", k=8),
                             s_part23[:, :, h0:h0 + HT]).then_inc(d_ld.h, 16)
                sy.dma_start(rms2h[:, :].rearrange("p (k n) -> p k n", k=8),
                             s_rms23[:, :, h0:h0 + HT]).then_inc(d_ld.h, 16)
            d_ld.n += 32
            blk.sync(sp_ld)
            def g_ldp(gp, n0=d_ld.n):
                gp.wait_ge(d_ld.h, n0)
                gp.nop().then_inc(d_ldc.h, 1)
            d_ldc.n += 1
            blk.gpsimd(g_ldp)

            for b in range(4):
                def te_rt(t, b=b, nld=d_ldc.n, nw=d_wc.n):
                    t.wait_ge(d_wc.h, nw)
                    t.wait_ge(d_ldc.h, nld)
                    chase(t, pR)
                    for k in range(8):
                        mm = t.matmul(pR[0:ER, 0:512], wro[:, k * 8:k * 8 + ER],
                                      rms2h[:, k * HT + b * 512:k * HT + (b + 1) * 512],
                                      start=(k == 0), stop=(k == 7))
                    mm.then_inc(d_rt.h, 1)
                c_n = d_rt.n = d_rt.n + 1
                blk.tensor(te_rt)
                def s_pr(s, b=b, n0=d_rt.n):
                    s.wait_ge(d_rt.h, n0)
                    s.activation(prb_p[0:ER, 0:512], pR[0:ER, 0:512], AF.Sigmoid,
                                 bias=rbias[0:ER, :]).then_inc(d_rs.h, 1)
                d_rs.n += 1
                blk.scalar(s_pr)
                set_last(pR, d_rs)
                for i4 in range(4):
                    def te_tp(t, b=b, i4=i4, n0=d_rs.n):
                        t.wait_ge(d_rs.h, n0)
                        chase(t, pW)
                        t.transpose(pW[0:128, 0:ER], prb_p[0:ER, i4 * 128:(i4 + 1) * 128],
                                    identf[0:ER, 0:ER]).then_inc(d_rt.h, 1)
                    d_rt.n += 1
                    blk.tensor(te_tp)
                    def v_tp(v, i4=i4, n0=d_rt.n):
                        v.wait_ge(d_rt.h, n0)
                        v.tensor_copy(prt_p[:, i4 * 16:i4 * 16 + ER], pW[0:128, 0:ER]).then_inc(d_rv.h, 1)
                    d_rv.n += 1
                    blk.vector(v_tp)
                    set_last(pW, d_rv)
                for i4 in range(4):
                    def v_t2(v, i4=i4, n0=d_rv.n):
                        PR = prt_p[:, i4 * 16:i4 * 16 + ER]
                        M1 = wtt_p[:, i4 * 16 + 8:i4 * 16 + 9]
                        M2 = wtt_p[:, i4 * 16 + 9:i4 * 16 + 10]
                        RS = wtt_p[:, i4 * 16 + 10:i4 * 16 + 11]
                        W1 = wtt_p[:, i4 * 16 + 11:i4 * 16 + 12]
                        W2 = wtt_p[:, i4 * 16 + 12:i4 * 16 + 13]
                        EQ1 = prt_p[:, i4 * 16 + 8:i4 * 16 + 8 + ER]
                        PM = wtt_p[:, i4 * 16 + 0:i4 * 16 + ER]
                        k = [n0]
                        def step(op):
                            op.then_inc(d_rv.h, 1)
                            k[0] += 1
                            v.wait_ge(d_rv.h, k[0])
                        step(v.reduce_max(M1, PR, axis=AX.X))
                        step(v.tensor_scalar(EQ1, PR, M1, None, ALU.is_equal))
                        step(v.scalar_tensor_tensor(PM, EQ1, -1e9, PR, ALU.mult, ALU.add))
                        step(v.reduce_max(M2, PM, axis=AX.X))
                        step(v.tensor_add(RS, M1, M2))
                        step(v.reciprocal(RS, RS))
                        step(v.tensor_tensor(W1, M1, RS, ALU.mult))
                        step(v.tensor_tensor(W2, M2, RS, ALU.mult))
                        step(v.tensor_scalar(PM, PM, M2, None, ALU.is_equal))
                        step(v.tensor_scalar(EQ1, EQ1, W1, None, ALU.mult))
                        step(v.scalar_tensor_tensor(prt_p[:, i4 * 16:i4 * 16 + ER], PM, W2, EQ1,
                                                    ALU.mult, ALU.add))
                    d_rv.n += 11
                    blk.vector(v_t2)
                for i4 in range(4):
                    def te_tb(t, b=b, i4=i4, n0=d_rv.n):
                        t.wait_ge(d_rv.h, n0)
                        chase(t, pW)
                        t.transpose(pW[0:ER, 0:128], prt_p[:, i4 * 16:i4 * 16 + ER],
                                    identf[0:128, 0:128]).then_inc(d_rt.h, 1)
                    d_rt.n += 1
                    blk.tensor(te_tb)
                    def v_tb(v, b=b, i4=i4, n0=d_rt.n):
                        v.wait_ge(d_rt.h, n0)
                        v.tensor_copy(wts_fm[0:ER, b * 512 + i4 * 128:b * 512 + (i4 + 1) * 128],
                                      pW[0:ER, 0:128]).then_inc(d_rv.h, 1)
                    d_rv.n += 1
                    blk.vector(v_tb)
                    set_last(pW, d_rv)

            if s_wts is not None:
                def sp_wt(sy, h0=h0, n0=d_rv.n):
                    sy.wait_ge(d_rv.h, n0)
                    sy.dma_start(s_wts[0:32, h0:h0 + HT], wts_fm[:, :]).then_inc(d_st.h, 16)
                d_st.n += 16
                blk.sync(sp_wt)

            def wsrc(p):
                if p == 0:
                    return d_wgs, d_wus, d_wds
                return d_wgr[p - 1], d_wur[p - 1], d_wdr[p - 1]

            def g_wg0(gp, half=half, ndn=d_dn.n):
                if ndn > 0:
                    gp.wait_ge(d_dn.h, ndn)
                gp.dma_start(wg[0][:, :].rearrange("p (k n) -> p k n", k=8),
                             wsrc(0)[0].rearrange("(k p) n -> p k n", p=128)).then_inc(d_w.h, 16)
                gp.wait_ge(d_w.h, d_w.n + 16)
                gp.nop().then_inc(d_wc.h, 1)
            nwg = [0, 0]
            blk.gpsimd(g_wg0)
            d_w.n += 16
            d_wc.n += 1
            nwg[0] = d_wc.n

            for p in range(8):
                wgc = wg[p % 2]
                def g_wu(gp, p=p, nu=d_u.n):
                    if nu > 0:
                        gp.wait_ge(d_u.h, nu)
                    gp.dma_start(wu[:, :].rearrange("p (k n) -> p k n", k=8),
                                 wsrc(p)[1].rearrange("(k p) n -> p k n", p=128)).then_inc(d_w.h, 16)
                    gp.wait_ge(d_w.h, d_w.n + 16)
                    gp.nop().then_inc(d_wc.h, 1)
                blk.gpsimd(g_wu)
                d_w.n += 16
                d_wc.n += 1
                nwu = d_wc.n
                def g_wd(gp, p=p, ndn=d_dn.n):
                    if ndn > 0:
                        gp.wait_ge(d_dn.h, ndn)
                    gp.dma_start(wd[:, :].rearrange("p (k n) -> p k n", k=8),
                                 wsrc(p)[2].rearrange("(k p) n -> p k n", p=128)).then_inc(d_w.h, 16)
                    gp.wait_ge(d_w.h, d_w.n + 16)
                    gp.nop().then_inc(d_wc.h, 1)
                blk.gpsimd(g_wd)
                d_w.n += 16
                d_wc.n += 1
                nwd = d_wc.n
                for b in range(4):
                    for ot in range(8):
                        pb = pG[ot % 2]
                        def te_g(t, p=p, b=b, ot=ot, pb=pb, nw=nwg[p % 2], nld=d_ldc.n):
                            t.wait_ge(d_wc.h, nw)
                            t.wait_ge(d_ldc.h, nld)
                            chase(t, pb)
                            for k in range(8):
                                mm = t.matmul(pb[:, 0:512], wgc[:, k * 1024 + ot * 128:k * 1024 + (ot + 1) * 128],
                                              rms2h[:, k * HT + b * 512:k * HT + (b + 1) * 512],
                                              start=(k == 0), stop=(k == 7))
                            mm.then_inc(d_g.h, 1)
                        d_g.n += 1
                        blk.tensor(te_g)
                        def s_sg(s, ot=ot, pb=pb, n0=d_g.n):
                            s.wait_ge(d_g.h, n0)
                            s.activation(sg_p[:, (ot % 2) * 512:(ot % 2 + 1) * 512], pb[:, 0:512],
                                         AF.Silu).then_inc(d_sg.h, 1)
                        d_sg.n += 1
                        blk.scalar(s_sg)
                        set_last(pb, d_sg)
                        pu = pU[ot % 2]
                        def te_u(t, p=p, b=b, ot=ot, pu=pu, nw=nwu, nld=d_ldc.n):
                            t.wait_ge(d_wc.h, nw)
                            chase(t, pu)
                            for k in range(8):
                                mm = t.matmul(pu[:, 0:512], wu[:, k * 1024 + ot * 128:k * 1024 + (ot + 1) * 128],
                                              rms2h[:, k * HT + b * 512:k * HT + (b + 1) * 512],
                                              start=(k == 0), stop=(k == 7))
                            mm.then_inc(d_u.h, 1)
                        d_u.n += 1
                        blk.tensor(te_u)
                        def s_ub(s, ot=ot, pu=pu, n0=d_u.n):
                            s.wait_ge(d_u.h, n0)
                            s.copy(ub_p[:, (ot % 2) * 512:(ot % 2 + 1) * 512],
                                   pu[:, 0:512]).then_inc(d_ub.h, 1)
                        d_ub.n += 1
                        blk.scalar(s_ub)
                        set_last(pu, d_ub)
                        if p >= 1:
                            def te_wb(t, p=p, b=b, ot=ot, ndum=0):
                                if ot == 0:
                                    chase(t, pW)
                                    t.matmul(pW[:, 0:512], esel[0:32, (p - 1) * 128:p * 128],
                                             wts_fm[0:32, b * 512:(b + 1) * 512],
                                             start=True, stop=True).then_inc(d_wb.h, 1)
                            if ot == 0:
                                d_wb.n += 1
                            blk.tensor(te_wb)
                        def v_hm(v, p=p, ot=ot, n1=d_sg.n, n2=d_ub.n, n3=d_wb.n):
                            v.wait_ge(d_sg.h, n1)
                            v.wait_ge(d_ub.h, n2)
                            op = v.tensor_tensor(hmid[:, ot * 512:(ot + 1) * 512],
                                                 sg_p[:, (ot % 2) * 512:(ot % 2 + 1) * 512],
                                                 ub_p[:, (ot % 2) * 512:(ot % 2 + 1) * 512], ALU.mult)
                            if p >= 1:
                                v.wait_ge(d_wb.h, n3)
                                op = v.tensor_tensor(hmid[:, ot * 512:(ot + 1) * 512],
                                                     hmid[:, ot * 512:(ot + 1) * 512],
                                                     pW[:, 0:512], ALU.mult)
                            op.then_inc(d_hm.h, 1)
                        d_hm.n += 1
                        blk.vector(v_hm)
                    if p >= 1:
                        set_last(pW, d_hm)
                    for ot in range(8):
                        pdn = pDn[ot % 2]
                        def te_dn(t, b=b, ot=ot, pdn=pdn, n0=d_hm.n, nw=nwd):
                            t.wait_ge(d_hm.h, n0)
                            t.wait_ge(d_wc.h, nw)
                            chase(t, pdn)
                            for k in range(8):
                                mm = t.matmul(pdn[:, 0:512], wd[:, k * 1024 + ot * 128:k * 1024 + (ot + 1) * 128],
                                              hmid[:, k * 512:(k + 1) * 512],
                                              start=(k == 0), stop=(k == 7))
                            mm.then_inc(d_dn.h, 1)
                        d_dn.n += 1
                        blk.tensor(te_dn)
                        def v_ac(v, b=b, ot=ot, pdn=pdn, n0=d_dn.n):
                            v.wait_ge(d_dn.h, n0)
                            v.tensor_tensor(acc[:, ot * HT + b * 512:ot * HT + (b + 1) * 512],
                                            acc[:, ot * HT + b * 512:ot * HT + (b + 1) * 512],
                                            pdn[:, 0:512], ALU.add).then_inc(d_ac.h, 1)
                        d_ac.n += 1
                        blk.vector(v_ac)
                        set_last(pdn, d_ac)
                if p < 7:
                    def g_wg(gp, p=p, ng=d_g.n):
                        gp.wait_ge(d_g.h, ng)
                        gp.dma_start(wg[(p + 1) % 2][:, :].rearrange("p (k n) -> p k n", k=8),
                                     wsrc(p + 1)[0].rearrange("(k p) n -> p k n", p=128)).then_inc(d_w.h, 16)
                        gp.wait_ge(d_w.h, d_w.n + 16)
                        gp.nop().then_inc(d_wc.h, 1)
                    blk.gpsimd(g_wg)
                    d_w.n += 16
                    d_wc.n += 1
                    nwg[(p + 1) % 2] = d_wc.n

            def sp_st(sy, h0=h0, n0=d_ac.n):
                sy.wait_ge(d_ac.h, n0)
                sy.dma_start(d_out3[:, :, h0:h0 + HT],
                             acc[:, :].rearrange("p (k n) -> p k n", k=8)).then_inc(d_st.h, 16)
            d_st.n += 16
            blk.sync(sp_st)

        def sp_fin(sy):
            d_st.wait(sy)
        blk.sync(sp_fin)
    es4.close()


_BF = ml_dtypes.bfloat16

def _host_constants():
    c = {}
    c["ident_bf"] = np.eye(128, dtype=np.float32).astype(_BF)
    c["ident_f"] = np.eye(128, dtype=np.float32)
    c["onescol"] = np.ones((128, 1), np.float32).astype(_BF)
    e = np.zeros((32, 32 * 128), np.float32)
    for h in range(32):
        e[h, h * 128:(h + 1) * 128] = 1
    c["esel32"] = np.tile(e, (4, 1)).astype(_BF)
    mn = np.zeros((128, 2 * CH), np.float32)
    for j in range(2):
        ss = np.arange(128)[:, None] + j * 128
        ll = np.arange(CH)[None, :]
        mn[:, j * CH:(j + 1) * CH] = np.where(ll >= ss, 0.0, NEG)
    c["maskneg"] = mn.astype(_BF)
    am = np.zeros((128, 4 * 512), np.float32)
    for dlt in range(4):
        kp = np.arange(128)[:, None]
        qf = np.arange(512)[None, :]
        am[:, dlt * 512:(dlt + 1) * 512] = (qf >= kp + dlt * 128).astype(np.float32)
    c["attn_mask"] = am.astype(_BF)
    invf = (1.0 / (10000.0 ** (np.arange(0, HH, 2, dtype=np.float64) / HH)))
    t = np.arange(S, dtype=np.float64)
    ang = invf[:, None] * t[None, :]
    cos = np.cos(ang).astype(np.float32)
    sin = np.sin(ang).astype(np.float32)
    ctab = np.zeros((128, S), np.float32)
    stabm = np.zeros((128, S), np.float32)
    for q in range(4):
        base = q * 32
        if q % 2 == 0:
            ctab[base:base + 32, :] = 1.0
        else:
            ctab[base:base + 16, :] = cos
            ctab[base + 16:base + 32, :] = cos
            stabm[base:base + 16, :] = -sin
            stabm[base + 16:base + 32, :] = sin
    c["rope_cos"] = ctab.astype(_BF)
    c["rope_sin"] = stabm.astype(_BF)
    return c


def _host_weights(inp):
    f32 = lambda a: np.ascontiguousarray(np.asarray(a, dtype=np.float32))
    w = {}
    w["w_inT"] = f32(np.asarray(inp["L0_attn_W_in"]).T)
    w["convw_r"] = f32(np.asarray(inp["L0_attn_conv_w"]).reshape(NTC, 128, 4)
                       .transpose(1, 0, 2).reshape(128, NTC * 4))
    w["convb_r"] = f32(np.asarray(inp["L0_attn_conv_b"]).reshape(NTC, 128).T)
    w["dt_bias_c"] = f32(np.asarray(inp["L0_attn_dt_bias"]).reshape(32, 1))
    w["a_log_c"] = f32(np.asarray(inp["L0_attn_A_log"]).reshape(32, 1))
    w["dD_r"] = f32(np.repeat(np.asarray(inp["L0_attn_D"]), 64).reshape(16, 128).T)
    w["normw_r"] = f32(np.asarray(inp["L0_attn_norm_w"]).reshape(16, 128).T)
    w["w_outT"] = f32(np.asarray(inp["L0_attn_W_out"]).T)
    w["w1_r"] = f32(np.asarray(inp["L0_ffn_w_ln1"]).reshape(8, 128).T)
    w["w2_r"] = f32(np.asarray(inp["L0_ffn_w_ln2"]).reshape(8, 128).T)
    w["wkvd"] = f32(inp["L0_ffn_W_kvd"])
    w["wqd"] = f32(inp["L0_ffn_W_qd"])
    qint = np.zeros((256, D), np.float32)
    kexp = np.zeros((256, D), np.float32)
    rkexp = np.zeros((D, D), np.float32)
    wqu = np.asarray(inp["L0_ffn_W_qu"])
    wrq = np.asarray(inp["L0_ffn_W_ropeq"])
    wku = np.asarray(inp["L0_ffn_W_ku"])
    wrk = np.asarray(inp["L0_ffn_W_ropek"])
    for h in range(NHA):
        qint[:, 64 * h:64 * h + 32] = wqu[:, 32 * h:32 * (h + 1)]
        qint[:, 64 * h + 32:64 * h + 64] = wrq[:, 32 * h:32 * (h + 1)]
        kexp[:, 64 * h:64 * h + 32] = wku[:, 32 * h:32 * (h + 1)]
        rkexp[:, 64 * h + 32:64 * h + 64] = wrk[:, 32 * h:32 * (h + 1)]
    w["w_qint"] = f32(qint)
    w["wku_exp"] = f32(kexp)
    w["wropek_exp"] = f32(rkexp)
    w["wvu"] = f32(inp["L0_ffn_W_vu"])
    w["wo"] = f32(inp["L0_ffn_W_o"])
    w["wg_s"] = f32(inp["L0_ffn_Wg_s"])
    w["wu_s"] = f32(inp["L0_ffn_Wu_s"])
    w["wd_s"] = f32(inp["L0_ffn_Wd_s"])
    w["wg_r"] = f32(inp["L0_ffn_Wg_r"])
    w["wu_r"] = f32(inp["L0_ffn_Wu_r"])
    w["wd_r"] = f32(inp["L0_ffn_Wd_r"])
    w["wrouter"] = f32(inp["L0_ffn_W_router"])
    w["rbias_c"] = f32(np.asarray(inp["L0_ffn_routing_bias"]).reshape(ER, 1))
    return w


_CACHE = {}

def kernel(**inputs):
    if "nc" not in _CACHE:
        _CACHE["nc"] = build_program(dbg=False)
    nc = _CACHE["nc"]
    consts = _host_constants()
    ws = _host_weights(inputs)
    x = np.asarray(inputs["L0_attn_inputs"])
    in_maps = []
    for r in range(NCORE):
        m = dict(consts)
        m.update(ws)
        xs = x[BC * r:BC * (r + 1)].reshape(T, D)
        m["x_fm"] = np.ascontiguousarray(xs.T)
        in_maps.append(m)
    res = run_bass_kernel_spmd(nc, in_maps, core_ids=list(range(NCORE)))
    outs = []
    for r in range(NCORE):
        o = np.asarray(res.results[r]["out_fm"])
        outs.append(np.ascontiguousarray(o.T).reshape(BC, S, D))
    return np.concatenate(outs, axis=0)


# version 02 (final)
# speedup vs baseline: 1.1747x; 1.1747x over previous
import numpy as np
import ml_dtypes
from contextlib import ExitStack

import concourse.bass as bass
import concourse.mybir as mybir
from concourse.bass_utils import run_bass_kernel_spmd

F32 = mybir.dt.float32
BF16 = mybir.dt.bfloat16
AF = mybir.ActivationFunctionType
ALU = mybir.AluOpType
AX = mybir.AxisListType

B, S, D = 16, 2048, 1024
NCORE = 8
BC = B // NCORE
T = BC * S
CH = 256
NCH = S // CH
NCHT = BC * NCH
NHM, PD, NST = 32, 64, 128
DIN, CONVD, DIP = 2048, 2304, 4384
NTC = 18
NHA, HH = 16, 32
ER = 7
EPS = 1e-5
NEG = -1.0e9


class Sem:
    def __init__(self, nc, es, name):
        self.h = es.enter_context(nc.semaphore(name))
        self.n = 0
    def inc(self, inst, k=1):
        inst.then_inc(self.h, k)
        self.n += k
    def wait(self, eng, val=None):
        v = self.n if val is None else val
        if v > 0:
            eng.wait_ge(self.h, v)


class BodyBatch:
    def __init__(self, blk):
        self._blk = blk
        self._q = {k: [] for k in ("gpsimd", "sync", "tensor", "scalar", "vector")}
    def gpsimd(self, f): self._blk.gpsimd(f)
    def sync(self, f): self._blk.sync(f)
    def tensor(self, f): self._blk.tensor(f)
    def scalar(self, f): self._q["scalar"].append(f)
    def vector(self, f): self._q["vector"].append(f)
    def flush(self):
        for k in ("scalar", "vector"):
            fs = self._q[k]
            if fs:
                def run(eng, fs=fs):
                    for f in fs:
                        f(eng)
                getattr(self._blk, k)(run)
        self._q = {k: [] for k in self._q}


def build_program(dbg=False, upto=4):
    nc = bass.Bass("TRN2", target_bir_lowering=False, detect_race_conditions=False)
    es = ExitStack()

    def sb(name, shape, dt=F32):
        return es.enter_context(nc.sbuf_tensor(name, shape, dt))
    def sem(name):
        return Sem(nc, es, name)

    dk = "ExternalOutput" if dbg else "Internal"
    di = lambda n, shp, dt=F32: nc.dram_tensor(n, shp, dt, kind="ExternalInput").ap()

    d_x = di("x_fm", [D, T])
    d_winT = di("w_inT", [D, DIP])
    d_convw = di("convw_r", [128, NTC * 4])
    d_convb = di("convb_r", [128, NTC])
    d_dtbias = di("dt_bias_c", [32, 1])
    d_alog = di("a_log_c", [32, 1])
    d_dD = di("dD_r", [128, 16])
    d_normw = di("normw_r", [128, 16])
    d_woutT = di("w_outT", [DIN, D])
    d_w1 = di("w1_r", [128, 8])
    d_w2 = di("w2_r", [128, 8])
    d_wkvd = di("wkvd", [D, 256])
    d_wqd = di("wqd", [D, 256])
    d_wqint = di("w_qint", [256, D])
    d_wkuexp = di("wku_exp", [256, D])
    d_wrkexp = di("wropek_exp", [D, D])
    d_wvu = di("wvu", [256, D])
    d_wo = di("wo", [D, D])
    d_wgs = di("wg_s", [D, D])
    d_wus = di("wu_s", [D, D])
    d_wds = di("wd_s", [D, D])
    d_wgr = di("wg_r", [ER, D, D])
    d_wur = di("wu_r", [ER, D, D])
    d_wdr = di("wd_r", [ER, D, D])
    d_wrout = di("wrouter", [D, ER])
    d_rbias = di("rbias_c", [ER, 1])
    d_identb = di("ident_bf", [128, 128], BF16)
    d_identf = di("ident_f", [128, 128])
    d_onescol = di("onescol", [128, 1], BF16)
    d_esel = di("esel32", [128, 32 * 128], BF16)
    d_maskneg = di("maskneg", [128, 2 * CH], BF16)
    d_amask = di("attn_mask", [128, 4 * 512], BF16)
    d_ctab = di("rope_cos", [128, S], BF16)
    d_stab = di("rope_sin", [128, S], BF16)

    s_xn = nc.dram_tensor("s_xn", [D, T], BF16, kind=dk).ap()
    s_sz = nc.dram_tensor("s_sz", [DIN, T], BF16, kind=dk).ap()
    s_cvo = nc.dram_tensor("s_cvo", [CONVD, T], BF16, kind=dk).ap()
    s_dtr = nc.dram_tensor("s_dtr", [32, T], F32, kind=dk).ap()
    s_xbc = nc.dram_tensor("s_xbc", [CONVD, T], BF16, kind=dk).ap() if dbg else None
    s_part = nc.dram_tensor("s_part", [D, T], F32, kind=dk).ap()
    s_part2 = nc.dram_tensor("s_part2", [D, T], F32, kind=dk).ap()
    s_rms2 = nc.dram_tensor("s_rms2", [D, T], BF16, kind=dk).ap()
    s_wts = nc.dram_tensor("s_wts", [32, T], BF16, kind="ExternalOutput").ap() if dbg else None
    d_out = nc.dram_tensor("out_fm", [D, T], F32, kind="ExternalOutput").ap()

    d_x3 = d_x.rearrange("(k p) n -> p k n", p=128)
    s_xn3 = s_xn.rearrange("(k p) n -> p k n", p=128)
    s_sz3 = s_sz.rearrange("(k p) n -> p k n", p=128)
    s_cvo3 = s_cvo.rearrange("(k p) n -> p k n", p=128)
    s_part3 = s_part.rearrange("(k p) n -> p k n", p=128)
    s_part23 = s_part2.rearrange("(k p) n -> p k n", p=128)
    s_rms23 = s_rms2.rearrange("(k p) n -> p k n", p=128)
    d_out3 = d_out.rearrange("(k p) n -> p k n", p=128)

    identb = sb("identb", [128, 128], BF16)
    identf = sb("identf", [128, 128])
    onescol = sb("onescol_s", [128, 1], BF16)
    esel = sb("esel_s", [128, 32 * 128], BF16)
    maskneg = sb("maskneg_s", [128, 2 * CH], BF16)
    convw = sb("convw_s", [128, NTC * 4])
    convb = sb("convb_s", [128, NTC])
    dtbias = sb("dtbias_s", [32, 1])
    aneg = sb("aneg_s", [32, 1])
    dDr = sb("dD_s", [128, 16])
    normw = sb("normw_s", [128, 16])
    w1r = sb("w1_s", [128, 8])
    w2r = sb("w2_s", [128, 8])
    rbias = sb("rbias_s", [32, 1])
    zer32 = sb("zer32", [32, CH])
    cstp = sb("cstp", [128, 2])
    rows_f = sb("rows_f", [128, 2048])
    rows_b = sb("rows_b", [128, 1024], BF16)

    s_boot = sem("s_boot")
    s_bootc = sem("s_bootc")
    nc.const_aps.register(F32, EPS, cstp[:, 0:1])
    nc.const_aps.register(F32, 1.0, cstp[:, 1:2])

    with nc.Block() as blk:
        def g_boot(g):
            for dst, src in [(identb[:, :], d_identb), (identf[:, :], d_identf),
                             (onescol[:, :], d_onescol),
                             (esel[:, :], d_esel),
                             (maskneg[:, :], d_maskneg),
                             (convw[:, :], d_convw), (convb[:, :], d_convb),
                             (dtbias[:, :], d_dtbias),
                             (aneg[:, :], d_alog), (dDr[:, :], d_dD), (normw[:, :], d_normw),
                             (w1r[:, :], d_w1), (w2r[:, :], d_w2), (rbias[0:ER, :], d_rbias)]:
                g.dma_start(dst, src).then_inc(s_boot.h, 16)
            g.memset(zer32[:, :], 0.0)
            g.memset(rows_f[:, :], 0.0)
            g.memset(rows_b[:, :], 0.0)
            g.memset(cstp[:, 0:1], EPS)
            g.memset(cstp[:, 1:2], 1.0)
        s_boot.n += 14 * 16
        blk.gpsimd(g_boot)
        def s_bootf(s):
            s.wait_ge(s_boot.h, s_boot.n)
            s.activation(aneg[:, :], aneg[:, :], AF.Exp).then_inc(s_bootc.h, 1)
        s_bootc.n += 1
        blk.scalar(s_bootf)
    with nc.Block() as blk:
        def v_bootf(v):
            v.tensor_scalar(aneg[:, :], aneg[:, :], -1.0, None, ALU.mult)
        blk.vector(v_bootf)

    if upto >= 1:
        _phase_ab1(nc, locals())
    if upto >= 2:
        _phase_ab2(nc, locals())
    if upto >= 3:
        _phase_c(nc, locals())
    _phase_d(nc, locals())

    es.close()
    return nc


def _phase_ab1(nc, g):
    stage = 99
    neuter = set()
    es1 = ExitStack()
    def sb1(name, shape, dt=F32):
        return es1.enter_context(nc.sbuf_tensor(name, shape, dt))
    def ps1(name, shape, dt=F32):
        return es1.enter_context(nc.psum_tensor(name, shape, dt))
    def sem(name):
        return Sem(nc, es1, name)

    esel, maskneg, convw, convb, dtbias = g["esel"], g["maskneg"], g["convw"], g["convb"], g["dtbias"]
    onescol, rows_f, rows_b, s_boot = g["onescol"], g["rows_f"], g["rows_b"], g["s_boot"]
    s_bootc = g["s_bootc"]
    d_x3, s_xn3, s_sz3, s_cvo3, s_dtr = g["d_x3"], g["s_xn3"], g["s_sz3"], g["s_cvo3"], g["s_dtr"]
    d_winT = g["d_winT"]
    s_xbc = g.get("s_xbc")
    o_xb = sem("o_xb")

    w_inT = sb1("w_inT_s", [128, 8 * DIP], BF16)
    x_sb = [sb1(f"x{i}", [128, 8 * CH]) for i in range(2)]
    xn_sb = [sb1(f"xn{i}", [128, 8 * CH], BF16) for i in range(2)]
    sz_sb = sb1("sz", [128, 16 * CH], BF16)
    xbc_sb = sb1("xbc", [128, NTC * 260], BF16)
    cvo_sb = sb1("cvo", [128, NTC * CH], BF16)
    dtr_sb = sb1("dtr", [32, CH])
    lnb_p = sb1("lnb", [128, 4 * CH], BF16)
    lnq_p = sb1("lnq", [128, 4 * CH], BF16)
    xnb_p = sb1("xnb", [128, 2 * CH])
    cacc_p = sb1("cacc", [128, 2 * CH])

    pb_ln = ps1("pb_ln1", [128, 512])
    pb_tr = ps1("pb_tr1", [128, 512])
    pb_f0 = ps1("pb_f01", [128, 512])
    pb_f1 = ps1("pb_f11", [128, 512])

    a_w = sem("a_w")
    a_wc = sem("a_wc")
    a_x = sem("a_x")
    a_xbf = sem("a_xbf")
    a_xsq = sem("a_xsq")
    a_lnmm = sem("a_lnmm")
    a_srow = sem("a_srow")
    a_vrow = sem("a_vrow")
    a_bc = sem("a_bc")
    a_xn = sem("a_xn")
    a_ip = sem("a_ip")
    a_ze = sem("a_ze")
    a_xe = sem("a_xe")
    a_cv = sem("a_cv")
    a_cs = sem("a_cs")
    o_xn = sem("o_xn")
    o_sz = sem("o_sz")
    o_cv = sem("o_cv")
    o_dt = sem("o_dt")

    xbc3 = xbc_sb[:, :].rearrange("p (t n) -> p t n", t=NTC)

    with nc.Block() as _blk0:
        blk = BodyBatch(_blk0)
        def g_w(gp):
            gp.dma_start(w_inT[:, :].rearrange("p (k n) -> p k n", k=8),
                         d_winT.rearrange("(k p) n -> p k n", p=128)).then_inc(a_w.h, 16)
            gp.wait_ge(a_w.h, 16)
            gp.nop().then_inc(a_wc.h, 1)
        a_w.n += 16
        a_wc.n += 1
        blk.gpsimd(g_w)

        for c in range(NCHT):
            cc, i = c % NCH, c % 2
            c0 = c * CH

            def sp_ld(sy, c=c, i=i, c0=c0):
                if c >= 2:
                    a_xn.wait(sy, 8 * (c - 1))
                    a_lnmm.wait(sy, 8 * (c - 1))
                sy.dma_start(x_sb[i][:, :].rearrange("p (k n) -> p k n", k=8),
                             d_x3[:, :, c0:c0 + CH]).then_inc(a_x.h, 16)
            if stage >= 1:
                a_x.n += 16
                blk.sync(sp_ld)

            def v_xbf(v, c=c, i=i):
                a_x.wait(v, 16 * (c + 1))
                for k in range(8):
                    q = c * 8 + k
                    if q >= 4:
                        a_lnmm.wait(v, q - 4 + 1)
                    v.tensor_copy(lnb_p[:, (k % 4) * CH:(k % 4 + 1) * CH],
                                  x_sb[i][:, k * CH:(k + 1) * CH]).then_inc(a_xbf.h, 1)
            if stage >= 2:
                a_xbf.n += 8
                blk.vector(v_xbf)

            def s_xsq(s, c=c, i=i):
                a_x.wait(s, 16 * (c + 1))
                for k in range(8):
                    q = c * 8 + k
                    if q >= 4:
                        a_lnmm.wait(s, q - 4 + 1)
                    s.activation(lnq_p[:, (k % 4) * CH:(k % 4 + 1) * CH],
                                 x_sb[i][:, k * CH:(k + 1) * CH], AF.Square).then_inc(a_xsq.h, 1)
            if stage >= 2:
                a_xsq.n += 8
                blk.scalar(s_xsq)

            def te_ln(t, c=c):
                s_bootc.wait(t)
                if c >= 1:
                    a_srow.wait(t, 2 * c)
                    a_xn.wait(t, 8 * c)
                for k in range(8):
                    q = c * 8 + k
                    a_xbf.wait(t, q + 1)
                    a_xsq.wait(t, q + 1)
                    t.matmul(pb_ln[0:1, 0:CH], onescol[:, :], lnb_p[:, (k % 4) * CH:(k % 4 + 1) * CH],
                             start=(k == 0), stop=(k == 7))
                    t.matmul(pb_tr[0:1, 256:512], onescol[:, :], lnq_p[:, (k % 4) * CH:(k % 4 + 1) * CH],
                             start=(k == 0), stop=(k == 7)).then_inc(a_lnmm.h, 1)
            if stage >= 3:
                a_lnmm.n += 8
                blk.tensor(te_ln)

            bp = 32 * i
            RM = rows_f[bp:bp + 1, 0 * CH:1 * CH]
            REX = rows_f[bp:bp + 1, 1 * CH:2 * CH]
            RMSQ = rows_f[bp:bp + 1, 2 * CH:3 * CH]
            RVAR = rows_f[bp:bp + 1, 3 * CH:4 * CH]
            RLN = rows_f[bp:bp + 1, 4 * CH:5 * CH]
            RSTD = rows_f[bp:bp + 1, 5 * CH:6 * CH]
            RT1 = rows_f[bp:bp + 1, 6 * CH:7 * CH]
            RT2 = rows_f[bp:bp + 1, 7 * CH:8 * CH]
            MH = rows_b[bp:bp + 1, 0 * CH:1 * CH]
            ML = rows_b[bp:bp + 1, 1 * CH:2 * CH]
            SH = rows_b[bp:bp + 1, 2 * CH:3 * CH]
            SL = rows_b[bp:bp + 1, 3 * CH:4 * CH]

            def s_row1(s, c=c, RM=RM, REX=REX, RMSQ=RMSQ):
                a_lnmm.wait(s, 8 * (c + 1))
                if c >= 2:
                    a_bc.wait(s, c - 1)
                s.mul(RM, pb_ln[0:1, 0:CH], 1.0 / 1024.0)
                s.mul(REX, pb_tr[0:1, 256:512], 1.0 / 1024.0)
                s.activation(RMSQ, RM, AF.Square).then_inc(a_srow.h, 1)
            if stage >= 4:
                a_srow.n += 1
                blk.scalar(s_row1)
            def v_row1(v, REX=REX, RMSQ=RMSQ, RVAR=RVAR, n0=a_srow.n):
                v.wait_ge(a_srow.h, n0)
                v.tensor_sub(RVAR, REX, RMSQ).then_inc(a_vrow.h, 1)
            if stage >= 4:
                a_vrow.n += 1
                blk.vector(v_row1)
            def s_row2(s, RVAR=RVAR, RLN=RLN, RSTD=RSTD, n0=a_vrow.n):
                s.wait_ge(a_vrow.h, n0)
                s.activation(RLN, RVAR, AF.Ln, bias=EPS)
                s.activation(RSTD, RLN, AF.Exp, scale=-0.5).then_inc(a_srow.h, 1)
            if stage >= 4:
                a_srow.n += 1
                blk.scalar(s_row2)
            def v_row2(v, RM=RM, RSTD=RSTD, MH=MH, ML=ML, SH=SH, SL=SL, RT1=RT1, RT2=RT2,
                       n0=a_srow.n):
                v.wait_ge(a_srow.h, n0)
                v.tensor_copy(MH, RM)
                v.tensor_sub(RT1, RM, MH)
                v.tensor_copy(ML, RT1)
                v.tensor_copy(SH, RSTD)
                v.tensor_sub(RT2, RSTD, SH)
                v.tensor_copy(SL, RT2).then_inc(a_vrow.h, 1)
            if stage >= 4:
                a_vrow.n += 1
                blk.vector(v_row2)

            def te_bc(t, c=c, i=i, n0=a_vrow.n):
                t.wait_ge(a_vrow.h, n0)
                if c >= 1:
                    a_xn.wait(t, 8 * c)
                bp = 32 * i
                rt = rows_b[bp:bp + 32, :]
                esl = esel[bp:bp + 32, 0:128]
                t.matmul(pb_ln[:, 256:512], esl, rt[:, 0 * CH:1 * CH], start=True, stop=False)
                t.matmul(pb_ln[:, 256:512], esl, rt[:, 1 * CH:2 * CH], start=False, stop=True)
                t.matmul(pb_tr[:, 0:256], esl, rt[:, 2 * CH:3 * CH], start=True, stop=False)
                t.matmul(pb_tr[:, 0:256], esl, rt[:, 3 * CH:4 * CH], start=False, stop=True).then_inc(a_bc.h, 1)
            if stage >= 5:
                a_bc.n += 1
                blk.tensor(te_bc)

            def v_xn(v, c=c, i=i, n0=a_bc.n):
                v.wait_ge(a_bc.h, n0)
                if c >= 2:
                    o_xn.wait(v, 16 * (c - 1))
                    a_ip.wait(v, 35 * (c - 1))
                for k in range(8):
                    v.tensor_sub(xnb_p[:, (k % 2) * CH:(k % 2 + 1) * CH],
                                 x_sb[i][:, k * CH:(k + 1) * CH], pb_ln[:, 256:512])
                    v.tensor_tensor(xn_sb[i][:, k * CH:(k + 1) * CH],
                                    xnb_p[:, (k % 2) * CH:(k % 2 + 1) * CH],
                                    pb_tr[:, 0:256], ALU.mult).then_inc(a_xn.h, 1)
            if stage >= 6:
                a_xn.n += 8
                blk.vector(v_xn)

            def sp_sxn(sy, c=c, i=i, c0=c0):
                a_xn.wait(sy, 8 * (c + 1))
                sy.dma_start(s_xn3[:, :, c0:c0 + CH],
                             xn_sb[i][:, :].rearrange("p (k n) -> p k n", k=8)).then_inc(o_xn.h, 16)
            if stage >= 6:
                o_xn.n += 16
                blk.sync(sp_sxn)

            def te_ip(t, c=c, i=i):
                a_wc.wait(t, 1)
                a_xn.wait(t, 8 * (c + 1))
                for m in range(35):
                    if m >= 2:
                        mm = m - 2
                        if mm < 16:
                            a_ze.wait(t, c * 16 + mm + 1)
                        elif mm < 34:
                            a_xe.wait(t, c * 20 + 1 + (mm - 16) + 1)
                        else:
                            a_xe.wait(t, c * 20 + 20)
                    elif c >= 1:
                        a_xe.wait(t, (c - 1) * 20 + 20 if m == 0 else (c - 1) * 20 + 19)
                    pb = pb_f0 if (m % 2 == 0) else pb_f1
                    mw = 128 if m < 34 else 32
                    off = m * 128
                    if "m34" in neuter and m == 34:
                        mw, off = 128, 4224
                    for k in range(8):
                        mm = t.matmul(pb[0:mw, 0:CH], w_inT[:, k * DIP + off:k * DIP + off + mw],
                                      xn_sb[i][:, k * CH:(k + 1) * CH],
                                      start=(k == 0), stop=(k == 7))
                    mm.then_inc(a_ip.h, 1)
            if stage >= 7:
                a_ip.n += 35
                blk.tensor(te_ip)

            def s_zev(s, c=c, i=i):
                o_sz.wait(s, 16 * c)
                for m in range(16):
                    a_ip.wait(s, c * 35 + m + 1)
                    pb = pb_f0 if (m % 2 == 0) else pb_f1
                    if "zsilu" in neuter:
                        s.activation(sz_sb[:, m * CH:(m + 1) * CH], x_sb[i][:, 0:CH], AF.Silu).then_inc(a_ze.h, 1)
                    else:
                        s.activation(sz_sb[:, m * CH:(m + 1) * CH], pb[:, 0:CH], AF.Silu).then_inc(a_ze.h, 1)
            if stage >= 8:
                a_ze.n += 16
                blk.scalar(s_zev)

            def sp_ssz(sy, c=c, c0=c0):
                a_ze.wait(sy, 16 * (c + 1))
                if "szdma" in neuter:
                    sy.sem_inc(o_sz.h, 16)
                else:
                    sy.dma_start(s_sz3[:, :, c0:c0 + CH],
                                 sz_sb[:, :].rearrange("p (k n) -> p k n", k=16)).then_inc(o_sz.h, 16)
            if stage >= 8:
                o_sz.n += 16
                blk.sync(sp_ssz)

            def v_xev(v, c=c, cc=cc):
                if c >= 1:
                    a_cv.wait(v, NTC * c)
                if cc == 0:
                    v.memset(xbc3[:, :, 0:3], 0.0).then_inc(a_xe.h, 1)
                else:
                    v.tensor_copy(xbc3[:, :, 0:3], xbc3[:, :, 256:259]).then_inc(a_xe.h, 1)
                for m in range(16, 34):
                    tt = m - 16
                    a_ip.wait(v, c * 35 + m + 1)
                    pb = pb_f0 if (m % 2 == 0) else pb_f1
                    v.tensor_copy(xbc3[:, tt, 3:259], pb[:, 0:CH]).then_inc(a_xe.h, 1)
                a_ip.wait(v, c * 35 + 35)
                o_dt.wait(v, 16 * c)
                v.tensor_copy(dtr_sb[0:32, :], pb_f0[0:32, 0:CH]).then_inc(a_xe.h, 1)
            if stage >= 9:
                a_xe.n += 20
                blk.vector(v_xev)

            def sp_sdt(sy, c=c, c0=c0):
                a_xe.wait(sy, c * 20 + 20)
                sy.dma_start(s_dtr[0:32, c0:c0 + CH], dtr_sb[0:32, :]).then_inc(o_dt.h, 16)
            if stage >= 9:
                o_dt.n += 16
                blk.sync(sp_sdt)

            def v_conv(v, c=c):
                for tt in range(NTC):
                    a_xe.wait(v, c * 20 + 1 + tt + 1)
                    if c * NTC + tt >= 2:
                        a_cs.wait(v, c * NTC + tt - 2 + 1)
                    ac = cacc_p[:, (tt % 2) * CH:(tt % 2 + 1) * CH]
                    v.tensor_scalar(ac, xbc3[:, tt, 0:256], convw[:, tt * 4:tt * 4 + 1], None, ALU.mult)
                    for k in range(1, 4):
                        mm = v.scalar_tensor_tensor(ac, xbc3[:, tt, k:k + 256],
                                                    convw[:, tt * 4 + k:tt * 4 + k + 1],
                                                    ac, ALU.mult, ALU.add)
                    mm.then_inc(a_cv.h, 1)
            if stage >= 10:
                a_cv.n += NTC
                blk.vector(v_conv)

            def s_cvs(s, c=c):
                o_cv.wait(s, 16 * c)
                for tt in range(NTC):
                    a_cv.wait(s, c * NTC + tt + 1)
                    s.activation(cvo_sb[:, tt * CH:(tt + 1) * CH],
                                 cacc_p[:, (tt % 2) * CH:(tt % 2 + 1) * CH], AF.Silu,
                                 bias=convb[:, tt:tt + 1]).then_inc(a_cs.h, 1)
            if stage >= 11:
                a_cs.n += NTC
                blk.scalar(s_cvs)

            def sp_sxb(sy, c=c, c0=c0):
                a_cv.wait(sy, NTC * (c + 1))
                sy.dma_start(s_xbc.rearrange("(k p) n -> p k n", p=128)[:, :, c0:c0 + CH],
                             xbc_sb[:, :].rearrange("p (t n) -> p t n", t=NTC)[:, :, 3:259]).then_inc(o_xb.h, 16)
            if s_xbc is not None:
                o_xb.n += 16
                blk.sync(sp_sxb)

            def sp_scv(sy, c=c, c0=c0):
                a_cs.wait(sy, NTC * (c + 1))
                sy.dma_start(s_cvo3[:, :, c0:c0 + CH],
                             cvo_sb[:, :].rearrange("p (k n) -> p k n", k=NTC)).then_inc(o_cv.h, 16)
            if stage >= 11:
                o_cv.n += 16
                blk.sync(sp_scv)

        fulls = [(a_x, 16), (a_xbf, 8), (a_xsq, 8), (a_lnmm, 8), (a_srow, 2), (a_vrow, 2),
                 (a_bc, 1), (a_xn, 8), (a_ip, 35), (a_ze, 16), (a_xe, 20), (a_cv, 18),
                 (a_cs, 18), (o_xn, 16), (o_sz, 16), (o_cv, 16), (o_dt, 16)]
        need_fill = [(sm, per * NCHT - sm.n) for sm, per in fulls if per * NCHT - sm.n > 0]
        if need_fill:
            def g_fill(gp):
                for sm, d in need_fill:
                    gp.sem_inc(sm.h, d)
                    sm.n += 0
            blk.gpsimd(g_fill)
            for sm, d in need_fill:
                sm.n += d
        def sp_fin(sy):
            o_xn.wait(sy)
            o_sz.wait(sy)
            o_cv.wait(sy)
            o_dt.wait(sy)
        blk.sync(sp_fin)
        blk.flush()
    es1.close()


def _phase_ab2(nc, g):
    es2 = ExitStack()
    def sb2(name, shape, dt=F32):
        return es2.enter_context(nc.sbuf_tensor(name, shape, dt))
    def ps2(name, shape, dt=F32):
        return es2.enter_context(nc.psum_tensor(name, shape, dt))
    def sem(name):
        return Sem(nc, es2, name)

    esel, maskneg, dtbias, aneg = g["esel"], g["maskneg"], g["dtbias"], g["aneg"]
    onescol, identf, identb = g["onescol"], g["identf"], g["identb"]
    dDr, normw, zer32 = g["dDr"], g["normw"], g["zer32"]
    rows_f, rows_b = g["rows_f"], g["rows_b"]
    d_x3, s_sz3, s_cvo3, s_dtr, s_part3 = g["d_x3"], g["s_sz3"], g["s_cvo3"], g["s_dtr"], g["s_part3"]
    d_woutT = g["d_woutT"]

    w_outT = sb2("w_outT_s", [128, 16 * D], BF16)
    x_sb = [sb2(f"bx{i}", [128, 8 * CH], BF16) for i in range(2)]
    sz_sb = [sb2(f"bsz{i}", [128, 16 * CH], BF16) for i in range(2)]
    cvo_sb = [sb2(f"bcvo{i}", [128, NTC * CH], BF16) for i in range(2)]
    dtf = sb2("bdtf", [32, 12 * CH])
    cshl = sb2("bcshl", [32, 4 * CH], BF16)
    ecsb = sb2("becs", [32, 2 * CH], BF16)
    trn = [sb2(f"btrn{i}", [128, 8 * 32]) for i in range(2)]
    xtm_sb = sb2("bxtm", [128, 2 * DIN], BF16)
    btm_sb = sb2("bbtm", [128, 2 * 128], BF16)
    gt_sb = sb2("bgt", [128, 2 * CH], BF16)
    carry = sb2("bcarry", [128, DIN])
    t_sb = sb2("bt", [128, 16 * CH], BF16)
    part_sb = sb2("bpart", [128, 4 * CH])
    arg_p = sb2("barg", [128, 4 * CH])
    lh_p = sb2("blh", [128, 4 * CH], BF16)
    m_p = sb2("bm", [128, 4 * CH], BF16)
    csc_p = sb2("bcsc", [128, 2 * CH], BF16)
    xds_p = sb2("bxds", [128, 4 * 64], BF16)
    cbf_p = sb2("bcbf", [128, 2 * 64], BF16)
    cup_p = sb2("bcup", [128, 2 * 64])
    tv_p = sb2("btv", [128, 2 * CH])
    tsq_p = sb2("btsq", [128, 2 * CH], BF16)

    pb_t0 = ps2("pb_t0", [128, 512])
    pb_t1 = ps2("pb_t1", [128, 512])
    pb_o = [ps2("pb_bo0", [128, 512]), ps2("pb_bo1", [128, 512])]
    pb_y = [ps2("pb_by0", [128, 512]), ps2("pb_by1", [128, 512])]
    pb_l2 = ps2("pb_l2", [128, 512])

    b_w = sem("b_w")
    b_wc = sem("b_wc")
    b_ld = sem("b_ld")
    b_ldc = sem("b_ldc")
    b_lx = sem("b_lx")
    b_ds = sem("b_ds")
    b_dv = sem("b_dv")
    b_tr = sem("b_tr")
    b_te = sem("b_te")
    b_ho = sem("b_ho")
    b_ha = sem("b_ha")
    b_he = sem("b_he")
    b_hm = sem("b_hm")
    b_hy = sem("b_hy")
    b_hc = sem("b_hc")
    b_ye = sem("b_ye")
    b_ts = sem("b_ts")
    b_t2 = sem("b_t2")
    b_srow = sem("b_srow")
    b_vrow = sem("b_vrow")
    b_bc = sem("b_bc")
    b_yn = sem("b_yn")
    b_wog = sem("b_wog")
    b_ptt = sem("b_ptt")
    b_so = sem("b_so")

    with nc.Block() as _blk0:
        blk = BodyBatch(_blk0)
        def g_w(gp):
            gp.dma_start(w_outT[:, :].rearrange("p (k n) -> p k n", k=16),
                         d_woutT.rearrange("(k p) n -> p k n", p=128)).then_inc(b_w.h, 16)
            gp.wait_ge(b_w.h, 16)
            gp.nop().then_inc(b_wc.h, 1)
        b_w.n += 16
        b_wc.n += 1
        blk.gpsimd(g_w)

        for c in range(NCHT):
            cc, i = c % NCH, c % 2
            c0 = c * CH

            def sp_ld(sy, c=c, i=i, c0=c0):
                if c >= 2:
                    b_hy.wait(sy, 32 * (c - 1))
                    b_ye.wait(sy, 16 * (c - 1))
                    b_ptt.wait(sy, 8 * (c - 1))
                    b_ds.wait(sy, 2 * (c - 1))
                    b_tr.wait(sy, 42 * (c - 1))
                sy.dma_start(cvo_sb[i][:, :].rearrange("p (k n) -> p k n", k=NTC),
                             s_cvo3[:, :, c0:c0 + CH]).then_inc(b_ld.h, 16)
                sy.dma_start(sz_sb[i][:, :].rearrange("p (k n) -> p k n", k=16),
                             s_sz3[:, :, c0:c0 + CH]).then_inc(b_ld.h, 16)
                sy.dma_start(dtf[0:32, (0 + 6 * i) * CH:(1 + 6 * i) * CH],
                             s_dtr[0:32, c0:c0 + CH]).then_inc(b_ld.h, 16)
            b_ld.n += 48
            blk.sync(sp_ld)
            def g_ldx(gp, c=c, i=i, c0=c0):
                if c >= 2:
                    b_ptt.wait(gp, 8 * (c - 1))
                gp.dma_start(x_sb[i][:, :].rearrange("p (k n) -> p k n", k=8),
                             d_x3[:, :, c0:c0 + CH]).then_inc(b_lx.h, 16)
                gp.wait_ge(b_ld.h, 48 * (c + 1))
                gp.nop().then_inc(b_ldc.h, 1)
            b_lx.n += 16
            b_ldc.n += 1
            blk.gpsimd(g_ldx)

            def s_dt1(s, c=c, i=i):
                b_ld.wait(s, 48 * (c + 1))
                if c >= 2:
                    b_tr.wait(s, 42 * (c - 1))
                s.activation(dtf[0:32, (1 + 6 * i) * CH:(2 + 6 * i) * CH],
                             dtf[0:32, (0 + 6 * i) * CH:(1 + 6 * i) * CH], AF.Abs, bias=dtbias[:, :])
                s.activation(dtf[0:32, (1 + 6 * i) * CH:(2 + 6 * i) * CH],
                             dtf[0:32, (1 + 6 * i) * CH:(2 + 6 * i) * CH], AF.Exp, scale=-1.0)
                s.activation(dtf[0:32, (1 + 6 * i) * CH:(2 + 6 * i) * CH],
                             dtf[0:32, (1 + 6 * i) * CH:(2 + 6 * i) * CH], AF.Ln, bias=1.0).then_inc(b_ds.h, 1)
            b_ds.n += 1
            blk.scalar(s_dt1)
            def v_dt1(v, c=c, i=i, n0=b_ds.n):
                v.wait_ge(b_ds.h, n0)
                if c >= 2:
                    b_hy.wait(v, 32 * (c - 1))
                v.tensor_scalar(dtf[0:32, (2 + 6 * i) * CH:(3 + 6 * i) * CH],
                                dtf[0:32, (0 + 6 * i) * CH:(1 + 6 * i) * CH], dtbias[:, :], 0.0,
                                ALU.add, ALU.max)
                v.tensor_add(dtf[0:32, (2 + 6 * i) * CH:(3 + 6 * i) * CH],
                             dtf[0:32, (2 + 6 * i) * CH:(3 + 6 * i) * CH],
                             dtf[0:32, (1 + 6 * i) * CH:(2 + 6 * i) * CH])
                v.tensor_scalar(dtf[0:32, (3 + 6 * i) * CH:(4 + 6 * i) * CH],
                                dtf[0:32, (2 + 6 * i) * CH:(3 + 6 * i) * CH], aneg[:, :],
                                None, ALU.mult)
                v.tensor_tensor_scan(dtf[0:32, (4 + 6 * i) * CH:(5 + 6 * i) * CH],
                                     dtf[0:32, (3 + 6 * i) * CH:(4 + 6 * i) * CH],
                                     zer32[:, :], 0.0, ALU.add, ALU.add)
                v.tensor_copy(cshl[0:32, (2 * i) * CH:(2 * i + 1) * CH],
                              dtf[0:32, (4 + 6 * i) * CH:(5 + 6 * i) * CH])
                v.tensor_sub(dtf[0:32, (3 + 6 * i) * CH:(4 + 6 * i) * CH],
                             dtf[0:32, (4 + 6 * i) * CH:(5 + 6 * i) * CH],
                             cshl[0:32, (2 * i) * CH:(2 * i + 1) * CH])
                v.tensor_copy(cshl[0:32, (2 * i + 1) * CH:(2 * i + 2) * CH],
                              dtf[0:32, (3 + 6 * i) * CH:(4 + 6 * i) * CH]).then_inc(b_dv.h, 1)
            b_dv.n += 1
            blk.vector(v_dt1)
            def s_dt2(s, c=c, i=i, n0=b_dv.n):
                s.wait_ge(b_dv.h, n0)
                s.activation(ecsb[0:32, i * CH:(i + 1) * CH],
                             dtf[0:32, (4 + 6 * i) * CH:(5 + 6 * i) * CH], AF.Exp)
                s.activation(dtf[0:32, (5 + 6 * i) * CH:(6 + 6 * i) * CH],
                             dtf[0:32, (4 + 6 * i) * CH:(5 + 6 * i) * CH], AF.Exp,
                             scale=-1.0,
                             bias=dtf[0:32, (4 + 6 * i) * CH + 255:(4 + 6 * i) * CH + 256]).then_inc(b_ds.h, 1)
            b_ds.n += 1
            blk.scalar(s_dt2)

            def te_tr(t, c=c, i=i):
                b_ds.wait(t, 2 * (c + 1))
                b_ldc.wait(t, c + 1)
                if c >= 1:
                    b_ptt.wait(t, 8 * c)
                for n in range(42):
                    pb = pb_t0 if (n % 2 == 0) else pb_t1
                    if n >= 2:
                        b_te.wait(t, c * 43 + n - 2 + 1)
                    if n < 6:
                        src = {0: 4, 1: 4, 2: 2, 3: 2, 4: 5, 5: 5}[n]
                        jj = n % 2
                        t.transpose(pb[0:128, 0:32],
                                    dtf[0:32, (src + 6 * i) * CH + jj * 128:(src + 6 * i) * CH + (jj + 1) * 128],
                                    identf[0:32, 0:32]).then_inc(b_tr.h, 1)
                    elif n < 8:
                        j = n - 6
                        t.transpose(pb[0:128, 0:64].bitcast(BF16),
                                    cvo_sb[i][:, 16 * CH + j * 128:16 * CH + (j + 1) * 128],
                                    identb[:, :]).then_inc(b_tr.h, 1)
                    elif n < 40:
                        tt, j = (n - 8) // 2, (n - 8) % 2
                        t.transpose(pb[0:128, 0:64].bitcast(BF16),
                                    cvo_sb[i][:, tt * CH + j * 128:tt * CH + (j + 1) * 128],
                                    identb[:, :]).then_inc(b_tr.h, 1)
                    else:
                        j = n - 40
                        t.matmul(pb[:, 0:256],
                                 cvo_sb[i][:, 16 * CH + j * 128:16 * CH + (j + 1) * 128],
                                 cvo_sb[i][:, 17 * CH:18 * CH],
                                 start=True, stop=True).then_inc(b_tr.h, 1)
            b_tr.n += 42
            blk.tensor(te_tr)

            def v_tre(v, c=c, i=i):
                if c >= 2:
                    b_hm.wait(v, 64 * (c - 1))
                for n in range(42):
                    if n == 8 and c >= 1:
                        b_hy.wait(v, 32 * c)
                    if n == 40 and c >= 1:
                        b_hm.wait(v, 64 * c)
                    b_tr.wait(v, c * 42 + n + 1)
                    pb = pb_t0 if (n % 2 == 0) else pb_t1
                    if n < 2:
                        op = v.tensor_scalar(trn[i][:, n * 32:(n + 1) * 32],
                                             pb[0:128, 0:32], -1.0, None, ALU.mult)
                    elif n < 6:
                        op = v.tensor_copy(trn[i][:, n * 32:(n + 1) * 32], pb[0:128, 0:32])
                    elif n < 8:
                        op = v.tensor_copy(btm_sb[:, (n - 6) * 128:(n - 5) * 128],
                                           pb[0:128, 0:64].bitcast(BF16))
                    elif n < 40:
                        tt, j = (n - 8) // 2, (n - 8) % 2
                        op = v.tensor_copy(xtm_sb[:, j * DIN + tt * 128:j * DIN + (tt + 1) * 128],
                                           pb[0:128, 0:64].bitcast(BF16))
                    else:
                        j = n - 40
                        op = v.tensor_copy(gt_sb[:, j * CH:(j + 1) * CH], pb[:, 0:256])
                    op.then_inc(b_te.h, 1)
                v.tensor_mul(trn[i][:, 6 * 32:8 * 32], trn[i][:, 2 * 32:4 * 32],
                             trn[i][:, 4 * 32:6 * 32]).then_inc(b_te.h, 1)
            b_te.n += 43
            blk.vector(v_tre)

            if cc == 0:
                def v_cz(v):
                    v.memset(carry[:, :], 0.0)
                blk.vector(v_cz)

            for h in range(NHM):
                p, hb = h // 2, h % 2
                pb = pb_y[p % 2]
                pbo = pb_o[h % 2]
                hq = c * NHM + h
                gp_ = c * 16 + p

                def te_ho(t, c=c, i=i, h=h, hq=hq, pbo=pbo):
                    b_ds.wait(t, 2 * (c + 1))
                    b_dv.wait(t, c + 1)
                    if hq >= 2:
                        b_ha.wait(t, 6 * (hq - 2) + 3)
                    t.matmul(pbo[:, 0:CH], esel[0:32, h * 128:(h + 1) * 128],
                             cshl[0:32, (2 * i) * CH:(2 * i + 1) * CH], start=True, stop=False)
                    t.matmul(pbo[:, 0:CH], esel[0:32, h * 128:(h + 1) * 128],
                             cshl[0:32, (2 * i + 1) * CH:(2 * i + 2) * CH],
                             start=False, stop=True)
                    t.matmul(pbo[:, 256:256 + CH], esel[0:32, h * 128:(h + 1) * 128],
                             ecsb[0:32, i * CH:(i + 1) * CH], start=True, stop=True).then_inc(b_ho.h, 1)
                b_ho.n += 1
                blk.tensor(te_ho)

                def v_ha(v, c=c, i=i, h=h, hq=hq, pbo=pbo):
                    b_ho.wait(v, hq + 1)
                    b_te.wait(v, c * 43 + 43)
                    if hq >= 2:
                        b_he.wait(v, 2 * (hq - 2) + 2)
                        b_hy.wait(v, hq - 1)
                    sl2 = hq % 2
                    for j in range(2):
                        v.scalar_tensor_tensor(arg_p[:, (2 * sl2 + j) * CH:(2 * sl2 + j + 1) * CH],
                                               pbo[:, 0:CH], trn[i][:, j * 32 + h:j * 32 + h + 1],
                                               maskneg[:, j * CH:(j + 1) * CH],
                                               ALU.add, ALU.add).then_inc(b_ha.h, 1)
                    v.tensor_tensor(csc_p[:, sl2 * CH:(sl2 + 1) * CH], cvo_sb[i][:, 17 * CH:18 * CH],
                                    pbo[:, 256:256 + CH], ALU.mult).then_inc(b_ha.h, 1)
                    for j in range(2):
                        v.tensor_scalar(xds_p[:, (2 * sl2 + j) * 64:(2 * sl2 + j + 1) * 64],
                                        xtm_sb[:, j * DIN + h * 64:j * DIN + (h + 1) * 64],
                                        trn[i][:, 6 * 32 + j * 32 + h:6 * 32 + j * 32 + h + 1],
                                        None, ALU.mult).then_inc(b_ha.h, 1)
                    v.tensor_copy(cbf_p[:, sl2 * 64:(sl2 + 1) * 64],
                                  carry[:, h * 64:(h + 1) * 64]).then_inc(b_ha.h, 1)
                b_ha.n += 6
                blk.vector(v_ha)

                def s_he(s, hq=hq):
                    sl2 = hq % 2
                    for j in range(2):
                        b_ha.wait(s, 6 * hq + j + 1)
                        s.activation(lh_p[:, (2 * sl2 + j) * CH:(2 * sl2 + j + 1) * CH],
                                     arg_p[:, (2 * sl2 + j) * CH:(2 * sl2 + j + 1) * CH],
                                     AF.Exp).then_inc(b_he.h, 1)
                b_he.n += 2
                blk.scalar(s_he)

                def v_hm(v, i=i, h=h, hq=hq):
                    sl2 = hq % 2
                    if hq >= 2:
                        b_hy.wait(v, hq - 1)
                    for j in range(2):
                        b_he.wait(v, 2 * hq + j + 1)
                        v.scalar_tensor_tensor(m_p[:, (2 * sl2 + j) * CH:(2 * sl2 + j + 1) * CH],
                                               lh_p[:, (2 * sl2 + j) * CH:(2 * sl2 + j + 1) * CH],
                                               trn[i][:, 2 * 32 + j * 32 + h:2 * 32 + j * 32 + h + 1],
                                               gt_sb[:, j * CH:(j + 1) * CH],
                                               ALU.mult, ALU.mult).then_inc(b_hm.h, 1)
                b_hm.n += 2
                blk.vector(v_hm)

                def te_hy(t, c=c, i=i, h=h, p=p, hb=hb, hq=hq, gp_=gp_, pb=pb):
                    if hb == 0 and gp_ >= 2:
                        b_ye.wait(t, gp_ - 2 + 1)
                        b_hc.wait(t, 2 * (gp_ - 2) + 2)
                    b_ha.wait(t, 6 * hq + 6)
                    b_hm.wait(t, 2 * hq + 2)
                    sl2 = hq % 2
                    t.matmul(pb[hb * 64:(hb + 1) * 64, 0:CH], cbf_p[:, sl2 * 64:(sl2 + 1) * 64],
                             csc_p[:, sl2 * CH:(sl2 + 1) * CH], start=True, stop=False)
                    for j in range(2):
                        t.matmul(pb[hb * 64:(hb + 1) * 64, 0:CH],
                                 xtm_sb[:, j * DIN + h * 64:j * DIN + (h + 1) * 64],
                                 m_p[:, (2 * sl2 + j) * CH:(2 * sl2 + j + 1) * CH],
                                 start=False, stop=(j == 1))
                    for j in range(2):
                        t.matmul(pb[:, 256 + hb * 64:256 + (hb + 1) * 64],
                                 btm_sb[:, j * 128:(j + 1) * 128],
                                 xds_p[:, (2 * sl2 + j) * 64:(2 * sl2 + j + 1) * 64],
                                 start=(j == 0), stop=(j == 1))
                    t.matmul(pb[:, 384 + hb:384 + hb + 1], esel[0:32, h * 128:(h + 1) * 128],
                             ecsb[0:32, i * CH + 255:i * CH + 256], start=True, stop=True).then_inc(b_hy.h, 1)
                b_hy.n += 1
                blk.tensor(te_hy)

                def v_hc(v, h=h, hb=hb, hq=hq, pb=pb):
                    b_hy.wait(v, hq + 1)
                    sl2 = hq % 2
                    v.tensor_scalar(cup_p[:, sl2 * 64:(sl2 + 1) * 64], carry[:, h * 64:(h + 1) * 64],
                                    pb[:, 384 + hb:384 + hb + 1], None, ALU.mult)
                    v.tensor_tensor(carry[:, h * 64:(h + 1) * 64], cup_p[:, sl2 * 64:(sl2 + 1) * 64],
                                    pb[:, 256 + hb * 64:256 + (hb + 1) * 64], ALU.add).then_inc(b_hc.h, 1)
                b_hc.n += 1
                blk.vector(v_hc)

                if hb == 1:
                    def v_ye(v, c=c, i=i, p=p, hq=hq, gp_=gp_, pb=pb):
                        b_hy.wait(v, hq + 1)
                        if c >= 1:
                            if p == 0:
                                b_wog.wait(v, 8 * c)
                            b_ts.wait(v, 16 * (c - 1) + p + 1)
                        slp = p % 2
                        v.scalar_tensor_tensor(tv_p[:, slp * CH:(slp + 1) * CH],
                                               cvo_sb[i][:, p * CH:(p + 1) * CH], dDr[:, p:p + 1],
                                               pb[:, 0:CH], ALU.mult, ALU.add)
                        v.tensor_tensor(t_sb[:, p * CH:(p + 1) * CH], tv_p[:, slp * CH:(slp + 1) * CH],
                                        sz_sb[i][:, p * CH:(p + 1) * CH], ALU.mult).then_inc(b_ye.h, 1)
                    b_ye.n += 1
                    blk.vector(v_ye)

                    def s_ts(s, p=p, gp_=gp_):
                        b_ye.wait(s, gp_ + 1)
                        if gp_ >= 2:
                            b_t2.wait(s, gp_ - 2 + 1)
                        slp = p % 2
                        s.activation(tsq_p[:, slp * CH:(slp + 1) * CH], t_sb[:, p * CH:(p + 1) * CH],
                                     AF.Square).then_inc(b_ts.h, 1)
                    b_ts.n += 1
                    blk.scalar(s_ts)

                    def te_t2(t, c=c, p=p, gp_=gp_):
                        b_ts.wait(t, gp_ + 1)
                        if p == 0 and c >= 1:
                            b_yn.wait(t, 16 * c)
                            b_srow.wait(t, c)
                        t.matmul(pb_l2[0:1, 0:CH], onescol[:, :], tsq_p[:, (p % 2) * CH:(p % 2 + 1) * CH],
                                 start=(p == 0), stop=(p == 15)).then_inc(b_t2.h, 1)
                    b_t2.n += 1
                    blk.tensor(te_t2)

            RS1 = rows_f[64:65, (3 * i) * CH:(3 * i + 1) * CH]
            RS2 = rows_f[64:65, (3 * i + 1) * CH:(3 * i + 2) * CH]
            RT3 = rows_f[64:65, (3 * i + 2) * CH:(3 * i + 3) * CH]
            YH = rows_b[64:65, (2 * i) * CH:(2 * i + 1) * CH]
            YL = rows_b[64:65, (2 * i + 1) * CH:(2 * i + 2) * CH]
            def s_rs(s, c=c, RS1=RS1, RS2=RS2):
                b_t2.wait(s, 16 * (c + 1))
                if c >= 2:
                    b_bc.wait(s, c - 1)
                s.activation(RS1, pb_l2[0:1, 0:CH], AF.Ln, scale=1.0 / 2048.0, bias=EPS)
                s.activation(RS2, RS1, AF.Exp, scale=-0.5).then_inc(b_srow.h, 1)
            b_srow.n += 1
            blk.scalar(s_rs)
            def v_rs(v, RS2=RS2, YH=YH, YL=YL, RT3=RT3, n0=b_srow.n):
                v.wait_ge(b_srow.h, n0)
                v.tensor_copy(YH, RS2)
                v.tensor_sub(RT3, RS2, YH)
                v.tensor_copy(YL, RT3).then_inc(b_vrow.h, 1)
            b_vrow.n += 1
            blk.vector(v_rs)
            def te_rs(t, c=c, i=i, n0=b_vrow.n):
                t.wait_ge(b_vrow.h, n0)
                if c >= 1:
                    b_yn.wait(t, 16 * c)
                t.matmul(pb_l2[:, 256:512], esel[64:96, 0:128],
                         rows_b[64:96, (2 * i) * CH:(2 * i + 1) * CH], start=True, stop=False)
                t.matmul(pb_l2[:, 256:512], esel[64:96, 0:128],
                         rows_b[64:96, (2 * i + 1) * CH:(2 * i + 2) * CH], start=False, stop=True).then_inc(b_bc.h, 1)
            b_bc.n += 1
            blk.tensor(te_rs)
            def v_yn(v, c=c, n0=b_bc.n):
                v.wait_ge(b_bc.h, n0)
                if c >= 1:
                    b_wog.wait(v, 8 * c)
                for m in range(16):
                    v.scalar_tensor_tensor(t_sb[:, m * CH:(m + 1) * CH], t_sb[:, m * CH:(m + 1) * CH],
                                           normw[:, m:m + 1], pb_l2[:, 256:512],
                                           ALU.mult, ALU.mult).then_inc(b_yn.h, 1)
            b_yn.n += 16
            blk.vector(v_yn)

            def te_wo(t, c=c):
                b_wc.wait(t, 1)
                b_yn.wait(t, 16 * (c + 1))
                b_te.wait(t, 43 * (c + 1))
                for m in range(8):
                    if m >= 2:
                        b_ptt.wait(t, c * 8 + m - 2 + 1)
                    pb = pb_t0 if (m % 2 == 0) else pb_t1
                    for k in range(16):
                        mm = t.matmul(pb[:, 0:CH], w_outT[:, k * D + m * 128:k * D + (m + 1) * 128],
                                      t_sb[:, k * CH:(k + 1) * CH], start=(k == 0), stop=(k == 15))
                    mm.then_inc(b_wog.h, 1)
            b_wog.n += 8
            blk.tensor(te_wo)

            def v_pt(v, c=c, i=i):
                b_lx.wait(v, 16 * (c + 1))
                for m in range(8):
                    b_wog.wait(v, c * 8 + m + 1)
                    if m % 4 == 0:
                        b_so.wait(v, 32 * c + 16 * (m // 4))
                    pb = pb_t0 if (m % 2 == 0) else pb_t1
                    v.tensor_tensor(part_sb[:, (m % 4) * CH:(m % 4 + 1) * CH], pb[:, 0:CH],
                                    x_sb[i][:, m * CH:(m + 1) * CH], ALU.add).then_inc(b_ptt.h, 1)
            b_ptt.n += 8
            blk.vector(v_pt)

            def sp_spt(sy, c=c, c0=c0):
                for half in range(2):
                    b_ptt.wait(sy, c * 8 + 4 * (half + 1))
                    sy.dma_start(s_part3[:, 4 * half:4 * (half + 1), c0:c0 + CH],
                                 part_sb[:, :].rearrange("p (k n) -> p k n", k=4)).then_inc(b_so.h, 16)
            b_so.n += 32
            blk.sync(sp_spt)

        def sp_fin(sy):
            b_so.wait(sy)
        blk.sync(sp_fin)
        blk.flush()
    es2.close()


def _phase_c(nc, g):
    es3 = ExitStack()
    def sb3(name, shape, dt=F32):
        return es3.enter_context(nc.sbuf_tensor(name, shape, dt))
    def ps3(name, shape, dt=F32):
        return es3.enter_context(nc.psum_tensor(name, shape, dt))
    def sem(name):
        return Sem(nc, es3, name)

    esel, onescol = g["esel"], g["onescol"]
    rows_f, rows_b = g["rows_f"], g["rows_b"]
    w1r, w2r = g["w1r"], g["w2r"]
    s_xn3, s_part3, s_part23, s_rms23 = g["s_xn3"], g["s_part3"], g["s_part23"], g["s_rms23"]
    d_amask, d_ctab, d_stab = g["d_amask"], g["d_ctab"], g["d_stab"]
    d_wkvd, d_wqd, d_wqint, d_wkuexp, d_wrkexp, d_wvu, d_wo = (
        g["d_wkvd"], g["d_wqd"], g["d_wqint"], g["d_wkuexp"], g["d_wrkexp"], g["d_wvu"], g["d_wo"])

    amask = sb3("amask_c", [128, 4 * 512], BF16)
    ctab = sb3("ctab_c", [128, S], BF16)
    stab = sb3("stab_c", [128, S], BF16)
    wkvd = sb3("wkvd_c", [128, 8 * 256], BF16)
    wqd = sb3("wqd_c", [128, 8 * 256], BF16)
    wqint = sb3("wqint_c", [128, 2 * 1024], BF16)
    wkuexp = sb3("wkuexp_c", [128, 2 * 1024], BF16)
    wsh = sb3("wsh_c", [128, 8 * 1024], BF16)
    wrkexp = wsh
    wo = wsh
    wvu = sb3("wvu_c", [128, 2 * 1024], BF16)
    qq = sb3("qq_c", [128, 8 * 1024], BF16)
    kk = sb3("kk_c", [128, 8 * 2048], BF16)
    vt = sb3("vt_c", [128, 16 * 1024], BF16)
    ysb = sb3("y_c", [128, 8 * 1024], BF16)
    pp = sb3("pp_c", [128, 2 * 1024], BF16)
    st_av = sb3("stav_c", [128, 1024], BF16)
    xnb = sb3("xnb_c", [128, 8 * 512], BF16)
    xnq = sb3("xnq_c", [128, 2 * 512], BF16)
    rm1 = sb3("rm1_c", [128, 8 * 512], BF16)
    kvl = sb3("kvl_c", [128, 2 * 512], BF16)
    ql = sb3("ql_c", [128, 2 * 512], BF16)
    rop0 = sb3("rop0_c", [128, 512], BF16)
    rop1 = sb3("rop1_c", [128, 512], BF16)
    rop2 = sb3("rop2_c", [128, 512], BF16)
    rop3 = sb3("rop3_c", [128, 512], BF16)
    xn3b = xnb
    x2b = qq[:, 0:8192].bitcast(F32)
    pld = sb3("pld_c", [128, 4 * 512])
    rm2o = rm1

    pA = ps3("pA_c", [128, 1024])
    pB = ps3("pB_c", [128, 1024])
    pC = ps3("pC_c", [128, 1024])
    pL = ps3("pL_c", [128, 1024])
    pD = pA
    pE = pB

    c_w = sem("c_w")
    c_wc = sem("c_wc")
    c_lxn = sem("c_lxn")
    c_xq = sem("c_xq")
    c_s1 = sem("c_s1")
    c_sr = sem("c_sr")
    c_vr = sem("c_vr")
    c_bc = sem("c_bc")
    c_rm = sem("c_rm")
    c_pj = sem("c_pj")
    c_pe = sem("c_pe")
    c_sm = sem("c_sm")
    c_ex = sem("c_ex")
    c_mk = sem("c_mk")
    c_av = sem("c_av")
    c_ave = sem("c_ave")
    c_sr2 = sem("c_sr2")
    c_vr2 = sem("c_vr2")
    c_b2 = sem("c_b2")
    c_ye = sem("c_ye")
    c_wog = sem("c_wog")
    c_x2 = sem("c_x2")
    c_xq3 = sem("c_xq3")
    c_s3 = sem("c_s3")
    c_sr3 = sem("c_sr3")
    c_vr3 = sem("c_vr3")
    c_bc3 = sem("c_bc3")
    c_r2 = sem("c_r2")
    c_p2 = sem("c_p2")
    c_ld3 = sem("c_ld3")
    c_st = sem("c_st")

    bank_last = {}
    def chase(t, bank):
        e = bank_last.get(id(bank))
        if e is not None:
            sm, n = e
            sm.wait(t, n)
    def set_last(bank, sm):
        bank_last[id(bank)] = (sm, sm.n)

    with nc.Block() as _blk0:
        blk = BodyBatch(_blk0)
        def g_w(gp):
            gp.dma_start(amask[:, :], d_amask).then_inc(c_w.h, 16)
            gp.dma_start(ctab[:, :], d_ctab).then_inc(c_w.h, 16)
            gp.dma_start(stab[:, :], d_stab).then_inc(c_w.h, 16)
            gp.dma_start(wkvd[:, :].rearrange("p (k n) -> p k n", k=8),
                         d_wkvd.rearrange("(k p) n -> p k n", p=128)).then_inc(c_w.h, 16)
            gp.dma_start(wqd[:, :].rearrange("p (k n) -> p k n", k=8),
                         d_wqd.rearrange("(k p) n -> p k n", p=128)).then_inc(c_w.h, 16)
            gp.dma_start(wqint[:, :].rearrange("p (k n) -> p k n", k=2),
                         d_wqint.rearrange("(k p) n -> p k n", p=128)).then_inc(c_w.h, 16)
            gp.dma_start(wkuexp[:, :].rearrange("p (k n) -> p k n", k=2),
                         d_wkuexp.rearrange("(k p) n -> p k n", p=128)).then_inc(c_w.h, 16)
            gp.dma_start(wvu[:, :].rearrange("p (k n) -> p k n", k=2),
                         d_wvu.rearrange("(k p) n -> p k n", p=128)).then_inc(c_w.h, 16)
            gp.wait_ge(c_w.h, 128)
            gp.memset(vt[:, :], 0.0).then_inc(c_wc.h, 1)
        c_w.n += 128
        c_wc.n += 1
        blk.gpsimd(g_w)
        c_wl = sem("c_wl")
        c_wlc = sem("c_wlc")
        def _load_wsh(dsrc):
            def g_l(gp, dsrc=dsrc, npj=c_pj.n, nwog=c_wog.n, nw=c_wl.n + 16):
                if npj > 0:
                    gp.wait_ge(c_pj.h, npj)
                if nwog > 0:
                    gp.wait_ge(c_wog.h, nwog)
                gp.dma_start(wsh[:, :].rearrange("p (k n) -> p k n", k=8),
                             dsrc.rearrange("(k p) n -> p k n", p=128)).then_inc(c_wl.h, 16)
                gp.wait_ge(c_wl.h, nw)
                gp.nop().then_inc(c_wlc.h, 1)
            blk.gpsimd(g_l)
            c_wl.n += 16
            c_wlc.n += 1

        CR1 = rows_f[0:1, 0:512]
        CR2 = rows_f[0:1, 512:1024]
        CRL = rows_f[0:1, 1024:2048]
        CBH = rows_b[0:1, 0:512]
        CBL = rows_b[0:1, 512:1024]

        for sq in range(BC):
            base = sq * S
            for qh in range(2):
                _load_wsh(d_wrkexp)
                nw_c1 = c_wlc.n
                for bb in range(2):
                    b = qh * 2 + bb
                    gb = (sq * 2 + qh) * 2 + bb
                    t0 = base + b * 512

                    def sp_ld(sy, gb=gb, t0=t0):
                        if gb >= 1:
                            c_rm.wait(sy, 8 * gb)
                        sy.dma_start(xnb[:, :].rearrange("p (k n) -> p k n", k=8),
                                     s_xn3[:, :, t0:t0 + 512]).then_inc(c_lxn.h, 16)
                    c_lxn.n += 16
                    blk.sync(sp_ld)

                    def s_xq(s, gb=gb, ns3=c_s3.n):
                        c_lxn.wait(s, 16 * (gb + 1))
                        if ns3 > 0:
                            s.wait_ge(c_s3.h, ns3)
                        for k in range(8):
                            q = gb * 8 + k
                            if q >= 2:
                                c_s1.wait(s, q - 2 + 1)
                            s.activation(xnq[:, (k % 2) * 512:(k % 2 + 1) * 512],
                                         xnb[:, k * 512:(k + 1) * 512], AF.Square).then_inc(c_xq.h, 1)
                    c_xq.n += 8
                    blk.scalar(s_xq)

                    def te_s1(t, gb=gb):
                        c_wc.wait(t, 1)
                        chase(t, pL)
                        for k in range(8):
                            t.matmul(pL[0:1, 0:512], onescol[:, :],
                                     xnq[:, (k % 2) * 512:(k % 2 + 1) * 512],
                                     start=(k == 0), stop=(k == 7)).then_inc(c_s1.h, 1)
                    c_s1.n += 8
                    blk.tensor(te_s1)

                    def s_r1(s, n0=c_s1.n):
                        s.wait_ge(c_s1.h, n0)
                        s.activation(CR1, pL[0:1, 0:512], AF.Sqrt, scale=1.0 / 1024.0)
                        s.activation(CR2, CR1, AF.Ln, bias=EPS)
                        s.activation(CR1, CR2, AF.Exp, scale=-1.0).then_inc(c_sr.h, 1)
                    c_sr.n += 1
                    blk.scalar(s_r1)
                    def v_r1(v, n0=c_sr.n, nb=c_bc.n, nb2=c_b2.n, nb3=c_bc3.n):
                        v.wait_ge(c_sr.h, n0)
                        if nb: v.wait_ge(c_bc.h, nb)
                        if nb2: v.wait_ge(c_b2.h, nb2)
                        if nb3: v.wait_ge(c_bc3.h, nb3)
                        v.tensor_copy(CBH, CR1)
                        v.tensor_sub(CR2, CR1, CBH)
                        v.tensor_copy(CBL, CR2).then_inc(c_vr.h, 1)
                    c_vr.n += 1
                    blk.vector(v_r1)
                    def te_bc1(t, n0=c_vr.n):
                        t.wait_ge(c_vr.h, n0)
                        chase(t, pL)
                        t.matmul(pL[:, 512:1024], esel[0:32, 0:128], rows_b[0:32, 0:512],
                                 start=True, stop=False)
                        t.matmul(pL[:, 512:1024], esel[0:32, 0:128], rows_b[0:32, 512:1024],
                                 start=False, stop=True).then_inc(c_bc.h, 1)
                    c_bc.n += 1
                    blk.tensor(te_bc1)

                    def v_rm(v, gb=gb, n0=c_bc.n):
                        v.wait_ge(c_bc.h, n0)
                        if gb >= 1:
                            c_pj.wait(v, 28 * gb)
                        for k in range(8):
                            v.scalar_tensor_tensor(rm1[:, k * 512:(k + 1) * 512],
                                                   xnb[:, k * 512:(k + 1) * 512],
                                                   w1r[:, k:k + 1], pL[:, 512:1024],
                                                   ALU.mult, ALU.mult).then_inc(c_rm.h, 1)
                    c_rm.n += 8
                    blk.vector(v_rm)
                    set_last(pL, c_rm)

                    for ot in range(2):
                        pb = pD if (ot % 2 == 0) else pE
                        def te_kv(t, ot=ot, pb=pb, n0=c_rm.n):
                            t.wait_ge(c_rm.h, n0)
                            chase(t, pb)
                            for k in range(8):
                                mm = t.matmul(pb[:, 0:512], wkvd[:, k * 256 + ot * 128:k * 256 + (ot + 1) * 128],
                                              rm1[:, k * 512:(k + 1) * 512],
                                              start=(k == 0), stop=(k == 7))
                            mm.then_inc(c_pj.h, 1)
                        c_pj.n += 1
                        blk.tensor(te_kv)
                        def s_kve(s, ot=ot, pb=pb, n0=c_pj.n):
                            s.wait_ge(c_pj.h, n0)
                            s.copy(kvl[:, ot * 512:(ot + 1) * 512], pb[:, 0:512]).then_inc(c_pe.h, 1)
                        c_pe.n += 1
                        blk.scalar(s_kve)
                        set_last(pb, c_pe)
                    for ot in range(2):
                        pb = pD if (ot % 2 == 0) else pE
                        def te_ql(t, ot=ot, pb=pb):
                            chase(t, pb)
                            for k in range(8):
                                mm = t.matmul(pb[:, 0:512], wqd[:, k * 256 + ot * 128:k * 256 + (ot + 1) * 128],
                                              rm1[:, k * 512:(k + 1) * 512],
                                              start=(k == 0), stop=(k == 7))
                            mm.then_inc(c_pj.h, 1)
                        c_pj.n += 1
                        blk.tensor(te_ql)
                        def s_qle(s, ot=ot, pb=pb, n0=c_pj.n):
                            s.wait_ge(c_pj.h, n0)
                            s.copy(ql[:, ot * 512:(ot + 1) * 512], pb[:, 0:512]).then_inc(c_pe.h, 1)
                        c_pe.n += 1
                        blk.scalar(s_qle)
                        set_last(pb, c_pe)
                    for ot in range(8):
                        pb = pD if (ot % 2 == 0) else pE
                        def te_qq(t, ot=ot, pb=pb, n0=c_pe.n):
                            s = None
                            t.wait_ge(c_pe.h, n0)
                            chase(t, pb)
                            for kt in range(2):
                                mm = t.matmul(pb[:, 0:512], wqint[:, kt * 1024 + ot * 128:kt * 1024 + (ot + 1) * 128],
                                              ql[:, kt * 512:(kt + 1) * 512],
                                              start=(kt == 0), stop=(kt == 1))
                            mm.then_inc(c_pj.h, 1)
                        c_pj.n += 1
                        blk.tensor(te_qq)
                        def v_qqe(v, ot=ot, pb=pb, bb=bb, b=b, n0=c_pj.n):
                            v.wait_ge(c_pj.h, n0)
                            tc = slice(b * 512, (b + 1) * 512)
                            v.tensor_copy(rop0[:, :], pb[:, 0:512])
                            v.stream_shuffle(rop1[:, :], rop0[:, :],
                                             list(range(16, 32)) + list(range(0, 16)))
                            v.tensor_tensor(rop2[:, :], rop1[:, :], stab[:, tc], ALU.mult)
                            v.tensor_tensor(rop3[:, :], rop0[:, :], ctab[:, tc], ALU.mult)
                            v.tensor_tensor(qq[:, ot * 1024 + bb * 512:ot * 1024 + (bb + 1) * 512],
                                            rop3[:, :], rop2[:, :], ALU.add).then_inc(c_pe.h, 1)
                        c_pe.n += 1
                        blk.vector(v_qqe)
                        set_last(pb, c_pe)
                    for ot in range(8):
                        pb = pD if (ot % 2 == 0) else pE
                        def te_kk(t, ot=ot, pb=pb, nw=nw_c1):
                            t.wait_ge(c_wlc.h, nw)
                            chase(t, pb)
                            for kt in range(2):
                                t.matmul(pb[:, 0:512], wkuexp[:, kt * 1024 + ot * 128:kt * 1024 + (ot + 1) * 128],
                                         kvl[:, kt * 512:(kt + 1) * 512],
                                         start=(kt == 0), stop=False)
                            for k in range(8):
                                mm = t.matmul(pb[:, 0:512], wrkexp[:, k * 1024 + ot * 128:k * 1024 + (ot + 1) * 128],
                                              rm1[:, k * 512:(k + 1) * 512],
                                              start=False, stop=(k == 7))
                            mm.then_inc(c_pj.h, 1)
                        c_pj.n += 1
                        blk.tensor(te_kk)
                        def v_kke(v, ot=ot, pb=pb, b=b, n0=c_pj.n):
                            v.wait_ge(c_pj.h, n0)
                            tc = slice(b * 512, (b + 1) * 512)
                            v.tensor_copy(rop0[:, :], pb[:, 0:512])
                            v.stream_shuffle(rop1[:, :], rop0[:, :],
                                             list(range(16, 32)) + list(range(0, 16)))
                            v.tensor_tensor(rop2[:, :], rop1[:, :], stab[:, tc], ALU.mult)
                            v.tensor_tensor(rop3[:, :], rop0[:, :], ctab[:, tc], ALU.mult)
                            v.tensor_tensor(kk[:, ot * 2048 + b * 512:ot * 2048 + (b + 1) * 512],
                                            rop3[:, :], rop2[:, :], ALU.add).then_inc(c_pe.h, 1)
                        c_pe.n += 1
                        blk.vector(v_kke)
                        set_last(pb, c_pe)
                    for tth in range(8):
                        tt, half = tth // 2, tth % 2
                        pb = pD if (tth % 2 == 0) else pE
                        def te_v(t, tt=tt, half=half, pb=pb):
                            chase(t, pb)
                            for kt in range(2):
                                mm = t.matmul(pb[:, 0:512], kvl[:, kt * 512 + tt * 128:kt * 512 + (tt + 1) * 128],
                                              wvu[:, kt * 1024 + half * 512:kt * 1024 + (half + 1) * 512],
                                              start=(kt == 0), stop=(kt == 1))
                            mm.then_inc(c_pj.h, 1)
                        c_pj.n += 1
                        blk.tensor(te_v)
                        def v_ve(v, tt=tt, half=half, pb=pb, b=b, n0=c_pj.n):
                            v.wait_ge(c_pj.h, n0)
                            ttg = b * 4 + tt
                            for hh in range(8):
                                h = half * 8 + hh
                                op = v.tensor_copy(vt[:, ttg * 1024 + h * 64:ttg * 1024 + (h + 1) * 64],
                                                   pb[:, hh * 64:(hh + 1) * 64])
                            op.then_inc(c_pe.h, 1)
                        c_pe.n += 1
                        blk.vector(v_ve)
                        set_last(pb, c_pe)

                ktmax = (qh + 1) * 8
                for h in range(NHA):
                    hb = h % 2
                    ht = h // 2
                    for kt in range(ktmax):
                        pS = pA if (kt % 2 == 0) else pB
                        sl0 = max(0, (kt * 128 - qh * 1024) // 512)
                        def te_sm(t, h=h, hb=hb, ht=ht, kt=kt, qh=qh, pS=pS, sl0=sl0):
                            chase(t, pS)
                            for sl in range(sl0, 2):
                                mm = t.matmul(pS[:, sl * 512:(sl + 1) * 512],
                                              kk[hb * 64:(hb + 1) * 64, ht * 2048 + kt * 128:ht * 2048 + (kt + 1) * 128],
                                              qq[hb * 64:(hb + 1) * 64, ht * 1024 + sl * 512:ht * 1024 + (sl + 1) * 512],
                                              start=True, stop=True)
                            mm.then_inc(c_sm.h, 1)
                        c_sm.n += 1
                        blk.tensor(te_sm)
                        def s_ex(s, kt=kt, pS=pS, sl0=sl0, n0=c_sm.n):
                            s.wait_ge(c_sm.h, n0)
                            s.activation(pp[:, (kt % 2) * 1024 + sl0 * 512:(kt % 2) * 1024 + 1024],
                                         pS[:, sl0 * 512:1024], AF.Exp, scale=0.125).then_inc(c_ex.h, 1)
                        c_ex.n += 1
                        blk.scalar(s_ex)
                        set_last(pS, c_ex)
                        if kt >= qh * 8:
                            def v_mk(v, kt=kt, qh=qh, sl0=sl0, n0=c_ex.n):
                                v.wait_ge(c_ex.h, n0)
                                dlt = (kt * 128 - qh * 1024 - sl0 * 512) // 128
                                sle = (kt % 2) * 1024 + sl0 * 512
                                v.tensor_tensor(pp[:, sle:sle + 512], pp[:, sle:sle + 512],
                                                amask[:, dlt * 512:(dlt + 1) * 512],
                                                ALU.mult).then_inc(c_mk.h, 1)
                            c_mk.n += 1
                            blk.vector(v_mk)
                        def te_av(t, h=h, hb=hb, kt=kt, qh=qh, ktmax=ktmax, sl0=sl0,
                                  nex=c_ex.n, nmk=c_mk.n):
                            t.wait_ge(c_ex.h, nex)
                            if kt >= qh * 8:
                                t.wait_ge(c_mk.h, nmk)
                            if kt == 0:
                                chase(t, pC)
                                chase(t, pL)
                            for sl in range(sl0, 2):
                                kmax_sl = min(ktmax, (qh * 1024 + sl * 512 + 512) // 128)
                                last = (kt == kmax_sl - 1)
                                first = (kt == 0)
                                psl = (kt % 2) * 1024 + sl * 512
                                t.matmul(pC[hb * 64:(hb + 1) * 64, sl * 512:(sl + 1) * 512],
                                         vt[:, kt * 1024 + h * 64:kt * 1024 + (h + 1) * 64],
                                         pp[:, psl:psl + 512], start=first, stop=last)
                                mm = t.matmul(pL[0:1, sl * 512:(sl + 1) * 512],
                                              onescol[:, :], pp[:, psl:psl + 512],
                                              start=first, stop=last)
                            mm.then_inc(c_av.h, 1)
                        c_av.n += 1
                        blk.tensor(te_av)
                    def s_l(s, n0=c_av.n):
                        s.wait_ge(c_av.h, n0)
                        s.activation(CRL, pL[0:1, 0:1024], AF.Ln)
                        s.activation(CRL, CRL, AF.Exp, scale=-1.0).then_inc(c_sr2.h, 1)
                    c_sr2.n += 1
                    blk.scalar(s_l)
                    def v_l(v, n0=c_sr2.n, nb=c_bc.n, nb2=c_b2.n, nb3=c_bc3.n):
                        v.wait_ge(c_sr2.h, n0)
                        if nb: v.wait_ge(c_bc.h, nb)
                        if nb2: v.wait_ge(c_b2.h, nb2)
                        if nb3: v.wait_ge(c_bc3.h, nb3)
                        v.tensor_copy(rows_b[0:1, 0:512], CRL[0:1, 0:512])
                        v.tensor_copy(rows_b[0:1, 512:1024], CRL[0:1, 512:1024]).then_inc(c_vr2.h, 1)
                    c_vr2.n += 1
                    blk.vector(v_l)
                    def te_b2(t, n0=c_vr2.n):
                        t.wait_ge(c_vr2.h, n0)
                        t.matmul(pL[:, 0:512], esel[0:32, 0:128], rows_b[0:32, 0:512],
                                 start=True, stop=True)
                        t.matmul(pL[:, 512:1024], esel[0:32, 0:128], rows_b[0:32, 512:1024],
                                 start=True, stop=True).then_inc(c_b2.h, 1)
                    c_b2.n += 1
                    blk.tensor(te_b2)
                    def s_av(s, hb=hb, n0=c_av.n):
                        s.wait_ge(c_av.h, n0)
                        s.copy(st_av[hb * 64:(hb + 1) * 64, 0:1024],
                               pC[hb * 64:(hb + 1) * 64, 0:1024]).then_inc(c_ave.h, 1)
                    c_ave.n += 1
                    blk.scalar(s_av)
                    set_last(pC, c_ave)
                    def v_y(v, h=h, hb=hb, ht=ht, qh=qh, n0=c_b2.n, nav=c_ave.n, nwog=c_wog.n):
                        v.wait_ge(c_b2.h, n0)
                        v.wait_ge(c_ave.h, nav)
                        if nwog > 0 and h == 0:
                            v.wait_ge(c_wog.h, nwog)
                        for sl in range(2):
                            op = v.tensor_tensor(ysb[hb * 64:(hb + 1) * 64, ht * 1024 + sl * 512:ht * 1024 + (sl + 1) * 512],
                                                 st_av[hb * 64:(hb + 1) * 64, sl * 512:(sl + 1) * 512],
                                                 pL[hb * 64:(hb + 1) * 64, sl * 512:(sl + 1) * 512],
                                                 ALU.mult)
                        op.then_inc(c_ye.h, 1)
                    c_ye.n += 1
                    blk.vector(v_y)
                    set_last(pL, c_ye)

                _load_wsh(d_wo)
                nw_c3 = c_wlc.n
                for bb in range(2):
                    b = qh * 2 + bb
                    gb3 = (sq * 2 + qh) * 2 + bb
                    t0 = base + b * 512

                    def sp_l3(sy, gb3=gb3, t0=t0, nrm=c_rm.n, nx2=c_x2.n, np2=c_p2.n):
                        if nrm > 0:
                            c_rm.wait(sy, nrm)
                        if nx2 > 0:
                            c_x2.wait(sy, nx2)
                        if np2 > 0:
                            c_p2.wait(sy, np2)
                        sy.dma_start(xn3b[:, :].rearrange("p (k n) -> p k n", k=8),
                                     s_xn3[:, :, t0:t0 + 512]).then_inc(c_ld3.h, 16)
                        sy.dma_start(pld[:, :].rearrange("p (k n) -> p k n", k=4),
                                     s_part3[:, 0:4, t0:t0 + 512]).then_inc(c_ld3.h, 16)
                    c_ld3.n += 32
                    blk.sync(sp_l3)

                    for ot in range(8):
                        pb = pD if (ot % 2 == 0) else pE
                        def te_wo3(t, ot=ot, bb=bb, pb=pb, nye=c_ye.n, nw=nw_c3):
                            if ot == 0:
                                t.wait_ge(c_ye.h, nye)
                            t.wait_ge(c_wlc.h, nw)
                            chase(t, pb)
                            for k in range(8):
                                mm = t.matmul(pb[:, 0:512], wo[:, k * 1024 + ot * 128:k * 1024 + (ot + 1) * 128],
                                              ysb[:, k * 1024 + bb * 512:k * 1024 + (bb + 1) * 512],
                                              start=(k == 0), stop=(k == 7))
                            mm.then_inc(c_wog.h, 1)
                        c_wog.n += 1
                        blk.tensor(te_wo3)
                        def v_x23(v, ot=ot, pb=pb, n0=c_wog.n, nld=c_ld3.n):
                            v.wait_ge(c_wog.h, n0)
                            v.wait_ge(c_ld3.h, nld)
                            v.tensor_tensor(x2b[:, ot * 512:(ot + 1) * 512], pb[:, 0:512],
                                            xn3b[:, ot * 512:(ot + 1) * 512], ALU.add).then_inc(c_x2.h, 1)
                        c_x2.n += 1
                        blk.vector(v_x23)
                        set_last(pb, c_x2)

                    def s_xq3(s, n0=c_x2.n, ns1=c_s1.n, ns3=c_s3.n):
                        s.wait_ge(c_s1.h, ns1)
                        for k in range(8):
                            s.wait_ge(c_x2.h, n0 - 8 + k + 1)
                            if ns3 + k - 2 >= 0 and (ns3 > 0 or k >= 2):
                                s.wait_ge(c_s3.h, max(0, ns3 + k - 2) + 1 - 1) if False else None
                            if ns3 + k >= 2:
                                s.wait_ge(c_s3.h, ns3 + k - 2 + 1)
                            s.activation(xnq[:, (k % 2) * 512:(k % 2 + 1) * 512],
                                         x2b[:, k * 512:(k + 1) * 512], AF.Square).then_inc(c_xq3.h, 1)
                    c_xq3.n += 8
                    blk.scalar(s_xq3)
                    def te_s3(t, n0=c_xq3.n):
                        chase(t, pL)
                        for k in range(8):
                            t.wait_ge(c_xq3.h, n0 - 8 + k + 1)
                            t.matmul(pL[0:1, 0:512], onescol[:, :],
                                     xnq[:, (k % 2) * 512:(k % 2 + 1) * 512],
                                     start=(k == 0), stop=(k == 7)).then_inc(c_s3.h, 1)
                    c_s3.n += 8
                    blk.tensor(te_s3)
                    def s_r3(s, n0=c_s3.n):
                        s.wait_ge(c_s3.h, n0)
                        s.activation(CR1, pL[0:1, 0:512], AF.Sqrt, scale=1.0 / 1024.0)
                        s.activation(CR2, CR1, AF.Ln, bias=EPS)
                        s.activation(CR1, CR2, AF.Exp, scale=-1.0).then_inc(c_sr3.h, 1)
                    c_sr3.n += 1
                    blk.scalar(s_r3)
                    def v_r3(v, n0=c_sr3.n, nb=c_bc.n, nb2=c_b2.n, nb3=c_bc3.n):
                        v.wait_ge(c_sr3.h, n0)
                        if nb: v.wait_ge(c_bc.h, nb)
                        if nb2: v.wait_ge(c_b2.h, nb2)
                        if nb3: v.wait_ge(c_bc3.h, nb3)
                        v.tensor_copy(CBH, CR1)
                        v.tensor_sub(CR2, CR1, CBH)
                        v.tensor_copy(CBL, CR2).then_inc(c_vr3.h, 1)
                    c_vr3.n += 1
                    blk.vector(v_r3)
                    def te_b3(t, n0=c_vr3.n):
                        t.wait_ge(c_vr3.h, n0)
                        t.matmul(pL[:, 512:1024], esel[0:32, 0:128], rows_b[0:32, 0:512],
                                 start=True, stop=False)
                        t.matmul(pL[:, 512:1024], esel[0:32, 0:128], rows_b[0:32, 512:1024],
                                 start=False, stop=True).then_inc(c_bc3.h, 1)
                    c_bc3.n += 1
                    blk.tensor(te_b3)
                    def v_rm2(v, n0=c_bc3.n):
                        v.wait_ge(c_bc3.h, n0)
                        for k in range(8):
                            v.scalar_tensor_tensor(rm2o[:, k * 512:(k + 1) * 512],
                                                   x2b[:, k * 512:(k + 1) * 512],
                                                   w2r[:, k:k + 1], pL[:, 512:1024],
                                                   ALU.mult, ALU.mult).then_inc(c_r2.h, 1)
                    c_r2.n += 8
                    blk.vector(v_rm2)
                    set_last(pL, c_r2)
                    def v_p2a(v, n0=c_ld3.n):
                        v.wait_ge(c_ld3.h, n0)
                        for k in range(4):
                            v.tensor_tensor(pld[:, k * 512:(k + 1) * 512], pld[:, k * 512:(k + 1) * 512],
                                            x2b[:, k * 512:(k + 1) * 512], ALU.add).then_inc(c_p2.h, 1)
                    c_p2.n += 4
                    blk.vector(v_p2a)
                    def sp_s3a(sy, t0=t0, n1=c_r2.n, n2=c_p2.n):
                        sy.wait_ge(c_r2.h, n1)
                        sy.dma_start(s_rms23[:, :, t0:t0 + 512],
                                     rm2o[:, :].rearrange("p (k n) -> p k n", k=8)).then_inc(c_st.h, 16)
                        sy.wait_ge(c_p2.h, n2)
                        sy.dma_start(s_part23[:, 0:4, t0:t0 + 512],
                                     pld[:, :].rearrange("p (k n) -> p k n", k=4)).then_inc(c_st.h, 16)
                        sy.dma_start(pld[:, :].rearrange("p (k n) -> p k n", k=4),
                                     s_part3[:, 4:8, t0:t0 + 512]).then_inc(c_ld3.h, 16)
                    c_st.n += 32
                    c_ld3.n += 16
                    blk.sync(sp_s3a)
                    def v_p2b(v, n0=c_ld3.n):
                        v.wait_ge(c_ld3.h, n0)
                        for k in range(4):
                            v.tensor_tensor(pld[:, k * 512:(k + 1) * 512], pld[:, k * 512:(k + 1) * 512],
                                            x2b[:, (4 + k) * 512:(5 + k) * 512], ALU.add).then_inc(c_p2.h, 1)
                    c_p2.n += 4
                    blk.vector(v_p2b)
                    def sp_s3b(sy, t0=t0, n2=c_p2.n):
                        sy.wait_ge(c_p2.h, n2)
                        sy.dma_start(s_part23[:, 4:8, t0:t0 + 512],
                                     pld[:, :].rearrange("p (k n) -> p k n", k=4)).then_inc(c_st.h, 16)
                    c_st.n += 16
                    blk.sync(sp_s3b)

        def sp_cfin(sy):
            c_st.wait(sy)
        blk.sync(sp_cfin)
        blk.flush()
    es3.close()


def _phase_d(nc, g):
    es4 = ExitStack()
    def sb4(name, shape, dt=F32):
        return es4.enter_context(nc.sbuf_tensor(name, shape, dt))
    def ps4(name, shape, dt=F32):
        return es4.enter_context(nc.psum_tensor(name, shape, dt))
    def sem(name):
        return Sem(nc, es4, name)

    esel, onescol, identf, rbias = g["esel"], g["onescol"], g["identf"], g["rbias"]
    s_part23, s_rms23, d_out3 = g["s_part23"], g["s_rms23"], g["d_out3"]
    s_wts = g.get("s_wts")
    d_wgs, d_wus, d_wds = g["d_wgs"], g["d_wus"], g["d_wds"]
    d_wgr, d_wur, d_wdr = g["d_wgr"], g["d_wur"], g["d_wdr"]
    d_wrout = g["d_wrout"]

    HT = T // 2
    acc = sb4("acc_d", [128, 8 * HT])
    rms2h = sb4("rms2_d", [128, 8 * HT], BF16)
    wg = [sb4(f"wg{i}_d", [128, 8 * 1024], BF16) for i in range(2)]
    wu = sb4("wu_d", [128, 8 * 1024], BF16)
    wd = sb4("wd_d", [128, 8 * 1024], BF16)
    wro = sb4("wro_d", [128, 8 * 8], BF16)
    hmid = sb4("hmid_d", [128, 8 * 512], BF16)
    sg_p = sb4("sg_d", [128, 2 * 512], BF16)
    ub_p = sb4("ub_d", [128, 2 * 512], BF16)
    prt_p = sb4("prt_d", [128, 64])
    wtt_p = sb4("wtt_d", [128, 64])
    wts_fm = sb4("wts_d", [32, HT], BF16)
    prb_p = sb4("prb_d", [32, 512])

    pG = [ps4("pG0_d", [128, 512]), ps4("pG1_d", [128, 512])]
    pU = [ps4("pU0_d", [128, 512]), ps4("pU1_d", [128, 512])]
    pDn = [ps4("pD0_d", [128, 512]), ps4("pD1_d", [128, 512])]
    pW = ps4("pW_d", [128, 512])
    pR = ps4("pR_d", [128, 512])

    d_w = sem("d_w")
    d_wc = sem("d_wc")
    d_ld = sem("d_ld")
    d_ldc = sem("d_ldc")
    d_rt = sem("d_rt")
    d_rs = sem("d_rs")
    d_rv = sem("d_rv")
    d_g = sem("d_g")
    d_u = sem("d_u")
    d_wb = sem("d_wb")
    d_sg = sem("d_sg")
    d_ub = sem("d_ub")
    d_hm = sem("d_hm")
    d_dn = sem("d_dn")
    d_ac = sem("d_ac")
    d_st = sem("d_st")

    bank_last = {}
    def chase(t, bank):
        e = bank_last.get(id(bank))
        if e is not None:
            sm, n = e
            sm.wait(t, n)
    def set_last(bank, sm):
        bank_last[id(bank)] = (sm, sm.n)

    with nc.Block() as _blk0:
        blk = BodyBatch(_blk0)
        def g_w0(gp):
            gp.dma_start(wro[:, :].rearrange("p (k n) -> p k n", k=8)[:, :, 0:ER],
                         d_wrout.rearrange("(k p) n -> p k n", p=128)).then_inc(d_w.h, 16)
            gp.memset(wts_fm[:, :], 0.0)
            gp.memset(prt_p[:, :], 0.0)
            gp.memset(wtt_p[:, :], 0.0)
            gp.wait_ge(d_w.h, 16)
            gp.nop().then_inc(d_wc.h, 1)
        blk.gpsimd(g_w0)
        d_w.n += 16
        d_wc.n += 1

        for half in range(2):
            h0 = half * HT

            def sp_ld(sy, h0=h0, nac=d_ac.n, nst=d_st.n):
                if nac > 0:
                    sy.wait_ge(d_ac.h, nac)
                if nst > 0:
                    sy.wait_ge(d_st.h, nst)
                sy.dma_start(acc[:, :].rearrange("p (k n) -> p k n", k=8),
                             s_part23[:, :, h0:h0 + HT]).then_inc(d_ld.h, 16)
                sy.dma_start(rms2h[:, :].rearrange("p (k n) -> p k n", k=8),
                             s_rms23[:, :, h0:h0 + HT]).then_inc(d_ld.h, 16)
            d_ld.n += 32
            blk.sync(sp_ld)
            def g_ldp(gp, n0=d_ld.n):
                gp.wait_ge(d_ld.h, n0)
                gp.nop().then_inc(d_ldc.h, 1)
            d_ldc.n += 1
            blk.gpsimd(g_ldp)

            for b in range(4):
                def te_rt(t, b=b, nld=d_ldc.n, nw=d_wc.n):
                    t.wait_ge(d_wc.h, nw)
                    t.wait_ge(d_ldc.h, nld)
                    chase(t, pR)
                    for k in range(8):
                        mm = t.matmul(pR[0:ER, 0:512], wro[:, k * 8:k * 8 + ER],
                                      rms2h[:, k * HT + b * 512:k * HT + (b + 1) * 512],
                                      start=(k == 0), stop=(k == 7))
                    mm.then_inc(d_rt.h, 1)
                c_n = d_rt.n = d_rt.n + 1
                blk.tensor(te_rt)
                def s_pr(s, b=b, n0=d_rt.n):
                    s.wait_ge(d_rt.h, n0)
                    s.activation(prb_p[0:ER, 0:512], pR[0:ER, 0:512], AF.Sigmoid,
                                 bias=rbias[0:ER, :]).then_inc(d_rs.h, 1)
                d_rs.n += 1
                blk.scalar(s_pr)
                set_last(pR, d_rs)
                for i4 in range(4):
                    def te_tp(t, b=b, i4=i4, n0=d_rs.n):
                        t.wait_ge(d_rs.h, n0)
                        chase(t, pW)
                        t.transpose(pW[0:128, 0:ER], prb_p[0:ER, i4 * 128:(i4 + 1) * 128],
                                    identf[0:ER, 0:ER]).then_inc(d_rt.h, 1)
                    d_rt.n += 1
                    blk.tensor(te_tp)
                    def v_tp(v, i4=i4, n0=d_rt.n):
                        v.wait_ge(d_rt.h, n0)
                        v.tensor_copy(prt_p[:, i4 * 16:i4 * 16 + ER], pW[0:128, 0:ER]).then_inc(d_rv.h, 1)
                    d_rv.n += 1
                    blk.vector(v_tp)
                    set_last(pW, d_rv)
                for i4 in range(4):
                    def v_t2(v, i4=i4, n0=d_rv.n):
                        PR = prt_p[:, i4 * 16:i4 * 16 + ER]
                        M1 = wtt_p[:, i4 * 16 + 8:i4 * 16 + 9]
                        M2 = wtt_p[:, i4 * 16 + 9:i4 * 16 + 10]
                        RS = wtt_p[:, i4 * 16 + 10:i4 * 16 + 11]
                        W1 = wtt_p[:, i4 * 16 + 11:i4 * 16 + 12]
                        W2 = wtt_p[:, i4 * 16 + 12:i4 * 16 + 13]
                        EQ1 = prt_p[:, i4 * 16 + 8:i4 * 16 + 8 + ER]
                        PM = wtt_p[:, i4 * 16 + 0:i4 * 16 + ER]
                        k = [n0]
                        def step(op):
                            op.then_inc(d_rv.h, 1)
                            k[0] += 1
                            v.wait_ge(d_rv.h, k[0])
                        step(v.reduce_max(M1, PR, axis=AX.X))
                        step(v.tensor_scalar(EQ1, PR, M1, None, ALU.is_equal))
                        step(v.scalar_tensor_tensor(PM, EQ1, -1e9, PR, ALU.mult, ALU.add))
                        step(v.reduce_max(M2, PM, axis=AX.X))
                        step(v.tensor_add(RS, M1, M2))
                        step(v.reciprocal(RS, RS))
                        step(v.tensor_tensor(W1, M1, RS, ALU.mult))
                        step(v.tensor_tensor(W2, M2, RS, ALU.mult))
                        step(v.tensor_scalar(PM, PM, M2, None, ALU.is_equal))
                        step(v.tensor_scalar(EQ1, EQ1, W1, None, ALU.mult))
                        step(v.scalar_tensor_tensor(prt_p[:, i4 * 16:i4 * 16 + ER], PM, W2, EQ1,
                                                    ALU.mult, ALU.add))
                    d_rv.n += 11
                    blk.vector(v_t2)
                for i4 in range(4):
                    def te_tb(t, b=b, i4=i4, n0=d_rv.n):
                        t.wait_ge(d_rv.h, n0)
                        chase(t, pW)
                        t.transpose(pW[0:ER, 0:128], prt_p[:, i4 * 16:i4 * 16 + ER],
                                    identf[0:128, 0:128]).then_inc(d_rt.h, 1)
                    d_rt.n += 1
                    blk.tensor(te_tb)
                    def v_tb(v, b=b, i4=i4, n0=d_rt.n):
                        v.wait_ge(d_rt.h, n0)
                        v.tensor_copy(wts_fm[0:ER, b * 512 + i4 * 128:b * 512 + (i4 + 1) * 128],
                                      pW[0:ER, 0:128]).then_inc(d_rv.h, 1)
                    d_rv.n += 1
                    blk.vector(v_tb)
                    set_last(pW, d_rv)

            if s_wts is not None:
                def sp_wt(sy, h0=h0, n0=d_rv.n):
                    sy.wait_ge(d_rv.h, n0)
                    sy.dma_start(s_wts[0:32, h0:h0 + HT], wts_fm[:, :]).then_inc(d_st.h, 16)
                d_st.n += 16
                blk.sync(sp_wt)

            def wsrc(p):
                if p == 0:
                    return d_wgs, d_wus, d_wds
                return d_wgr[p - 1], d_wur[p - 1], d_wdr[p - 1]

            def g_wg0(gp, half=half, ndn=d_dn.n):
                if ndn > 0:
                    gp.wait_ge(d_dn.h, ndn)
                gp.dma_start(wg[0][:, :].rearrange("p (k n) -> p k n", k=8),
                             wsrc(0)[0].rearrange("(k p) n -> p k n", p=128)).then_inc(d_w.h, 16)
                gp.wait_ge(d_w.h, d_w.n + 16)
                gp.nop().then_inc(d_wc.h, 1)
            nwg = [0, 0]
            blk.gpsimd(g_wg0)
            d_w.n += 16
            d_wc.n += 1
            nwg[0] = d_wc.n

            for p in range(8):
                wgc = wg[p % 2]
                def g_wu(gp, p=p, nu=d_u.n):
                    if nu > 0:
                        gp.wait_ge(d_u.h, nu)
                    gp.dma_start(wu[:, :].rearrange("p (k n) -> p k n", k=8),
                                 wsrc(p)[1].rearrange("(k p) n -> p k n", p=128)).then_inc(d_w.h, 16)
                    gp.wait_ge(d_w.h, d_w.n + 16)
                    gp.nop().then_inc(d_wc.h, 1)
                blk.gpsimd(g_wu)
                d_w.n += 16
                d_wc.n += 1
                nwu = d_wc.n
                def g_wd(gp, p=p, ndn=d_dn.n):
                    if ndn > 0:
                        gp.wait_ge(d_dn.h, ndn)
                    gp.dma_start(wd[:, :].rearrange("p (k n) -> p k n", k=8),
                                 wsrc(p)[2].rearrange("(k p) n -> p k n", p=128)).then_inc(d_w.h, 16)
                    gp.wait_ge(d_w.h, d_w.n + 16)
                    gp.nop().then_inc(d_wc.h, 1)
                blk.gpsimd(g_wd)
                d_w.n += 16
                d_wc.n += 1
                nwd = d_wc.n
                for b in range(4):
                    for ot in range(8):
                        pb = pG[ot % 2]
                        def te_g(t, p=p, b=b, ot=ot, pb=pb, nw=nwg[p % 2], nld=d_ldc.n):
                            t.wait_ge(d_wc.h, nw)
                            t.wait_ge(d_ldc.h, nld)
                            chase(t, pb)
                            for k in range(8):
                                mm = t.matmul(pb[:, 0:512], wgc[:, k * 1024 + ot * 128:k * 1024 + (ot + 1) * 128],
                                              rms2h[:, k * HT + b * 512:k * HT + (b + 1) * 512],
                                              start=(k == 0), stop=(k == 7))
                            mm.then_inc(d_g.h, 1)
                        d_g.n += 1
                        blk.tensor(te_g)
                        def s_sg(s, ot=ot, pb=pb, n0=d_g.n):
                            s.wait_ge(d_g.h, n0)
                            s.activation(sg_p[:, (ot % 2) * 512:(ot % 2 + 1) * 512], pb[:, 0:512],
                                         AF.Silu).then_inc(d_sg.h, 1)
                        d_sg.n += 1
                        blk.scalar(s_sg)
                        set_last(pb, d_sg)
                        pu = pU[ot % 2]
                        def te_u(t, p=p, b=b, ot=ot, pu=pu, nw=nwu, nld=d_ldc.n):
                            t.wait_ge(d_wc.h, nw)
                            chase(t, pu)
                            for k in range(8):
                                mm = t.matmul(pu[:, 0:512], wu[:, k * 1024 + ot * 128:k * 1024 + (ot + 1) * 128],
                                              rms2h[:, k * HT + b * 512:k * HT + (b + 1) * 512],
                                              start=(k == 0), stop=(k == 7))
                            mm.then_inc(d_u.h, 1)
                        d_u.n += 1
                        blk.tensor(te_u)
                        def s_ub(s, ot=ot, pu=pu, n0=d_u.n):
                            s.wait_ge(d_u.h, n0)
                            s.copy(ub_p[:, (ot % 2) * 512:(ot % 2 + 1) * 512],
                                   pu[:, 0:512]).then_inc(d_ub.h, 1)
                        d_ub.n += 1
                        blk.scalar(s_ub)
                        set_last(pu, d_ub)
                        if p >= 1:
                            def te_wb(t, p=p, b=b, ot=ot, ndum=0):
                                if ot == 0:
                                    chase(t, pW)
                                    t.matmul(pW[:, 0:512], esel[0:32, (p - 1) * 128:p * 128],
                                             wts_fm[0:32, b * 512:(b + 1) * 512],
                                             start=True, stop=True).then_inc(d_wb.h, 1)
                            if ot == 0:
                                d_wb.n += 1
                            blk.tensor(te_wb)
                        def v_hm(v, p=p, ot=ot, n1=d_sg.n, n2=d_ub.n, n3=d_wb.n):
                            v.wait_ge(d_sg.h, n1)
                            v.wait_ge(d_ub.h, n2)
                            op = v.tensor_tensor(hmid[:, ot * 512:(ot + 1) * 512],
                                                 sg_p[:, (ot % 2) * 512:(ot % 2 + 1) * 512],
                                                 ub_p[:, (ot % 2) * 512:(ot % 2 + 1) * 512], ALU.mult)
                            if p >= 1:
                                v.wait_ge(d_wb.h, n3)
                                op = v.tensor_tensor(hmid[:, ot * 512:(ot + 1) * 512],
                                                     hmid[:, ot * 512:(ot + 1) * 512],
                                                     pW[:, 0:512], ALU.mult)
                            op.then_inc(d_hm.h, 1)
                        d_hm.n += 1
                        blk.vector(v_hm)
                    if p >= 1:
                        set_last(pW, d_hm)
                    for ot in range(8):
                        pdn = pDn[ot % 2]
                        def te_dn(t, b=b, ot=ot, pdn=pdn, n0=d_hm.n, nw=nwd):
                            t.wait_ge(d_hm.h, n0)
                            t.wait_ge(d_wc.h, nw)
                            chase(t, pdn)
                            for k in range(8):
                                mm = t.matmul(pdn[:, 0:512], wd[:, k * 1024 + ot * 128:k * 1024 + (ot + 1) * 128],
                                              hmid[:, k * 512:(k + 1) * 512],
                                              start=(k == 0), stop=(k == 7))
                            mm.then_inc(d_dn.h, 1)
                        d_dn.n += 1
                        blk.tensor(te_dn)
                        def v_ac(v, b=b, ot=ot, pdn=pdn, n0=d_dn.n):
                            v.wait_ge(d_dn.h, n0)
                            v.tensor_tensor(acc[:, ot * HT + b * 512:ot * HT + (b + 1) * 512],
                                            acc[:, ot * HT + b * 512:ot * HT + (b + 1) * 512],
                                            pdn[:, 0:512], ALU.add).then_inc(d_ac.h, 1)
                        d_ac.n += 1
                        blk.vector(v_ac)
                        set_last(pdn, d_ac)
                if p < 7:
                    def g_wg(gp, p=p, ng=d_g.n):
                        gp.wait_ge(d_g.h, ng)
                        gp.dma_start(wg[(p + 1) % 2][:, :].rearrange("p (k n) -> p k n", k=8),
                                     wsrc(p + 1)[0].rearrange("(k p) n -> p k n", p=128)).then_inc(d_w.h, 16)
                        gp.wait_ge(d_w.h, d_w.n + 16)
                        gp.nop().then_inc(d_wc.h, 1)
                    blk.gpsimd(g_wg)
                    d_w.n += 16
                    d_wc.n += 1
                    nwg[(p + 1) % 2] = d_wc.n

            def sp_st(sy, h0=h0, n0=d_ac.n):
                sy.wait_ge(d_ac.h, n0)
                sy.dma_start(d_out3[:, :, h0:h0 + HT],
                             acc[:, :].rearrange("p (k n) -> p k n", k=8)).then_inc(d_st.h, 16)
            d_st.n += 16
            blk.sync(sp_st)

        def sp_fin(sy):
            d_st.wait(sy)
        blk.sync(sp_fin)
        blk.flush()
    es4.close()


_BF = ml_dtypes.bfloat16

def _host_constants():
    c = {}
    c["ident_bf"] = np.eye(128, dtype=np.float32).astype(_BF)
    c["ident_f"] = np.eye(128, dtype=np.float32)
    c["onescol"] = np.ones((128, 1), np.float32).astype(_BF)
    e = np.zeros((32, 32 * 128), np.float32)
    for h in range(32):
        e[h, h * 128:(h + 1) * 128] = 1
    c["esel32"] = np.tile(e, (4, 1)).astype(_BF)
    mn = np.zeros((128, 2 * CH), np.float32)
    for j in range(2):
        ss = np.arange(128)[:, None] + j * 128
        ll = np.arange(CH)[None, :]
        mn[:, j * CH:(j + 1) * CH] = np.where(ll >= ss, 0.0, NEG)
    c["maskneg"] = mn.astype(_BF)
    am = np.zeros((128, 4 * 512), np.float32)
    for dlt in range(4):
        kp = np.arange(128)[:, None]
        qf = np.arange(512)[None, :]
        am[:, dlt * 512:(dlt + 1) * 512] = (qf >= kp + dlt * 128).astype(np.float32)
    c["attn_mask"] = am.astype(_BF)
    invf = (1.0 / (10000.0 ** (np.arange(0, HH, 2, dtype=np.float64) / HH)))
    t = np.arange(S, dtype=np.float64)
    ang = invf[:, None] * t[None, :]
    cos = np.cos(ang).astype(np.float32)
    sin = np.sin(ang).astype(np.float32)
    ctab = np.zeros((128, S), np.float32)
    stabm = np.zeros((128, S), np.float32)
    for q in range(4):
        base = q * 32
        if q % 2 == 0:
            ctab[base:base + 32, :] = 1.0
        else:
            ctab[base:base + 16, :] = cos
            ctab[base + 16:base + 32, :] = cos
            stabm[base:base + 16, :] = -sin
            stabm[base + 16:base + 32, :] = sin
    c["rope_cos"] = ctab.astype(_BF)
    c["rope_sin"] = stabm.astype(_BF)
    return c


def _host_weights(inp):
    f32 = lambda a: np.ascontiguousarray(np.asarray(a, dtype=np.float32))
    w = {}
    w["w_inT"] = f32(np.asarray(inp["L0_attn_W_in"]).T)
    w["convw_r"] = f32(np.asarray(inp["L0_attn_conv_w"]).reshape(NTC, 128, 4)
                       .transpose(1, 0, 2).reshape(128, NTC * 4))
    w["convb_r"] = f32(np.asarray(inp["L0_attn_conv_b"]).reshape(NTC, 128).T)
    w["dt_bias_c"] = f32(np.asarray(inp["L0_attn_dt_bias"]).reshape(32, 1))
    w["a_log_c"] = f32(np.asarray(inp["L0_attn_A_log"]).reshape(32, 1))
    w["dD_r"] = f32(np.repeat(np.asarray(inp["L0_attn_D"]), 64).reshape(16, 128).T)
    w["normw_r"] = f32(np.asarray(inp["L0_attn_norm_w"]).reshape(16, 128).T)
    w["w_outT"] = f32(np.asarray(inp["L0_attn_W_out"]).T)
    w["w1_r"] = f32(np.asarray(inp["L0_ffn_w_ln1"]).reshape(8, 128).T)
    w["w2_r"] = f32(np.asarray(inp["L0_ffn_w_ln2"]).reshape(8, 128).T)
    w["wkvd"] = f32(inp["L0_ffn_W_kvd"])
    w["wqd"] = f32(inp["L0_ffn_W_qd"])
    qint = np.zeros((256, D), np.float32)
    kexp = np.zeros((256, D), np.float32)
    rkexp = np.zeros((D, D), np.float32)
    wqu = np.asarray(inp["L0_ffn_W_qu"])
    wrq = np.asarray(inp["L0_ffn_W_ropeq"])
    wku = np.asarray(inp["L0_ffn_W_ku"])
    wrk = np.asarray(inp["L0_ffn_W_ropek"])
    for h in range(NHA):
        qint[:, 64 * h:64 * h + 32] = wqu[:, 32 * h:32 * (h + 1)]
        qint[:, 64 * h + 32:64 * h + 64] = wrq[:, 32 * h:32 * (h + 1)]
        kexp[:, 64 * h:64 * h + 32] = wku[:, 32 * h:32 * (h + 1)]
        rkexp[:, 64 * h + 32:64 * h + 64] = wrk[:, 32 * h:32 * (h + 1)]
    w["w_qint"] = f32(qint)
    w["wku_exp"] = f32(kexp)
    w["wropek_exp"] = f32(rkexp)
    w["wvu"] = f32(inp["L0_ffn_W_vu"])
    w["wo"] = f32(inp["L0_ffn_W_o"])
    w["wg_s"] = f32(inp["L0_ffn_Wg_s"])
    w["wu_s"] = f32(inp["L0_ffn_Wu_s"])
    w["wd_s"] = f32(inp["L0_ffn_Wd_s"])
    w["wg_r"] = f32(inp["L0_ffn_Wg_r"])
    w["wu_r"] = f32(inp["L0_ffn_Wu_r"])
    w["wd_r"] = f32(inp["L0_ffn_Wd_r"])
    w["wrouter"] = f32(inp["L0_ffn_W_router"])
    w["rbias_c"] = f32(np.asarray(inp["L0_ffn_routing_bias"]).reshape(ER, 1))
    return w


_CACHE = {}

def kernel(**inputs):
    if "nc" not in _CACHE:
        _CACHE["nc"] = build_program(dbg=False)
    nc = _CACHE["nc"]
    consts = _host_constants()
    ws = _host_weights(inputs)
    x = np.asarray(inputs["L0_attn_inputs"])
    in_maps = []
    for r in range(NCORE):
        m = dict(consts)
        m.update(ws)
        xs = x[BC * r:BC * (r + 1)].reshape(T, D)
        m["x_fm"] = np.ascontiguousarray(xs.T)
        in_maps.append(m)
    res = run_bass_kernel_spmd(nc, in_maps, core_ids=list(range(NCORE)))
    outs = []
    for r in range(NCORE):
        o = np.asarray(res.results[r]["out_fm"])
        outs.append(np.ascontiguousarray(o.T).reshape(BC, S, D))
    return np.concatenate(outs, axis=0)
